# version 07
# baseline (speedup 1.0000x reference)
import numpy as np
from contextlib import ExitStack

import concourse.bass as bass
import concourse.mybir as mybir
from concourse.bass_utils import run_bass_kernel_spmd
from concourse.replica_groups import maybe_share_collective_output_space

F32 = mybir.dt.float32
BF16 = mybir.dt.bfloat16
I8 = mybir.dt.int8
AF = mybir.ActivationFunctionType
ALU = mybir.AluOpType
AX = mybir.AxisListType

NCORES = 8
SEQ = 8192
TPC = SEQ // NCORES
D = 2048
H = 16
HD = 128
RK = 16
INTER = 8192
NDC = D // 128
NTT = TPC // 128
SCALE = float(HD) ** -0.5
EXP_SHIFT = -3.0
EPS = 1e-5

ENGINES = ("sp", "pe", "act", "dve", "pl")
_SECTION = {"sp": "sync", "pe": "tensor", "act": "scalar", "dve": "vector", "pl": "gpsimd"}


class DGroup:
    def __init__(self, sem):
        self.sem = sem
        self.n = 0

    def tok(self):
        return (self.sem, self.n * 16)


class Prog:

    def __init__(self, nc, ctx):
        self.nc = nc
        self.ctx = ctx
        self.blocks = []
        self.cur = {e: [] for e in ENGINES}
        self.esem = {e: ctx.enter_context(nc.semaphore(f"es_{e}")) for e in ENGINES}
        self.ecount = {e: 0 for e in ENGINES}
        self.waited = {e: {} for e in ENGINES}

    def dgroup(self, name):
        return DGroup(self.ctx.enter_context(self.nc.semaphore(f"dg_{name}")))

    def _w(self, eng, deps):
        out = []
        w = self.waited[eng]
        for d in deps:
            if d is None:
                continue
            sem, val = d
            if w.get(sem, 0) >= val:
                continue
            w[sem] = val
            out.append((sem, val))
        return out

    def op(self, eng, fn, deps=(), auto=True, inc_extra=None):
        waits = self._w(eng, deps)
        tok = None
        if auto:
            self.ecount[eng] += 1
            tok = (self.esem[eng], self.ecount[eng])

        def emit(e, fn=fn, waits=waits, tok=tok, inc_extra=inc_extra):
            for s, v in waits:
                e.wait_ge(s, v)
            ins = fn(e)
            if tok is not None:
                ins = ins.then_inc(tok[0], 1)
            if inc_extra is not None:
                ins.then_inc(*inc_extra)

        self.cur[eng].append(emit)
        return tok

    def dma(self, eng, group, out, in_, deps=()):
        waits = self._w(eng, deps)
        group.n += 1
        tok = (group.sem, group.n * 16)

        def emit(e, out=out, in_=in_, waits=waits, gsem=group.sem):
            for s, v in waits:
                e.wait_ge(s, v)
            e.dma_start(out=out, in_=in_).then_inc(gsem, 16)

        self.cur[eng].append(emit)
        return tok

    def wait(self, eng, deps):
        waits = self._w(eng, deps)
        if not waits:
            return

        def emit(e, waits=waits):
            for s, v in waits:
                e.wait_ge(s, v)

        self.cur[eng].append(emit)

    def barrier(self):
        self.blocks.append(self.cur)
        self.cur = {e: [] for e in ENGINES}
        for e2 in ENGINES:
            for e1 in ENGINES:
                s = self.esem[e1]
                self.waited[e2][s] = max(self.waited[e2].get(s, 0), self.ecount[e1])

    def finalize(self):
        if any(self.cur[e] for e in ENGINES):
            self.barrier()
        nc = self.nc
        for blk_ops in self.blocks:
            with nc.Block() as blk:
                for eng in ENGINES:
                    ops = blk_ops[eng]
                    if not ops:
                        continue

                    def section(e, ops=ops):
                        for emit in ops:
                            emit(e)

                    getattr(blk, _SECTION[eng])(section)


def _ln_chain(P, x_ap, out_ap, scr, deps):
    xsq, s1t, s2t, mt, msqt, biast, stdt, rstdt = scr
    t_s1 = P.op("dve", lambda e: e.reduce_sum(s1t[:, :], x_ap, axis=AX.X), deps=deps)
    t_sq = P.op("act", lambda e: e.activation(xsq[:, :], x_ap, AF.Square),
                deps=deps + [t_s1])
    t_s2 = P.op("dve", lambda e: e.reduce_sum(s2t[:, :], xsq[:, :], axis=AX.X),
                deps=[t_sq])
    t_m = P.op("dve", lambda e: e.tensor_scalar(mt[:, :], s1t[:, :], 1.0 / D, None,
                                                ALU.mult), deps=[t_s1])
    t_msq = P.op("dve", lambda e: e.tensor_tensor(msqt[:, :], mt[:, :], mt[:, :],
                                                  ALU.mult), deps=[t_m])
    t_bias = P.op("dve", lambda e: e.tensor_scalar(biast[:, :], msqt[:, :], EPS, -1.0,
                                                   ALU.subtract, ALU.mult), deps=[t_msq])
    t_std = P.op("act", lambda e: e.activation(stdt[:, :], s2t[:, :], AF.Sqrt,
                                               bias=biast[:, 0:1], scale=1.0 / D),
                 deps=[t_s2, t_bias])
    t_rstd = P.op("dve", lambda e: e.reciprocal(rstdt[:, :], stdt[:, :]), deps=[t_std])
    t_ln = P.op("dve", lambda e: e.tensor_scalar(out_ap, x_ap, mt[:, 0:1],
                                                 rstdt[:, 0:1], ALU.subtract, ALU.mult),
                deps=[t_rstd, t_m])
    return t_ln


def build_nc(dbg=()):
    nc = bass.Bass("TRN2", target_bir_lowering=False)

    xr = nc.dram_tensor("x", [TPC, D], F32, kind="ExternalInput").ap()
    wqt = nc.dram_tensor("wqt", [D, D], F32, kind="ExternalInput").ap()
    wkt = nc.dram_tensor("wkt", [D, D], F32, kind="ExternalInput").ap()
    wvt = nc.dram_tensor("wvt", [D, D], F32, kind="ExternalInput").ap()
    wot = nc.dram_tensor("wot", [D, D], F32, kind="ExternalInput").ap()
    bqkv_h = nc.dram_tensor("bqkv", [128, 48], F32, kind="ExternalInput").ap()
    bo_h = nc.dram_tensor("bo_row", [1, D], F32, kind="ExternalInput").ap()
    bv_h = nc.dram_tensor("bv_row", [1, D], F32, kind="ExternalInput").ap()
    bt_h = nc.dram_tensor("btcat", [D, 4 * RK], F32, kind="ExternalInput").ap()
    at_h = nc.dram_tensor("atcat", [RK, 4 * D], F32, kind="ExternalInput").ap()
    sgu_h = nc.dram_tensor("sgu", [128, 128], F32, kind="ExternalInput").ap()
    sd_h = nc.dram_tensor("sdrow", [1, D], F32, kind="ExternalInput").ap()
    wg_h = nc.dram_tensor("wg", [D, INTER], I8, kind="ExternalInput").ap()
    wu_h = nc.dram_tensor("wu", [D, INTER], I8, kind="ExternalInput").ap()
    wd_h = nc.dram_tensor("wd", [INTER, D], I8, kind="ExternalInput").ap()
    out_h = nc.dram_tensor("out", [TPC, D], F32, kind="ExternalOutput").ap()

    cc_in = nc.dram_tensor("cc_in", [4096, TPC], BF16, kind="Internal").ap()
    aspace = maybe_share_collective_output_space("AllGather", [list(range(NCORES))])
    cc_out = nc.dram_tensor("cc_out", [NCORES * 4096, TPC], BF16, kind="Internal",
                            addr_space=aspace).ap()
    x1_sp = nc.dram_tensor("x1_spill", [TPC, D], F32, kind="Internal").ap()

    dbg_aps = {}
    for name, shape in (("d_lnt", [128, NDC * TPC]), ("d_q", [128, NDC * TPC]),
                        ("d_ao", [128, NDC * TPC]), ("d_x1", [TPC, D]),
                        ("d_ln2t", [128, NDC * TPC])):
        if name in dbg:
            dbg_aps[name] = nc.dram_tensor(name, shape, F32, kind="ExternalOutput").ap()

    with ExitStack() as ctx:
        P = Prog(nc, ctx)
        sb = lambda n, s, d: ctx.enter_context(nc.sbuf_tensor(n, s, d))

        eye = sb("eye", [128, 128], BF16)
        onescol = sb("onescol", [128, 1], BF16)
        onesrow = sb("onesrow", [1, 128], BF16)
        onesrow_f = sb("onesrow_f", [1, 128], F32)
        ut = sb("ut_sb", [RK, 4 * TPC], BF16)

        g_cpl = P.dgroup("c_pl")
        g_csp = P.dgroup("c_sp")

        with nc.sbuf_tensor("eyef", [128, 128], F32) as eyef:
            t_iota = P.op("pl", lambda e: e.iota(eyef[:, :], [[1, 128]],
                                                 channel_multiplier=-1,
                                                 allow_small_or_imprecise_dtypes=True))
            t_eye = P.op("dve", lambda e: e.tensor_scalar(eye[:, :], eyef[:, :], 0.0,
                                                          None, ALU.is_equal),
                         deps=[t_iota])
        t_oc = P.op("pl", lambda e: e.memset(onescol[:, :], 1.0))
        t_or = P.op("pl", lambda e: e.memset(onesrow[:, :], 1.0))
        t_orf = P.op("pl", lambda e: e.memset(onesrow_f[:, :], 1.0))

        s_q = ExitStack()
        q_sb = s_q.enter_context(nc.sbuf_tensor("q_sb", [128, NDC * TPC], BF16))
        cc_sem = ctx.enter_context(nc.semaphore("cc_sem"))

        with ExitStack() as p1:
            sb1 = lambda n, s, d: p1.enter_context(nc.sbuf_tensor(n, s, d))
            ln1t = sb1("ln1t", [128, NDC * TPC], BF16)
            xt = [sb1(f"xt{i}", [128, D], F32) for i in range(2)]
            lnt = [sb1(f"lnt{i}", [128, D], BF16) for i in range(2)]
            scr = (sb1("xsq", [128, D], F32), sb1("s1t", [128, 1], F32),
                   sb1("s2t", [128, 1], F32), sb1("mt", [128, 1], F32),
                   sb1("msqt", [128, 1], F32), sb1("biast", [128, 1], F32),
                   sb1("stdt", [128, 1], F32), sb1("rstdt", [128, 1], F32))
            wslab = [sb1(f"wslab{i}", [128, NDC * 512], BF16) for i in range(2)]
            ktile = [sb1(f"ktile{i}", [128, TPC], BF16) for i in range(2)]
            vtile = [sb1(f"vtile{i}", [128, 512], BF16) for i in range(2)]
            bqkv = sb1("bqkv_sb", [128, 48], F32)
            bv_row = sb1("bv_sb", [1, D], BF16)
            bt = sb1("bt_sb", [128, NDC * 4 * RK], BF16)
            at = sb1("at_sb", [RK, 4 * D], BF16)
            t_bqkv = P.dma("sp", P.dgroup("c_bqkv"), bqkv[:, :], bqkv_h[:, :])
            t_bv = P.dma("pl", P.dgroup("c_bv"), bv_row[:, :], bv_h[:, :])
            t_bt = P.dma("pl", P.dgroup("c_bt1"),
                         bt[:, :].rearrange("p (c r) -> p c r", r=4 * RK),
                         bt_h.rearrange("(c p) r -> p c r", p=128))
            t_at = P.dma("pl", P.dgroup("c_at1"), at[:, :], at_h[:, :])

            tps = p1.enter_context(nc.psum_tensor("tps", [128, 512], F32))
            u_ps = p1.enter_context(nc.psum_tensor("u_ps", [RK, 512], F32))
            qk_ps = [p1.enter_context(nc.psum_tensor(f"qk_ps{i}", [128, TPC], F32))
                     for i in range(2)]
            v_ps = [p1.enter_context(nc.psum_tensor(f"v_ps{i}", [128, 512], F32))
                    for i in range(2)]

            g_x = [P.dgroup(f"x1ld{i}") for i in range(2)]
            g_ws = [P.dgroup(f"ws{i}") for i in range(2)]
            g_kst = [P.dgroup(f"kst{i}") for i in range(2)]
            g_vst = [P.dgroup(f"vst{i}") for i in range(2)]

            wsrc = {0: wqt, 1: wkt, 2: wvt}
            slab_loads = {}
            slab_loads[0] = P.dma("pl", g_ws[0],
                                  wslab[0][:, :].rearrange("p (c o) -> p c o", o=512),
                                  wqt[:, 0:512].rearrange("(c p) o -> p c o", p=128))

            xt_tok = [None, None]
            lnt_tok = [None, None]
            tp_tok = None
            for tt in range(NTT):
                sl = tt % 2
                deps = [xt_tok[sl]] if xt_tok[sl] else []
                t_x = P.dma("sp", g_x[sl], xt[sl][:, :], xr[tt * 128:(tt + 1) * 128, :],
                            deps=deps)
                nd = [t_x]
                if lnt_tok[sl]:
                    nd.append(lnt_tok[sl])
                t_ln = _ln_chain(P, xt[sl][:, :], lnt[sl][:, :], scr, nd)
                xt_tok[sl] = t_ln
                for g in range(4):
                    mm = []
                    for j in range(4):
                        dc = 4 * g + j
                        d0 = [t_ln, t_eye]
                        if tp_tok:
                            d0.append(tp_tok)
                        mm.append(P.op("pe", lambda e, sl=sl, dc=dc, j=j: e.matmul(
                            tps[:, j * 128:(j + 1) * 128],
                            lnt[sl][:, dc * 128:(dc + 1) * 128], eye[:, :],
                            start=True, stop=True), deps=d0))
                    tp_tok = P.op("act", lambda e, tt=tt, g=g: e.copy(
                        ln1t[:, :].rearrange("p (c n) -> p c n", n=TPC)
                        [:, 4 * g:4 * g + 4, tt * 128:(tt + 1) * 128],
                        tps[:, :].rearrange("p (c n) -> p c n", n=128)), deps=mm)
                    lnt_tok[sl] = tp_tok
            ln1t_ready = tp_tok

            if "d_lnt" in dbg_aps:
                P.dma("pl", g_cpl, dbg_aps["d_lnt"][:, :], ln1t[:, :],
                      deps=[ln1t_ready])

            ut_ready = []
            prev_u = None
            for l in range(3):
                for qh in range(2):
                    mm = []
                    for dc in range(NDC):
                        d0 = [ln1t_ready, t_bt]
                        if dc == 0 and prev_u:
                            d0.append(prev_u)
                        mm.append(P.op("pe", lambda e, l=l, dc=dc, qh=qh: e.matmul(
                            u_ps[:, :],
                            bt[:, dc * 64 + l * RK: dc * 64 + (l + 1) * RK],
                            ln1t[:, dc * TPC + qh * 512: dc * TPC + qh * 512 + 512],
                            start=(dc == 0), stop=(dc == NDC - 1)), deps=d0))
                    prev_u = P.op("act", lambda e, l=l, qh=qh: e.mul(
                        ut[:, l * TPC + qh * 512: l * TPC + qh * 512 + 512],
                        u_ps[:, :], 0.5), deps=mm)
                ut_ready.append(prev_u)
            ut_all = ut_ready[2]

            loads = [(l, oq) for l in range(3) for oq in range(4)]
            slab_use = {}
            qkps_tok = [None, None]
            vps_tok = [None, None]
            ktile_tok = [None, None]
            vtile_tok = [None, None]

            for idx, (l, oq) in enumerate(loads):
                slot = idx % 2
                if idx == 0:
                    t_slab = slab_loads[0]
                else:
                    wdeps = []
                    u = slab_use.get(idx - 2)
                    if u:
                        wdeps.append(u)
                    t_slab = P.dma("pl", g_ws[slot],
                                   wslab[slot][:, :].rearrange("p (c o) -> p c o", o=512),
                                   wsrc[l][:, oq * 512:(oq + 1) * 512]
                                   .rearrange("(c p) o -> p c o", p=128), deps=wdeps)
                last_pe = None
                if l < 2:
                    for j in range(4):
                        ot = oq * 4 + j
                        pp = ot % 2
                        mm = []
                        for qh in range(2):
                            for dc in range(NDC):
                                d0 = []
                                if dc == 0:
                                    d0 = [t_slab, ln1t_ready, ut_all, t_at]
                                    if qh == 0 and qkps_tok[pp]:
                                        d0.append(qkps_tok[pp])
                                mm.append(P.op("pe", lambda e, pp=pp, slot=slot, j=j, dc=dc, qh=qh: e.matmul(
                                    qk_ps[pp][:, qh * 512:qh * 512 + 512],
                                    wslab[slot][:, dc * 512 + j * 128: dc * 512 + (j + 1) * 128],
                                    ln1t[:, dc * TPC + qh * 512: dc * TPC + qh * 512 + 512],
                                    start=(dc == 0), stop=False), deps=d0))
                            mm.append(P.op("pe", lambda e, pp=pp, l=l, ot=ot, qh=qh: e.matmul(
                                qk_ps[pp][:, qh * 512:qh * 512 + 512],
                                at[:, l * D + ot * 128:l * D + (ot + 1) * 128],
                                ut[:, l * TPC + qh * 512:l * TPC + qh * 512 + 512],
                                start=False, stop=True), deps=[]))
                        last_pe = mm[-1]
                        if l == 0:
                            qkps_tok[pp] = P.op("act", lambda e, pp=pp, ot=ot: e.activation(
                                q_sb[:, ot * TPC:(ot + 1) * TPC], qk_ps[pp][:, :],
                                AF.Identity, bias=bqkv[:, ot:ot + 1]), deps=mm + [t_bqkv])
                        else:
                            ksl = ot % 2
                            d1 = mm + [t_bqkv]
                            if ktile_tok[ksl]:
                                d1.append(ktile_tok[ksl])
                            tk = P.op("act", lambda e, pp=pp, ot=ot, ksl=ksl: e.activation(
                                ktile[ksl][:, :], qk_ps[pp][:, :], AF.Identity,
                                bias=bqkv[:, 16 + ot:17 + ot]), deps=d1)
                            qkps_tok[pp] = tk
                            ktile_tok[ksl] = P.dma("sp", g_kst[ksl],
                                                   cc_in[ot * 128:(ot + 1) * 128, :],
                                                   ktile[ksl][:, :], deps=[tk])
                else:
                    for tt in range(NTT):
                        pp = tt % 2
                        mm = []
                        d0 = [t_slab, ln1t_ready, t_or, t_bv, ut_all, t_at]
                        if vps_tok[pp]:
                            d0.append(vps_tok[pp])
                        mm.append(P.op("pe", lambda e, pp=pp, oq=oq: e.matmul(
                            v_ps[pp][:, :], onesrow[:, :],
                            bv_row[0:1, oq * 512:(oq + 1) * 512],
                            start=True, stop=False), deps=d0))
                        for dc in range(NDC):
                            mm.append(P.op("pe", lambda e, pp=pp, slot=slot, dc=dc, tt=tt: e.matmul(
                                v_ps[pp][:, :],
                                ln1t[:, dc * TPC + tt * 128: dc * TPC + (tt + 1) * 128],
                                wslab[slot][:, dc * 512:(dc + 1) * 512],
                                start=False, stop=False), deps=[]))
                        mm.append(P.op("pe", lambda e, pp=pp, tt=tt, oq=oq: e.matmul(
                            v_ps[pp][:, :],
                            ut[:, 2 * TPC + tt * 128:2 * TPC + (tt + 1) * 128],
                            at[:, 2 * D + oq * 512:2 * D + (oq + 1) * 512],
                            start=False, stop=True), deps=[]))
                        last_pe = mm[-1]
                        vsl = tt % 2
                        d1 = list(mm)
                        if vtile_tok[vsl]:
                            d1.append(vtile_tok[vsl])
                        tv = P.op("act", lambda e, pp=pp, vsl=vsl: e.copy(
                            vtile[vsl][:, :], v_ps[pp][:, :]), deps=d1)
                        vps_tok[pp] = tv
                        vtile_tok[vsl] = P.dma(
                            "sp", g_vst[vsl],
                            bass.AP(cc_in.tensor,
                                    (2048 + 2 * tt * 128 + oq // 2) * TPC + (oq % 2) * 512,
                                    [[2 * TPC, 128], [1, 512]]),
                            vtile[vsl][:, :], deps=[tv])
                slab_use[idx] = last_pe

            P.op("pl", lambda e: e.collective_compute(
                "AllGather", ALU.bypass, replica_groups=[list(range(NCORES))],
                ins=[cc_in], outs=[cc_out]),
                deps=[g_kst[0].tok(), g_kst[1].tok(), g_vst[0].tok(),
                      g_vst[1].tok()], auto=False, inc_extra=(cc_sem, 1))
            cc_tok = (cc_sem, 1)

        P.barrier()

        s_ao = ExitStack()
        ao_sb = s_ao.enter_context(nc.sbuf_tensor("ao_sb", [128, NDC * TPC], BF16,
                                                   side="right"))

        with ExitStack() as p2:
            sb2 = lambda n, s, d: p2.enter_context(nc.sbuf_tensor(n, s, d))
            kh = [sb2(f"kh{i}", [128, SEQ], BF16) for i in range(2)]
            vp = [sb2(f"vp{i}", [128, 64 * 256], BF16) for i in range(2)]
            pt = [sb2(f"pt{i}", [128, TPC], BF16) for i in range(4)]
            acc_a = sb2("acc_a", [128, TPC], F32)
            acc_b = sb2("acc_b", [128, TPC], F32)
            recip = sb2("recip", [1, TPC], F32)
            bcast = sb2("bcast", [128, TPC], F32)
            onescol_f = sb2("onescol_f", [128, 1], F32)

            s_ps = [p2.enter_context(nc.psum_tensor(f"s_ps{i}", [128, TPC], F32))
                    for i in range(2)]
            pv_ps = [p2.enter_context(nc.psum_tensor(f"pv_ps{i}", [128, TPC], F32))
                     for i in range(2)]

            shiftc = sb2("shiftc", [128, 1], F32)
            t_shift = P.op("pl", lambda e: e.memset(shiftc[:, :], EXP_SHIFT))
            t_ocf = P.op("pl", lambda e: e.memset(onescol_f[:, :], 1.0))

            g_kh = [P.dgroup(f"kh{i}") for i in range(2)]
            g_vp = [P.dgroup(f"vp{i}") for i in range(2)]

            kh_use = [None, None]
            vp_use = [None, None]

            def load_kh(h):
                slot = h % 2
                deps = [cc_tok]
                if kh_use[slot]:
                    deps.append(kh_use[slot])
                for r in range(NCORES):
                    P.dma("sp", g_kh[slot], kh[slot][:, r * TPC:(r + 1) * TPC],
                          cc_out[r * 4096 + h * 128: r * 4096 + (h + 1) * 128, :],
                          deps=deps if r == 0 else [])

            def load_vp(pr):
                slot = pr % 2
                deps = [cc_tok]
                if vp_use[slot]:
                    deps.append(vp_use[slot])
                for r in range(NCORES):
                    src = bass.AP(cc_out.tensor,
                                  (r * 4096 + 2048) * TPC + pr * 256,
                                  [[2048, 128], [128 * 2048, 8], [1, 256]])
                    dst = vp[slot][:, r * 8 * 256:(r * 8 + 8) * 256] \
                        .rearrange("p (a d) -> p a d", d=256)
                    P.dma("act", g_vp[slot], dst, src, deps=deps if r == 0 else [])

            load_kh(0)
            load_kh(1)
            load_vp(0)

            recip_sp = nc.dram_tensor("recip_spill", [128, TPC], F32, kind="Internal").ap()
            g_rsp = P.dgroup("rspill")
            g_rbc = P.dgroup("rbcast")

            prev_exp62 = None
            prev_rec = None
            prev_sum_a = None
            prev_sum_b = None
            prev_spill = None
            prev_bcdma = None
            prev_evac = [None, None]
            last_bcast_reader = None
            pt_readers = {0: [], 1: [], 2: [], 3: []}
            pend = None

            def emit_epi1(pd):
                nonlocal prev_spill, prev_bcdma
                t_sum_b = None
                for half in range(2):
                    t_sum_b = P.op("pe", lambda e, half=half: e.matmul(
                        s_ps[1][0:1, half * 512:(half + 1) * 512], onescol_f[:, :],
                        acc_b[:, half * 512:(half + 1) * 512],
                        start=False, stop=True), deps=[pd['acc63']])
                dr = [t_sum_b]
                if pd['prev_spill']:
                    dr.append(pd['prev_spill'])
                t_rec = P.op("dve", lambda e: e.reciprocal(
                    recip[:, :], s_ps[1][0:1, :]), deps=dr)
                dsp = [t_rec]
                if pd['prev_bcdma']:
                    dsp.append(pd['prev_bcdma'])
                tok = P.dma("sp", g_rsp, recip_sp[0:1, :], recip[:, :], deps=dsp)
                w = 1
                while w < 128:
                    tok = P.dma("sp", g_rsp, recip_sp[w:2 * w, :],
                                recip_sp[0:w, :], deps=[tok])
                    w *= 2
                t_spill = tok
                dbc = [t_spill]
                if pd['last_bcast_reader']:
                    dbc.append(pd['last_bcast_reader'])
                t_bcd = P.dma("sp", g_rbc, bcast[:, :], recip_sp[:, :], deps=dbc)
                pd['t_rec'] = t_rec
                pd['t_bcd'] = t_bcd
                prev_spill = t_spill
                prev_bcdma = t_bcd
                return t_sum_b

            def emit_epi2(pd):
                return P.op("dve", lambda e, pvs=pd['pvs'], h=pd['h']: e.tensor_tensor(
                    ao_sb[:, h * TPC:(h + 1) * TPC], pv_ps[pvs][:, :], bcast[:, :],
                    ALU.mult), deps=[pd['t_bcd'], pd['pv63']])

            for h in range(H):
                hs = h % 2
                pvs = h % 2
                vslot = (h // 2) % 2
                khld = g_kh[hs].tok()
                vpld = g_vp[vslot].tok()
                if h % 2 == 0 and h // 2 + 1 < H // 2:
                    load_vp(h // 2 + 1)

                s_tok = {}
                exp_tok = {}
                acc_tok = {}
                pv_tok = {}
                j = h % 2
                for step in range(65):
                    if step < 64:
                        kt = step
                        sl = kt % 2
                        ps = kt % 4
                        d0 = [khld]
                        if kt >= 2:
                            d0.append(exp_tok[kt - 2])
                        elif kt == 0 and prev_exp62:
                            d0.append(prev_exp62)
                        elif kt == 1 and prev_rec:
                            d0.append(prev_rec)
                        for qh in range(2):
                            s_tok[kt] = P.op("pe", lambda e, hs=hs, kt=kt, h=h, sl=sl, qh=qh: e.matmul(
                                s_ps[sl][:, qh * 512:(qh + 1) * 512],
                                kh[hs][:, kt * 128:(kt + 1) * 128],
                                q_sb[:, h * TPC + qh * 512: h * TPC + (qh + 1) * 512],
                                start=True, stop=True), deps=d0 if qh == 0 else [])
                        if step == 0 and pend is not None:
                            prev_sum_b = emit_epi1(pend)
                            prev_rec = pend['t_rec']
                        d1 = [s_tok[kt], t_shift] + pt_readers[ps]
                        exp_tok[kt] = P.op("act", lambda e, ps=ps, sl=sl: e.activation(
                            pt[ps][:, :], s_ps[sl][:, :],
                            AF.Exp, bias=shiftc[:, 0:1], scale=SCALE), deps=d1)
                        a_buf = acc_a if kt % 2 == 0 else acc_b
                        if kt < 2:
                            da = [exp_tok[kt]]
                            if kt == 0 and prev_sum_a:
                                da.append(prev_sum_a)
                            if kt == 1 and prev_sum_b:
                                da.append(prev_sum_b)
                            acc_tok[kt] = P.op("dve", lambda e, ps=ps, a=a_buf: e.tensor_copy(
                                a[:, :], pt[ps][:, :]), deps=da)
                        else:
                            acc_tok[kt] = P.op("dve", lambda e, ps=ps, a=a_buf: e.tensor_tensor(
                                a[:, :], a[:, :], pt[ps][:, :], ALU.add),
                                deps=[exp_tok[kt], acc_tok[kt - 2]])
                    if step >= 1:
                        kt = step - 1
                        ps = kt % 4
                        d2 = [exp_tok[kt], vpld]
                        if kt == 0 and prev_evac[pvs]:
                            d2.append(prev_evac[pvs])
                        for qh in range(2):
                            pv_tok[kt] = P.op("pe", lambda e, pvs=pvs, vslot=vslot, j=j, kt=kt, qh=qh, ps=ps: e.matmul(
                                pv_ps[pvs][:, qh * 512:(qh + 1) * 512],
                                vp[vslot][:, kt * 256 + j * 128: kt * 256 + (j + 1) * 128],
                                pt[ps][:, qh * 512:(qh + 1) * 512],
                                start=(kt == 0), stop=(kt == 63)), deps=d2 if qh == 0 else [])
                        pt_readers[ps] = [pv_tok[kt], acc_tok[kt]]
                    if step == 4 and pend is not None:
                        t_evac = emit_epi2(pend)
                        prev_evac[pend['pvs']] = t_evac
                        last_bcast_reader = t_evac
                        pend = None
                kh_use[hs] = s_tok[63]
                vp_use[vslot] = pv_tok[63]
                if h + 2 < H:
                    load_kh(h + 2)

                t_sa = None
                for half in range(2):
                    t_sa = P.op("pe", lambda e, half=half: e.matmul(
                        s_ps[1][0:1, half * 512:(half + 1) * 512], onescol_f[:, :],
                        acc_a[:, half * 512:(half + 1) * 512],
                        start=True, stop=False),
                        deps=[acc_tok[62], exp_tok[63], t_ocf])
                prev_sum_a = t_sa
                prev_exp62 = exp_tok[62]
                pend = {'h': h, 'pvs': pvs, 'acc63': acc_tok[63],
                        'pv63': pv_tok[63], 'prev_spill': prev_spill,
                        'prev_bcdma': prev_bcdma,
                        'last_bcast_reader': last_bcast_reader}

            prev_sum_b = emit_epi1(pend)
            prev_rec = pend['t_rec']
            t_evac = emit_epi2(pend)
            prev_evac[pend['pvs']] = t_evac
            last_bcast_reader = t_evac
            pend = None

            if "d_ao" in dbg_aps:
                P.dma("pl", g_cpl, dbg_aps["d_ao"][:, :], ao_sb[:, :],
                      deps=[prev_evac[0], prev_evac[1]])
            if "d_q" in dbg_aps:
                P.dma("pl", g_cpl, dbg_aps["d_q"][:, :], q_sb[:, :],
                      deps=[prev_evac[0], prev_evac[1]])

        P.barrier()
        s_q.close()

        ln2t = sb("ln2t", [128, NDC * TPC], BF16)
        g_x1s = [P.dgroup(f"x1sp{i}") for i in range(2)]

        with ExitStack() as p3:
            sb3 = lambda n, s, d: p3.enter_context(nc.sbuf_tensor(n, s, d))
            wotb = [sb3(f"wotb{i}", [128, D], BF16) for i in range(2)]
            xt3 = [sb3(f"xt3_{i}", [128, D], F32) for i in range(2)]
            x1t = [sb3(f"x1t_{i}", [128, D], F32) for i in range(2)]
            ln2tm = sb3("ln2tm", [128, NTT * D], BF16)
            scr3 = (sb3("xsq3", [128, D], F32), sb3("s1t3", [128, 1], F32),
                    sb3("s2t3", [128, 1], F32), sb3("mt3", [128, 1], F32),
                    sb3("msqt3", [128, 1], F32), sb3("biast3", [128, 1], F32),
                    sb3("stdt3", [128, 1], F32), sb3("rstdt3", [128, 1], F32))

            bo_row = sb3("bo_sb", [1, D], BF16)
            bt3 = sb3("bt3_sb", [128, NDC * 4 * RK], BF16)
            at3 = sb3("at3_sb", [RK, 4 * D], BF16)
            t_bo = P.dma("pl", P.dgroup("c_bo"), bo_row[:, :], bo_h[:, :])
            t_bt = P.dma("pl", P.dgroup("c_bt3"),
                         bt3[:, :].rearrange("p (c r) -> p c r", r=4 * RK),
                         bt_h.rearrange("(c p) r -> p c r", p=128))
            t_at = P.dma("pl", P.dgroup("c_at3"), at3[:, :], at_h[:, :])

            g_wot = [P.dgroup(f"wot{i}") for i in range(2)]
            g_x3 = [P.dgroup(f"x3ld{i}") for i in range(2)]

            with ExitStack() as p3a:
                o_ps = [p3a.enter_context(nc.psum_tensor(f"o_ps{i}", [128, 512], F32))
                        for i in range(8)]

                uo_tok = None
                for qh in range(2):
                    mm = []
                    for dc in range(NDC):
                        d0 = [t_bt]
                        if dc == 0 and uo_tok:
                            d0.append(uo_tok)
                        mm.append(P.op("pe", lambda e, dc=dc, qh=qh: e.matmul(
                            o_ps[0][0:RK, :], bt3[:, dc * 64 + 48: dc * 64 + 64],
                            ao_sb[:, dc * TPC + qh * 512: dc * TPC + qh * 512 + 512],
                            start=(dc == 0), stop=(dc == NDC - 1)), deps=d0))
                    uo_tok = P.op("act", lambda e, qh=qh: e.mul(
                        ut[:, 3 * TPC + qh * 512:3 * TPC + qh * 512 + 512],
                        o_ps[0][0:RK, :], 0.5), deps=mm)
                uto_ready = uo_tok

                wot_use = [None, None]
                x1_tok = [None, None]
                xt3_tok = [None, None]
                ln2_tiles = []
                evac_tok = {}
                for tp in range(4):
                    seed = {}
                    for i in range(2):
                        for oc in range(4):
                            g = 4 * i + oc
                            d0 = [t_or, t_bo, uto_ready]
                            pv = evac_tok.get((tp - 1, g))
                            if pv:
                                d0.append(pv)
                            seed[g] = P.op("pe", lambda e, g=g, oc=oc: e.matmul(
                                o_ps[g][:, :], onesrow[:, :],
                                bo_row[0:1, oc * 512:(oc + 1) * 512],
                                start=True, stop=False), deps=d0)
                    last_mm = {g: [seed[g]] for g in range(8)}
                    for dc in range(NDC):
                        slot = dc % 2
                        wdeps = [wot_use[slot]] if wot_use[slot] else []
                        P.dma("pl", g_wot[slot], wotb[slot][:, :],
                              wot[dc * 128:(dc + 1) * 128, :], deps=wdeps)
                        wtok = g_wot[slot].tok()
                        lw = None
                        for i in range(2):
                            tt = 2 * tp + i
                            for oc in range(4):
                                g = 4 * i + oc
                                lw = P.op("pe", lambda e, g=g, slot=slot, dc=dc, tt=tt, oc=oc: e.matmul(
                                    o_ps[g][:, :],
                                    ao_sb[:, dc * TPC + tt * 128: dc * TPC + (tt + 1) * 128],
                                    wotb[slot][:, oc * 512:(oc + 1) * 512],
                                    start=False, stop=False), deps=[wtok])
                                last_mm[g].append(lw)
                        wot_use[slot] = lw
                    for i in range(2):
                        tt = 2 * tp + i
                        sl = tt % 2
                        xdeps = [xt3_tok[sl]] if xt3_tok[sl] else []
                        t_x = P.dma("sp", g_x3[sl], xt3[sl][:, :],
                                    xr[tt * 128:(tt + 1) * 128, :], deps=xdeps)
                        tile_done = []
                        for oc in range(4):
                            g = 4 * i + oc
                            t_lo = P.op("pe", lambda e, g=g, tt=tt, oc=oc: e.matmul(
                                o_ps[g][:, :],
                                ut[:, 3 * TPC + tt * 128:3 * TPC + (tt + 1) * 128],
                                at3[:, 3 * D + oc * 512:3 * D + (oc + 1) * 512],
                                start=False, stop=True), deps=[uto_ready, t_at])
                            dE = [t_lo, t_x] + last_mm[g]
                            if x1_tok[sl]:
                                dE.append(x1_tok[sl])
                            tkv = P.op("dve", lambda e, g=g, sl=sl, oc=oc: e.tensor_tensor(
                                x1t[sl][:, oc * 512:(oc + 1) * 512], o_ps[g][:, :],
                                xt3[sl][:, oc * 512:(oc + 1) * 512], ALU.add), deps=dE)
                            evac_tok[(tp, g)] = tkv
                            tile_done.append(tkv)
                        xt3_tok[sl] = tile_done[-1]
                        t_ln2 = _ln_chain(P, x1t[sl][:, :],
                                          ln2tm[:, tt * D:(tt + 1) * D], scr3,
                                          tile_done)
                        x1_tok[sl] = P.dma("sp", g_x1s[sl],
                                           x1_sp[tt * 128:(tt + 1) * 128, :],
                                           x1t[sl][:, :], deps=[t_ln2])
                        ln2_tiles.append((tt, t_ln2))
                last_o_evac = evac_tok[(3, 7)]

            P.barrier()
            with nc.psum_tensor("tps3", [128, 512], F32) as tps3:
                tp3_tok = None
                for tt, t_ln2 in ln2_tiles:
                    for g in range(4):
                        mm = []
                        for j in range(4):
                            dc = 4 * g + j
                            d0 = [t_ln2, t_eye]
                            if tp3_tok:
                                d0.append(tp3_tok)
                            mm.append(P.op("pe", lambda e, tt=tt, dc=dc, j=j: e.matmul(
                                tps3[:, j * 128:(j + 1) * 128],
                                ln2tm[:, tt * D + dc * 128: tt * D + (dc + 1) * 128],
                                eye[:, :], start=True, stop=True), deps=d0))
                        tp3_tok = P.op("act", lambda e, tt=tt, g=g: e.copy(
                            ln2t[:, :].rearrange("p (c n) -> p c n", n=TPC)
                            [:, 4 * g:4 * g + 4, tt * 128:(tt + 1) * 128],
                            tps3[:, :].rearrange("p (c n) -> p c n", n=128)), deps=mm)
                ln2t_ready = tp3_tok

            if "d_ln2t" in dbg_aps:
                P.dma("pl", g_cpl, dbg_aps["d_ln2t"][:, :], ln2t[:, :],
                      deps=[ln2t_ready])
            if "d_x1" in dbg_aps:
                pass

        P.barrier()
        s_ao.close()

        for _e in ENGINES:
            P.wait(_e, [g_x1s[0].tok(), g_x1s[1].tok()])
        with ExitStack() as p4:
            sb4 = lambda n, s, d: p4.enter_context(nc.sbuf_tensor(n, s, d))
            fuse = sb4("fuse", [128, 64 * 512], BF16)
            wgu = [sb4(f"wgu{i}", [128, NDC * 256], BF16) for i in range(2)]
            wub = [sb4(f"wub{i}", [128, NDC * 256], BF16) for i in range(2)]
            wdb = [sb4(f"wdb{i}", [128, 8 * 512], BF16) for i in range(2)]
            gtmp = sb4("gtmp", [128, 512], F32)
            utmp = sb4("utmp", [128, 512], F32)
            dtmp = sb4("dtmp", [128, 512], F32)
            x1rd = [sb4(f"x1rd{i}", [128, 512], F32) for i in range(2)]
            x2t = [sb4(f"x2t{i}", [128, 512], F32) for i in range(2)]
            sgu = sb4("sgu_sb", [128, 128], F32)
            sdb = sb4("sdb_sb", [128, D], F32)
            t_sgu = P.dma("sp", P.dgroup("c_sgu"), sgu[:, :], sgu_h[:, :])
            with nc.sbuf_tensor("sdrow_sb", [1, D], F32) as sdrow, \
                 nc.psum_tensor("sdb_ps", [128, D], F32) as sdb_ps:
                t_sdrow = P.dma("sp", P.dgroup("c_sdrow"), sdrow[:, :], sd_h[:, :])
                toks = []
                for j in range(4):
                    toks.append(P.op("pe", lambda e, j=j: e.matmul(
                        sdb_ps[:, j * 512:(j + 1) * 512], onesrow_f[:, :],
                        sdrow[0:1, j * 512:(j + 1) * 512], start=True, stop=True),
                        deps=[t_orf, t_sdrow]))
                t_sdb = P.op("dve", lambda e: e.tensor_copy(sdb[:, :], sdb_ps[:, :]),
                             deps=toks)

            g_wg = [P.dgroup(f"wg{i}") for i in range(2)]
            g_wd = [P.dgroup(f"wd{i}") for i in range(2)]
            g_x1r = [P.dgroup(f"x1rd{i}") for i in range(2)]
            g_out = [P.dgroup(f"outst{i}") for i in range(2)]

            wgu_use = [None, None]
            wd_use = [None, None]
            x2_tok = [None, None]
            x1rd_reader = [None, None]
            prev_ftok = None
            prev_x2 = None

            for th in range(2):
                with ExitStack() as p4a:
                    gu_ps = [p4a.enter_context(
                        nc.psum_tensor(f"gu_ps{th}_{i}", [128, 512], F32))
                        for i in range(4)]
                    gu_tok = [None] * 4
                    fuse_tok = {}
                    for grp in range(32):
                        slot = grp % 2
                        wdeps = [wgu_use[slot]] if wgu_use[slot] else []
                        P.dma("pl", g_wg[slot],
                              wgu[slot][:, :].rearrange("p (c o) -> p c o", o=256),
                              wg_h[:, grp * 256:(grp + 1) * 256]
                              .rearrange("(c p) o -> p c o", p=128), deps=wdeps)
                        P.dma("pl", g_wg[slot],
                              wub[slot][:, :].rearrange("p (c o) -> p c o", o=256),
                              wu_h[:, grp * 256:(grp + 1) * 256]
                              .rearrange("(c p) o -> p c o", p=128), deps=[])
                        wtok = g_wg[slot].tok()
                        lastg = None
                        for ii in range(2):
                            it = grp * 2 + ii
                            pg = it % 2
                            pu = 2 + it % 2
                            mmg, mmu = [], []
                            for hc in range(NDC):
                                d0 = []
                                if hc == 0:
                                    d0 = [wtok, ln2t_ready, t_sdb]
                                    if gu_tok[pg]:
                                        d0.append(gu_tok[pg])
                                d0u = []
                                if hc == 0:
                                    d0u = [wtok]
                                    if gu_tok[pu]:
                                        d0u.append(gu_tok[pu])
                                mmg.append(P.op("pe", lambda e, pg=pg, slot=slot, hc=hc, ii=ii, th=th: e.matmul(
                                    gu_ps[pg][:, :],
                                    wgu[slot][:, hc * 256 + ii * 128: hc * 256 + (ii + 1) * 128],
                                    ln2t[:, hc * TPC + th * 512: hc * TPC + th * 512 + 512],
                                    start=(hc == 0), stop=(hc == NDC - 1)), deps=d0))
                                mmu.append(P.op("pe", lambda e, pu=pu, slot=slot, hc=hc, ii=ii, th=th: e.matmul(
                                    gu_ps[pu][:, :],
                                    wub[slot][:, hc * 256 + ii * 128: hc * 256 + (ii + 1) * 128],
                                    ln2t[:, hc * TPC + th * 512: hc * TPC + th * 512 + 512],
                                    start=(hc == 0), stop=(hc == NDC - 1)), deps=d0u))
                            lastg = mmu[-1]
                            gdeps = mmg + [t_sgu]
                            udeps = mmu + [t_sgu]
                            if prev_ftok:
                                gdeps.append(prev_ftok)
                                udeps.append(prev_ftok)
                            t_g = P.op("act", lambda e, pg=pg, it=it: e.activation(
                                gtmp[:, :], gu_ps[pg][:, :], AF.Silu,
                                scale=sgu[:, it:it + 1]), deps=gdeps)
                            t_u = P.op("dve", lambda e, pu=pu, it=it: e.tensor_scalar(
                                utmp[:, :], gu_ps[pu][:, :], sgu[:, 64 + it:65 + it],
                                None, ALU.mult), deps=udeps)
                            ftok = P.op("dve", lambda e, it=it: e.tensor_tensor(
                                fuse[:, it * 512:(it + 1) * 512], gtmp[:, :],
                                utmp[:, :], ALU.mult), deps=[t_g, t_u])
                            gu_tok[pg] = ftok
                            gu_tok[pu] = ftok
                            fuse_tok[it] = ftok
                            prev_ftok = ftok
                        wgu_use[slot] = lastg
                    fuse_all = [fuse_tok[63]]

                P.barrier()

                with ExitStack() as p4b:
                    dn_ps = [p4b.enter_context(
                        nc.psum_tensor(f"dn_ps{th}_{i}", [128, 512], F32))
                        for i in range(8)]
                    dn_evac = {}
                    for oc in range(4):
                        base = 4 * (oc % 2)
                        lastd = None
                        for icg in range(8):
                            slot = icg % 2
                            wdeps = [wd_use[slot]] if wd_use[slot] else []
                            P.dma("pl", g_wd[slot],
                                  wdb[slot][:, :].rearrange("p (c o) -> p c o", o=512),
                                  wd_h[icg * 1024:(icg + 1) * 1024,
                                       oc * 512:(oc + 1) * 512]
                                  .rearrange("(c p) o -> p c o", p=128), deps=wdeps)
                            wtok = g_wd[slot].tok()
                            for ici in range(8):
                                ic = icg * 8 + ici
                                for ts in range(4):
                                    d0 = [wtok]
                                    if ic == 0 and ts == 0:
                                        d0 += fuse_all
                                        for tsx in range(4):
                                            if (oc - 2, tsx) in dn_evac:
                                                d0.append(dn_evac[(oc - 2, tsx)])
                                    lastd = P.op("pe", lambda e, base=base, ts=ts, slot=slot, ici=ici, ic=ic: e.matmul(
                                        dn_ps[base + ts][:, :],
                                        fuse[:, ic * 512 + ts * 128: ic * 512 + (ts + 1) * 128],
                                        wdb[slot][:, ici * 512:(ici + 1) * 512],
                                        start=(ic == 0), stop=(ic == 63)), deps=d0)
                            wd_use[slot] = lastd
                        for ts in range(4):
                            xsl = ts % 2
                            xd = [g_x1s[0].tok(), g_x1s[1].tok()]
                            if x1rd_reader[xsl]:
                                xd.append(x1rd_reader[xsl])
                            t_x1r = P.dma("sp", g_x1r[xsl],
                                          x1rd[xsl][:, :],
                                          bass.AP(x1_sp.tensor,
                                                  (th * 512 + ts * 128) * D + oc * 512,
                                                  [[D, 128], [1, 512]]), deps=xd)
                            d0 = [lastd, t_sdb]
                            if (oc - 1, ts) in dn_evac:
                                d0.append(dn_evac[(oc - 1, ts)])
                            if prev_x2:
                                d0.append(prev_x2)
                            t_sc = P.op("dve", lambda e, base=base, ts=ts, oc=oc: e.tensor_tensor(
                                dtmp[:, :], dn_ps[base + ts][:, :],
                                sdb[:, oc * 512:(oc + 1) * 512], ALU.mult), deps=d0)
                            d1 = [t_sc, t_x1r]
                            if x2_tok[xsl]:
                                d1.append(x2_tok[xsl])
                            t_x2 = P.op("dve", lambda e, ts=ts, xsl=xsl, oc=oc: e.tensor_tensor(
                                x2t[xsl][:, :], dtmp[:, :],
                                x1rd[xsl][:, :], ALU.add), deps=d1)
                            prev_x2 = t_x2
                            dn_evac[(oc, ts)] = t_x2
                            x1rd_reader[xsl] = t_x2
                            x2_tok[xsl] = P.dma("sp", g_out[xsl],
                                                bass.AP(out_h.tensor,
                                                        (th * 512 + ts * 128) * D + oc * 512,
                                                        [[D, 128], [1, 512]]),
                                                x2t[xsl][:, :], deps=[t_x2])
                if th == 0:
                    P.barrier()

            P.wait("sp", [g_out[0].tok(), g_out[1].tok()])

        P.finalize()
    return nc


def _host_layouts(inp):
    f = np.ascontiguousarray
    a = {k: np.asarray(v) for k, v in inp.items()}
    shared = {
        "wqt": f(a["L0_attn_Wq"].T),
        "wkt": f(a["L0_attn_Wk"].T),
        "wvt": f(a["L0_attn_Wv"].T),
        "wot": f(a["L0_attn_Wo"].T),
        "bqkv": f(np.concatenate([a["L0_attn_bq"].reshape(16, 128).T,
                                  a["L0_attn_bk"].reshape(16, 128).T,
                                  a["L0_attn_bv"].reshape(16, 128).T], axis=1)),
        "bo_row": f(a["L0_attn_bo"].reshape(1, D)),
        "bv_row": f(a["L0_attn_bv"].reshape(1, D)),
        "btcat": f(np.concatenate([a["L0_attn_Bq"].T, a["L0_attn_Bk"].T,
                                   a["L0_attn_Bv"].T, a["L0_attn_Bo"].T], axis=1)),
        "atcat": f(np.concatenate([a["L0_attn_Aq"].T, a["L0_attn_Ak"].T,
                                   a["L0_attn_Av"].T, a["L0_attn_Ao"].T], axis=1)),
        "sgu": f(np.concatenate([a["L0_ffn_s_gate"].reshape(64, 128).T,
                                 a["L0_ffn_s_up"].reshape(64, 128).T], axis=1)),
        "sdrow": f(a["L0_ffn_s_down"].reshape(1, D)),
        "wg": a["L0_ffn_w_gate"],
        "wu": a["L0_ffn_w_up"],
        "wd": a["L0_ffn_w_down"],
    }
    x = a["L0_attn_x"].reshape(SEQ, D)
    in_maps = []
    for r in range(NCORES):
        m = dict(shared)
        m["x"] = f(x[r * TPC:(r + 1) * TPC, :])
        in_maps.append(m)
    return in_maps


def kernel(**inputs):
    in_maps = _host_layouts(inputs)
    nc = build_nc()
    res = run_bass_kernel_spmd(nc, in_maps, core_ids=list(range(NCORES)))
    out = np.concatenate([np.asarray(r["out"]) for r in res.results], axis=0)
    return out.reshape(1, SEQ, D).astype(np.float32, copy=False)


# version 08
# speedup vs baseline: 1.0455x; 1.0455x over previous
import numpy as np
from contextlib import ExitStack

import concourse.bass as bass
import concourse.mybir as mybir
from concourse.bass_utils import run_bass_kernel_spmd
from concourse.replica_groups import maybe_share_collective_output_space

F32 = mybir.dt.float32
BF16 = mybir.dt.bfloat16
I8 = mybir.dt.int8
AF = mybir.ActivationFunctionType
ALU = mybir.AluOpType
AX = mybir.AxisListType

NCORES = 8
SEQ = 8192
TPC = SEQ // NCORES
D = 2048
H = 16
HD = 128
RK = 16
INTER = 8192
NDC = D // 128
NTT = TPC // 128
SCALE = float(HD) ** -0.5
EXP_SHIFT = -3.0
EPS = 1e-5

ENGINES = ("sp", "pe", "act", "dve", "pl")
_SECTION = {"sp": "sync", "pe": "tensor", "act": "scalar", "dve": "vector", "pl": "gpsimd"}


class DGroup:
    def __init__(self, sem):
        self.sem = sem
        self.n = 0

    def tok(self):
        return (self.sem, self.n * 16)


class Prog:

    def __init__(self, nc, ctx):
        self.nc = nc
        self.ctx = ctx
        self.blocks = []
        self.cur = {e: [] for e in ENGINES}
        self.esem = {e: ctx.enter_context(nc.semaphore(f"es_{e}")) for e in ENGINES}
        self.ecount = {e: 0 for e in ENGINES}
        self.waited = {e: {} for e in ENGINES}

    def dgroup(self, name):
        return DGroup(self.ctx.enter_context(self.nc.semaphore(f"dg_{name}")))

    def _w(self, eng, deps):
        out = []
        w = self.waited[eng]
        for d in deps:
            if d is None:
                continue
            sem, val = d
            if w.get(sem, 0) >= val:
                continue
            w[sem] = val
            out.append((sem, val))
        return out

    def op(self, eng, fn, deps=(), auto=True, inc_extra=None):
        waits = self._w(eng, deps)
        tok = None
        if auto:
            self.ecount[eng] += 1
            tok = (self.esem[eng], self.ecount[eng])

        def emit(e, fn=fn, waits=waits, tok=tok, inc_extra=inc_extra):
            for s, v in waits:
                e.wait_ge(s, v)
            ins = fn(e)
            if tok is not None:
                ins = ins.then_inc(tok[0], 1)
            if inc_extra is not None:
                ins.then_inc(*inc_extra)

        self.cur[eng].append(emit)
        return tok

    def dma(self, eng, group, out, in_, deps=()):
        waits = self._w(eng, deps)
        group.n += 1
        tok = (group.sem, group.n * 16)

        def emit(e, out=out, in_=in_, waits=waits, gsem=group.sem):
            for s, v in waits:
                e.wait_ge(s, v)
            e.dma_start(out=out, in_=in_).then_inc(gsem, 16)

        self.cur[eng].append(emit)
        return tok

    def wait(self, eng, deps):
        waits = self._w(eng, deps)
        if not waits:
            return

        def emit(e, waits=waits):
            for s, v in waits:
                e.wait_ge(s, v)

        self.cur[eng].append(emit)

    def barrier(self):
        self.blocks.append(self.cur)
        self.cur = {e: [] for e in ENGINES}
        for e2 in ENGINES:
            for e1 in ENGINES:
                s = self.esem[e1]
                self.waited[e2][s] = max(self.waited[e2].get(s, 0), self.ecount[e1])

    def finalize(self):
        if any(self.cur[e] for e in ENGINES):
            self.barrier()
        nc = self.nc
        for blk_ops in self.blocks:
            with nc.Block() as blk:
                for eng in ENGINES:
                    ops = blk_ops[eng]
                    if not ops:
                        continue

                    def section(e, ops=ops):
                        for emit in ops:
                            emit(e)

                    getattr(blk, _SECTION[eng])(section)


def _ln_chain(P, x_ap, out_ap, scr, deps):
    xsq, s1t, s2t, mt, msqt, biast, stdt, rstdt = scr
    t_s1 = P.op("dve", lambda e: e.reduce_sum(s1t[:, :], x_ap, axis=AX.X), deps=deps)
    t_sq = P.op("act", lambda e: e.activation(xsq[:, :], x_ap, AF.Square),
                deps=deps + [t_s1])
    t_s2 = P.op("dve", lambda e: e.reduce_sum(s2t[:, :], xsq[:, :], axis=AX.X),
                deps=[t_sq])
    t_m = P.op("dve", lambda e: e.tensor_scalar(mt[:, :], s1t[:, :], 1.0 / D, None,
                                                ALU.mult), deps=[t_s1])
    t_msq = P.op("dve", lambda e: e.tensor_tensor(msqt[:, :], mt[:, :], mt[:, :],
                                                  ALU.mult), deps=[t_m])
    t_bias = P.op("dve", lambda e: e.tensor_scalar(biast[:, :], msqt[:, :], EPS, -1.0,
                                                   ALU.subtract, ALU.mult), deps=[t_msq])
    t_std = P.op("act", lambda e: e.activation(stdt[:, :], s2t[:, :], AF.Sqrt,
                                               bias=biast[:, 0:1], scale=1.0 / D),
                 deps=[t_s2, t_bias])
    t_rstd = P.op("dve", lambda e: e.reciprocal(rstdt[:, :], stdt[:, :]), deps=[t_std])
    t_ln = P.op("dve", lambda e: e.tensor_scalar(out_ap, x_ap, mt[:, 0:1],
                                                 rstdt[:, 0:1], ALU.subtract, ALU.mult),
                deps=[t_rstd, t_m])
    return t_ln


def build_nc(dbg=()):
    nc = bass.Bass("TRN2", target_bir_lowering=False)

    xr = nc.dram_tensor("x", [TPC, D], F32, kind="ExternalInput").ap()
    wqt = nc.dram_tensor("wqt", [D, D], F32, kind="ExternalInput").ap()
    wkt = nc.dram_tensor("wkt", [D, D], F32, kind="ExternalInput").ap()
    wvt = nc.dram_tensor("wvt", [D, D], F32, kind="ExternalInput").ap()
    wot = nc.dram_tensor("wot", [D, D], F32, kind="ExternalInput").ap()
    bqkv_h = nc.dram_tensor("bqkv", [128, 48], F32, kind="ExternalInput").ap()
    bo_h = nc.dram_tensor("bo_row", [1, D], F32, kind="ExternalInput").ap()
    bv_h = nc.dram_tensor("bv_row", [1, D], F32, kind="ExternalInput").ap()
    bt_h = nc.dram_tensor("btcat", [D, 4 * RK], F32, kind="ExternalInput").ap()
    at_h = nc.dram_tensor("atcat", [RK, 4 * D], F32, kind="ExternalInput").ap()
    sgu_h = nc.dram_tensor("sgu", [128, 128], F32, kind="ExternalInput").ap()
    sd_h = nc.dram_tensor("sdrow", [1, D], F32, kind="ExternalInput").ap()
    wg_h = nc.dram_tensor("wg", [D, INTER], I8, kind="ExternalInput").ap()
    wu_h = nc.dram_tensor("wu", [D, INTER], I8, kind="ExternalInput").ap()
    wd_h = nc.dram_tensor("wd", [INTER, D], I8, kind="ExternalInput").ap()
    out_h = nc.dram_tensor("out", [TPC, D], F32, kind="ExternalOutput").ap()

    cc_in = nc.dram_tensor("cc_in", [4096, TPC], BF16, kind="Internal").ap()
    aspace = maybe_share_collective_output_space("AllGather", [list(range(NCORES))])
    cc_out = nc.dram_tensor("cc_out", [NCORES * 4096, TPC], BF16, kind="Internal",
                            addr_space=aspace).ap()
    x1_sp = nc.dram_tensor("x1_spill", [TPC, D], F32, kind="Internal").ap()

    dbg_aps = {}
    for name, shape in (("d_lnt", [128, NDC * TPC]), ("d_q", [128, NDC * TPC]),
                        ("d_ao", [128, NDC * TPC]), ("d_x1", [TPC, D]),
                        ("d_ln2t", [128, NDC * TPC])):
        if name in dbg:
            dbg_aps[name] = nc.dram_tensor(name, shape, F32, kind="ExternalOutput").ap()

    with ExitStack() as ctx:
        P = Prog(nc, ctx)
        sb = lambda n, s, d: ctx.enter_context(nc.sbuf_tensor(n, s, d))

        eye = sb("eye", [128, 128], BF16)
        onescol = sb("onescol", [128, 1], BF16)
        onesrow = sb("onesrow", [1, 128], BF16)
        onesrow_f = sb("onesrow_f", [1, 128], F32)
        ut = sb("ut_sb", [RK, 4 * TPC], BF16)

        g_cpl = P.dgroup("c_pl")
        g_csp = P.dgroup("c_sp")

        with nc.sbuf_tensor("eyef", [128, 128], F32) as eyef:
            t_iota = P.op("pl", lambda e: e.iota(eyef[:, :], [[1, 128]],
                                                 channel_multiplier=-1,
                                                 allow_small_or_imprecise_dtypes=True))
            t_eye = P.op("dve", lambda e: e.tensor_scalar(eye[:, :], eyef[:, :], 0.0,
                                                          None, ALU.is_equal),
                         deps=[t_iota])
        t_oc = P.op("pl", lambda e: e.memset(onescol[:, :], 1.0))
        t_or = P.op("pl", lambda e: e.memset(onesrow[:, :], 1.0))
        t_orf = P.op("pl", lambda e: e.memset(onesrow_f[:, :], 1.0))

        s_q = ExitStack()
        q_sb = s_q.enter_context(nc.sbuf_tensor("q_sb", [128, NDC * TPC], BF16))
        cc_sem = ctx.enter_context(nc.semaphore("cc_sem"))

        with ExitStack() as p1:
            sb1 = lambda n, s, d: p1.enter_context(nc.sbuf_tensor(n, s, d))
            ln1t = sb1("ln1t", [128, NDC * TPC], BF16)
            xt = [sb1(f"xt{i}", [128, D], F32) for i in range(2)]
            lnt = [sb1(f"lnt{i}", [128, D], BF16) for i in range(2)]
            scr = (sb1("xsq", [128, D], F32), sb1("s1t", [128, 1], F32),
                   sb1("s2t", [128, 1], F32), sb1("mt", [128, 1], F32),
                   sb1("msqt", [128, 1], F32), sb1("biast", [128, 1], F32),
                   sb1("stdt", [128, 1], F32), sb1("rstdt", [128, 1], F32))
            wslab = [sb1(f"wslab{i}", [128, NDC * 512], BF16) for i in range(2)]
            ktile = [sb1(f"ktile{i}", [128, TPC], BF16) for i in range(2)]
            vtile = [sb1(f"vtile{i}", [128, 512], BF16) for i in range(2)]
            bqkv = sb1("bqkv_sb", [128, 48], F32)
            bv_row = sb1("bv_sb", [1, D], BF16)
            bt = sb1("bt_sb", [128, NDC * 4 * RK], BF16)
            at = sb1("at_sb", [RK, 4 * D], BF16)
            t_bqkv = P.dma("sp", P.dgroup("c_bqkv"), bqkv[:, :], bqkv_h[:, :])
            t_bv = P.dma("pl", P.dgroup("c_bv"), bv_row[:, :], bv_h[:, :])
            t_bt = P.dma("pl", P.dgroup("c_bt1"),
                         bt[:, :].rearrange("p (c r) -> p c r", r=4 * RK),
                         bt_h.rearrange("(c p) r -> p c r", p=128))
            t_at = P.dma("pl", P.dgroup("c_at1"), at[:, :], at_h[:, :])

            tps = p1.enter_context(nc.psum_tensor("tps", [128, 512], F32))
            u_ps = p1.enter_context(nc.psum_tensor("u_ps", [RK, 512], F32))
            qk_ps = [p1.enter_context(nc.psum_tensor(f"qk_ps{i}", [128, TPC], F32))
                     for i in range(2)]
            v_ps = [p1.enter_context(nc.psum_tensor(f"v_ps{i}", [128, 512], F32))
                    for i in range(2)]

            g_x = [P.dgroup(f"x1ld{i}") for i in range(2)]
            g_ws = [P.dgroup(f"ws{i}") for i in range(2)]
            g_kst = [P.dgroup(f"kst{i}") for i in range(2)]
            g_vst = [P.dgroup(f"vst{i}") for i in range(2)]

            wsrc = {0: wqt, 1: wkt, 2: wvt}
            slab_loads = {}
            slab_loads[0] = P.dma("pl", g_ws[0],
                                  wslab[0][:, :].rearrange("p (c o) -> p c o", o=512),
                                  wqt[:, 0:512].rearrange("(c p) o -> p c o", p=128))

            xt_tok = [None, None]
            lnt_tok = [None, None]
            tp_tok = None
            for tt in range(NTT):
                sl = tt % 2
                deps = [xt_tok[sl]] if xt_tok[sl] else []
                t_x = P.dma("sp", g_x[sl], xt[sl][:, :], xr[tt * 128:(tt + 1) * 128, :],
                            deps=deps)
                nd = [t_x]
                if lnt_tok[sl]:
                    nd.append(lnt_tok[sl])
                t_ln = _ln_chain(P, xt[sl][:, :], lnt[sl][:, :], scr, nd)
                xt_tok[sl] = t_ln
                for g in range(4):
                    mm = []
                    for j in range(4):
                        dc = 4 * g + j
                        d0 = [t_ln, t_eye]
                        if tp_tok:
                            d0.append(tp_tok)
                        mm.append(P.op("pe", lambda e, sl=sl, dc=dc, j=j: e.matmul(
                            tps[:, j * 128:(j + 1) * 128],
                            lnt[sl][:, dc * 128:(dc + 1) * 128], eye[:, :],
                            start=True, stop=True), deps=d0))
                    tp_tok = P.op("act", lambda e, tt=tt, g=g: e.copy(
                        ln1t[:, :].rearrange("p (c n) -> p c n", n=TPC)
                        [:, 4 * g:4 * g + 4, tt * 128:(tt + 1) * 128],
                        tps[:, :].rearrange("p (c n) -> p c n", n=128)), deps=mm)
                    lnt_tok[sl] = tp_tok
            ln1t_ready = tp_tok

            if "d_lnt" in dbg_aps:
                P.dma("pl", g_cpl, dbg_aps["d_lnt"][:, :], ln1t[:, :],
                      deps=[ln1t_ready])

            ut_ready = []
            prev_u = None
            for l in range(3):
                for qh in range(2):
                    mm = []
                    for dc in range(NDC):
                        d0 = [ln1t_ready, t_bt]
                        if dc == 0 and prev_u:
                            d0.append(prev_u)
                        mm.append(P.op("pe", lambda e, l=l, dc=dc, qh=qh: e.matmul(
                            u_ps[:, :],
                            bt[:, dc * 64 + l * RK: dc * 64 + (l + 1) * RK],
                            ln1t[:, dc * TPC + qh * 512: dc * TPC + qh * 512 + 512],
                            start=(dc == 0), stop=(dc == NDC - 1)), deps=d0))
                    prev_u = P.op("act", lambda e, l=l, qh=qh: e.mul(
                        ut[:, l * TPC + qh * 512: l * TPC + qh * 512 + 512],
                        u_ps[:, :], 0.5), deps=mm)
                ut_ready.append(prev_u)
            ut_all = ut_ready[2]

            loads = [(l, oq) for l in range(3) for oq in range(4)]
            slab_use = {}
            qkps_tok = [None, None]
            vps_tok = [None, None]
            ktile_tok = [None, None]
            vtile_tok = [None, None]

            for idx, (l, oq) in enumerate(loads):
                slot = idx % 2
                if idx == 0:
                    t_slab = slab_loads[0]
                else:
                    wdeps = []
                    u = slab_use.get(idx - 2)
                    if u:
                        wdeps.append(u)
                    t_slab = P.dma("pl", g_ws[slot],
                                   wslab[slot][:, :].rearrange("p (c o) -> p c o", o=512),
                                   wsrc[l][:, oq * 512:(oq + 1) * 512]
                                   .rearrange("(c p) o -> p c o", p=128), deps=wdeps)
                last_pe = None
                if l < 2:
                    for j in range(4):
                        ot = oq * 4 + j
                        pp = ot % 2
                        mm = []
                        for qh in range(2):
                            for dc in range(NDC):
                                d0 = []
                                if dc == 0:
                                    d0 = [t_slab, ln1t_ready, ut_all, t_at]
                                    if qh == 0 and qkps_tok[pp]:
                                        d0.append(qkps_tok[pp])
                                mm.append(P.op("pe", lambda e, pp=pp, slot=slot, j=j, dc=dc, qh=qh: e.matmul(
                                    qk_ps[pp][:, qh * 512:qh * 512 + 512],
                                    wslab[slot][:, dc * 512 + j * 128: dc * 512 + (j + 1) * 128],
                                    ln1t[:, dc * TPC + qh * 512: dc * TPC + qh * 512 + 512],
                                    start=(dc == 0), stop=False), deps=d0))
                            mm.append(P.op("pe", lambda e, pp=pp, l=l, ot=ot, qh=qh: e.matmul(
                                qk_ps[pp][:, qh * 512:qh * 512 + 512],
                                at[:, l * D + ot * 128:l * D + (ot + 1) * 128],
                                ut[:, l * TPC + qh * 512:l * TPC + qh * 512 + 512],
                                start=False, stop=True), deps=[]))
                        last_pe = mm[-1]
                        if l == 0:
                            qkps_tok[pp] = P.op("act", lambda e, pp=pp, ot=ot: e.activation(
                                q_sb[:, ot * TPC:(ot + 1) * TPC], qk_ps[pp][:, :],
                                AF.Identity, bias=bqkv[:, ot:ot + 1]), deps=mm + [t_bqkv])
                        else:
                            ksl = ot % 2
                            d1 = mm + [t_bqkv]
                            if ktile_tok[ksl]:
                                d1.append(ktile_tok[ksl])
                            tk = P.op("act", lambda e, pp=pp, ot=ot, ksl=ksl: e.activation(
                                ktile[ksl][:, :], qk_ps[pp][:, :], AF.Identity,
                                bias=bqkv[:, 16 + ot:17 + ot]), deps=d1)
                            qkps_tok[pp] = tk
                            ktile_tok[ksl] = P.dma("sp", g_kst[ksl],
                                                   cc_in[ot * 128:(ot + 1) * 128, :],
                                                   ktile[ksl][:, :], deps=[tk])
                else:
                    for tt in range(NTT):
                        pp = tt % 2
                        mm = []
                        d0 = [t_slab, ln1t_ready, t_or, t_bv, ut_all, t_at]
                        if vps_tok[pp]:
                            d0.append(vps_tok[pp])
                        mm.append(P.op("pe", lambda e, pp=pp, oq=oq: e.matmul(
                            v_ps[pp][:, :], onesrow[:, :],
                            bv_row[0:1, oq * 512:(oq + 1) * 512],
                            start=True, stop=False), deps=d0))
                        for dc in range(NDC):
                            mm.append(P.op("pe", lambda e, pp=pp, slot=slot, dc=dc, tt=tt: e.matmul(
                                v_ps[pp][:, :],
                                ln1t[:, dc * TPC + tt * 128: dc * TPC + (tt + 1) * 128],
                                wslab[slot][:, dc * 512:(dc + 1) * 512],
                                start=False, stop=False), deps=[]))
                        mm.append(P.op("pe", lambda e, pp=pp, tt=tt, oq=oq: e.matmul(
                            v_ps[pp][:, :],
                            ut[:, 2 * TPC + tt * 128:2 * TPC + (tt + 1) * 128],
                            at[:, 2 * D + oq * 512:2 * D + (oq + 1) * 512],
                            start=False, stop=True), deps=[]))
                        last_pe = mm[-1]
                        vsl = tt % 2
                        d1 = list(mm)
                        if vtile_tok[vsl]:
                            d1.append(vtile_tok[vsl])
                        tv = P.op("act", lambda e, pp=pp, vsl=vsl: e.copy(
                            vtile[vsl][:, :], v_ps[pp][:, :]), deps=d1)
                        vps_tok[pp] = tv
                        vtile_tok[vsl] = P.dma(
                            "sp", g_vst[vsl],
                            bass.AP(cc_in.tensor,
                                    (2048 + 2 * tt * 128 + oq // 2) * TPC + (oq % 2) * 512,
                                    [[2 * TPC, 128], [1, 512]]),
                            vtile[vsl][:, :], deps=[tv])
                slab_use[idx] = last_pe

            P.op("pl", lambda e: e.collective_compute(
                "AllGather", ALU.bypass, replica_groups=[list(range(NCORES))],
                ins=[cc_in], outs=[cc_out]),
                deps=[g_kst[0].tok(), g_kst[1].tok(), g_vst[0].tok(),
                      g_vst[1].tok()], auto=False, inc_extra=(cc_sem, 1))
            cc_tok = (cc_sem, 1)

        P.barrier()

        s_ao = ExitStack()
        ao_sb = s_ao.enter_context(nc.sbuf_tensor("ao_sb", [128, NDC * TPC], BF16,
                                                   side="right"))

        with ExitStack() as p2:
            sb2 = lambda n, s, d: p2.enter_context(nc.sbuf_tensor(n, s, d))
            kh = [sb2(f"kh{i}", [128, SEQ], BF16) for i in range(2)]
            vp = [sb2(f"vp{i}", [128, 64 * 256], BF16) for i in range(2)]
            pt = [sb2(f"pt{i}", [128, TPC], BF16) for i in range(4)]
            acc_a = sb2("acc_a", [128, TPC], F32)
            acc_b = sb2("acc_b", [128, TPC], F32)
            recip = sb2("recip", [1, TPC], F32)
            bcast = sb2("bcast", [128, TPC], F32)
            onescol_f = sb2("onescol_f", [128, 1], F32)

            s_ps = [p2.enter_context(nc.psum_tensor(f"s_ps{i}", [128, TPC], F32))
                    for i in range(2)]
            pv_ps = [p2.enter_context(nc.psum_tensor(f"pv_ps{i}", [128, TPC], F32))
                     for i in range(2)]

            shiftc = sb2("shiftc", [128, 1], F32)
            t_shift = P.op("pl", lambda e: e.memset(shiftc[:, :], EXP_SHIFT))
            t_ocf = P.op("pl", lambda e: e.memset(onescol_f[:, :], 1.0))

            g_kh = [P.dgroup(f"kh{i}") for i in range(2)]
            g_vp = [P.dgroup(f"vp{i}") for i in range(2)]

            kh_use = [None, None]
            vp_use = [None, None]

            def load_kh(h):
                slot = h % 2
                deps = [cc_tok]
                if kh_use[slot]:
                    deps.append(kh_use[slot])
                for r in range(NCORES):
                    P.dma("sp", g_kh[slot], kh[slot][:, r * TPC:(r + 1) * TPC],
                          cc_out[r * 4096 + h * 128: r * 4096 + (h + 1) * 128, :],
                          deps=deps if r == 0 else [])

            def load_vp(pr):
                slot = pr % 2
                deps = [cc_tok]
                if vp_use[slot]:
                    deps.append(vp_use[slot])
                for r in range(NCORES):
                    src = bass.AP(cc_out.tensor,
                                  (r * 4096 + 2048) * TPC + pr * 256,
                                  [[2048, 128], [128 * 2048, 8], [1, 256]])
                    dst = vp[slot][:, r * 8 * 256:(r * 8 + 8) * 256] \
                        .rearrange("p (a d) -> p a d", d=256)
                    P.dma("act", g_vp[slot], dst, src, deps=deps if r == 0 else [])

            load_kh(0)
            load_kh(1)
            load_vp(0)

            g_rbc = P.dgroup("rbcast")

            prev_exp62 = None
            prev_rec = None
            prev_sum_a = None
            prev_sum_b = None
            prev_spill = None
            prev_evac = [None, None]
            last_bcast_reader = None
            pt_readers = {0: [], 1: [], 2: [], 3: []}
            pend = None

            def emit_epi1(pd):
                nonlocal prev_spill
                t_sum_b = None
                for half in range(2):
                    t_sum_b = P.op("pe", lambda e, half=half: e.matmul(
                        s_ps[1][0:1, half * 512:(half + 1) * 512], onescol_f[:, :],
                        acc_b[:, half * 512:(half + 1) * 512],
                        start=False, stop=True), deps=[pd['acc63']])
                dr = [t_sum_b]
                if pd['prev_spill']:
                    dr.append(pd['prev_spill'])
                t_rec = P.op("dve", lambda e: e.reciprocal(
                    recip[:, :], s_ps[1][0:1, :]), deps=dr)
                dsp = [t_rec]
                if pd['last_bcast_reader']:
                    dsp.append(pd['last_bcast_reader'])
                tok = P.dma("sp", g_rbc, bcast[0:1, :], recip[:, :], deps=dsp)
                t_spill = tok
                w = 1
                while w < 128:
                    tok = P.dma("sp", g_rbc, bcast[w:2 * w, :],
                                bcast[0:w, :], deps=[tok])
                    w *= 2
                t_bcd = tok
                pd['t_rec'] = t_rec
                pd['t_bcd'] = t_bcd
                prev_spill = t_spill
                return t_sum_b

            def emit_epi2(pd):
                return P.op("dve", lambda e, pvs=pd['pvs'], h=pd['h']: e.tensor_tensor(
                    ao_sb[:, h * TPC:(h + 1) * TPC], pv_ps[pvs][:, :], bcast[:, :],
                    ALU.mult), deps=[pd['t_bcd'], pd['pv63']])

            for h in range(H):
                hs = h % 2
                pvs = h % 2
                vslot = (h // 2) % 2
                khld = g_kh[hs].tok()
                vpld = g_vp[vslot].tok()
                if h % 2 == 0 and h // 2 + 1 < H // 2:
                    load_vp(h // 2 + 1)

                s_tok = {}
                exp_tok = {}
                acc_tok = {}
                pv_tok = {}
                j = h % 2
                for step in range(65):
                    if step < 64:
                        kt = step
                        sl = kt % 2
                        ps = kt % 4
                        d0 = [khld]
                        if kt >= 2:
                            d0.append(exp_tok[kt - 2])
                        elif kt == 0 and prev_exp62:
                            d0.append(prev_exp62)
                        elif kt == 1 and prev_rec:
                            d0.append(prev_rec)
                        for qh in range(2):
                            s_tok[kt] = P.op("pe", lambda e, hs=hs, kt=kt, h=h, sl=sl, qh=qh: e.matmul(
                                s_ps[sl][:, qh * 512:(qh + 1) * 512],
                                kh[hs][:, kt * 128:(kt + 1) * 128],
                                q_sb[:, h * TPC + qh * 512: h * TPC + (qh + 1) * 512],
                                start=True, stop=True), deps=d0 if qh == 0 else [])
                        if step == 0 and pend is not None:
                            prev_sum_b = emit_epi1(pend)
                            prev_rec = pend['t_rec']
                        d1 = [s_tok[kt], t_shift] + pt_readers[ps]
                        exp_tok[kt] = P.op("act", lambda e, ps=ps, sl=sl: e.activation(
                            pt[ps][:, :], s_ps[sl][:, :],
                            AF.Exp, bias=shiftc[:, 0:1], scale=SCALE), deps=d1)
                        a_buf = acc_a if kt % 2 == 0 else acc_b
                        if kt < 2:
                            da = [exp_tok[kt]]
                            if kt == 0 and prev_sum_a:
                                da.append(prev_sum_a)
                            if kt == 1 and prev_sum_b:
                                da.append(prev_sum_b)
                            acc_tok[kt] = P.op("dve", lambda e, ps=ps, a=a_buf: e.tensor_copy(
                                a[:, :], pt[ps][:, :]), deps=da)
                        else:
                            acc_tok[kt] = P.op("dve", lambda e, ps=ps, a=a_buf: e.tensor_tensor(
                                a[:, :], a[:, :], pt[ps][:, :], ALU.add),
                                deps=[exp_tok[kt], acc_tok[kt - 2]])
                    if step >= 1:
                        kt = step - 1
                        ps = kt % 4
                        d2 = [exp_tok[kt], vpld]
                        if kt == 0 and prev_evac[pvs]:
                            d2.append(prev_evac[pvs])
                        for qh in range(2):
                            pv_tok[kt] = P.op("pe", lambda e, pvs=pvs, vslot=vslot, j=j, kt=kt, qh=qh, ps=ps: e.matmul(
                                pv_ps[pvs][:, qh * 512:(qh + 1) * 512],
                                vp[vslot][:, kt * 256 + j * 128: kt * 256 + (j + 1) * 128],
                                pt[ps][:, qh * 512:(qh + 1) * 512],
                                start=(kt == 0), stop=(kt == 63)), deps=d2 if qh == 0 else [])
                        pt_readers[ps] = [pv_tok[kt], acc_tok[kt]]
                    if step == 16 and pend is not None:
                        t_evac = emit_epi2(pend)
                        prev_evac[pend['pvs']] = t_evac
                        last_bcast_reader = t_evac
                        pend = None
                kh_use[hs] = s_tok[63]
                vp_use[vslot] = pv_tok[63]
                if h + 2 < H:
                    load_kh(h + 2)

                t_sa = None
                for half in range(2):
                    t_sa = P.op("pe", lambda e, half=half: e.matmul(
                        s_ps[1][0:1, half * 512:(half + 1) * 512], onescol_f[:, :],
                        acc_a[:, half * 512:(half + 1) * 512],
                        start=True, stop=False),
                        deps=[acc_tok[62], exp_tok[63], t_ocf])
                prev_sum_a = t_sa
                prev_exp62 = exp_tok[62]
                pend = {'h': h, 'pvs': pvs, 'acc63': acc_tok[63],
                        'pv63': pv_tok[63], 'prev_spill': prev_spill,
                        'last_bcast_reader': last_bcast_reader}

            prev_sum_b = emit_epi1(pend)
            prev_rec = pend['t_rec']
            t_evac = emit_epi2(pend)
            prev_evac[pend['pvs']] = t_evac
            last_bcast_reader = t_evac
            pend = None

            if "d_ao" in dbg_aps:
                P.dma("pl", g_cpl, dbg_aps["d_ao"][:, :], ao_sb[:, :],
                      deps=[prev_evac[0], prev_evac[1]])
            if "d_q" in dbg_aps:
                P.dma("pl", g_cpl, dbg_aps["d_q"][:, :], q_sb[:, :],
                      deps=[prev_evac[0], prev_evac[1]])

        P.barrier()
        s_q.close()

        ln2t = sb("ln2t", [128, NDC * TPC], BF16)
        g_x1s = [P.dgroup(f"x1sp{i}") for i in range(2)]

        with ExitStack() as p3:
            sb3 = lambda n, s, d: p3.enter_context(nc.sbuf_tensor(n, s, d))
            wotb = [sb3(f"wotb{i}", [128, D], BF16) for i in range(2)]
            xt3 = [sb3(f"xt3_{i}", [128, D], F32) for i in range(2)]
            x1t = [sb3(f"x1t_{i}", [128, D], F32) for i in range(2)]
            ln2tm = sb3("ln2tm", [128, NTT * D], BF16)
            scr3 = (sb3("xsq3", [128, D], F32), sb3("s1t3", [128, 1], F32),
                    sb3("s2t3", [128, 1], F32), sb3("mt3", [128, 1], F32),
                    sb3("msqt3", [128, 1], F32), sb3("biast3", [128, 1], F32),
                    sb3("stdt3", [128, 1], F32), sb3("rstdt3", [128, 1], F32))

            bo_row = sb3("bo_sb", [1, D], BF16)
            bt3 = sb3("bt3_sb", [128, NDC * 4 * RK], BF16)
            at3 = sb3("at3_sb", [RK, 4 * D], BF16)
            t_bo = P.dma("pl", P.dgroup("c_bo"), bo_row[:, :], bo_h[:, :])
            t_bt = P.dma("pl", P.dgroup("c_bt3"),
                         bt3[:, :].rearrange("p (c r) -> p c r", r=4 * RK),
                         bt_h.rearrange("(c p) r -> p c r", p=128))
            t_at = P.dma("pl", P.dgroup("c_at3"), at3[:, :], at_h[:, :])

            g_wot = [P.dgroup(f"wot{i}") for i in range(2)]
            g_x3 = [P.dgroup(f"x3ld{i}") for i in range(2)]

            with ExitStack() as p3a:
                o_ps = [p3a.enter_context(nc.psum_tensor(f"o_ps{i}", [128, 512], F32))
                        for i in range(8)]

                uo_tok = None
                for qh in range(2):
                    mm = []
                    for dc in range(NDC):
                        d0 = [t_bt]
                        if dc == 0 and uo_tok:
                            d0.append(uo_tok)
                        mm.append(P.op("pe", lambda e, dc=dc, qh=qh: e.matmul(
                            o_ps[0][0:RK, :], bt3[:, dc * 64 + 48: dc * 64 + 64],
                            ao_sb[:, dc * TPC + qh * 512: dc * TPC + qh * 512 + 512],
                            start=(dc == 0), stop=(dc == NDC - 1)), deps=d0))
                    uo_tok = P.op("act", lambda e, qh=qh: e.mul(
                        ut[:, 3 * TPC + qh * 512:3 * TPC + qh * 512 + 512],
                        o_ps[0][0:RK, :], 0.5), deps=mm)
                uto_ready = uo_tok

                wot_use = [None, None]
                x1_tok = [None, None]
                xt3_tok = [None, None]
                ln2_tiles = []
                evac_tok = {}
                for tp in range(4):
                    seed = {}
                    for i in range(2):
                        for oc in range(4):
                            g = 4 * i + oc
                            d0 = [t_or, t_bo, uto_ready]
                            pv = evac_tok.get((tp - 1, g))
                            if pv:
                                d0.append(pv)
                            seed[g] = P.op("pe", lambda e, g=g, oc=oc: e.matmul(
                                o_ps[g][:, :], onesrow[:, :],
                                bo_row[0:1, oc * 512:(oc + 1) * 512],
                                start=True, stop=False), deps=d0)
                    last_mm = {g: [seed[g]] for g in range(8)}
                    for dc in range(NDC):
                        slot = dc % 2
                        wdeps = [wot_use[slot]] if wot_use[slot] else []
                        P.dma("pl", g_wot[slot], wotb[slot][:, :],
                              wot[dc * 128:(dc + 1) * 128, :], deps=wdeps)
                        wtok = g_wot[slot].tok()
                        lw = None
                        for i in range(2):
                            tt = 2 * tp + i
                            for oc in range(4):
                                g = 4 * i + oc
                                lw = P.op("pe", lambda e, g=g, slot=slot, dc=dc, tt=tt, oc=oc: e.matmul(
                                    o_ps[g][:, :],
                                    ao_sb[:, dc * TPC + tt * 128: dc * TPC + (tt + 1) * 128],
                                    wotb[slot][:, oc * 512:(oc + 1) * 512],
                                    start=False, stop=False), deps=[wtok])
                                last_mm[g].append(lw)
                        wot_use[slot] = lw
                    for i in range(2):
                        tt = 2 * tp + i
                        sl = tt % 2
                        xdeps = [xt3_tok[sl]] if xt3_tok[sl] else []
                        t_x = P.dma("sp", g_x3[sl], xt3[sl][:, :],
                                    xr[tt * 128:(tt + 1) * 128, :], deps=xdeps)
                        tile_done = []
                        for oc in range(4):
                            g = 4 * i + oc
                            t_lo = P.op("pe", lambda e, g=g, tt=tt, oc=oc: e.matmul(
                                o_ps[g][:, :],
                                ut[:, 3 * TPC + tt * 128:3 * TPC + (tt + 1) * 128],
                                at3[:, 3 * D + oc * 512:3 * D + (oc + 1) * 512],
                                start=False, stop=True), deps=[uto_ready, t_at])
                            dE = [t_lo, t_x] + last_mm[g]
                            if x1_tok[sl]:
                                dE.append(x1_tok[sl])
                            tkv = P.op("dve", lambda e, g=g, sl=sl, oc=oc: e.tensor_tensor(
                                x1t[sl][:, oc * 512:(oc + 1) * 512], o_ps[g][:, :],
                                xt3[sl][:, oc * 512:(oc + 1) * 512], ALU.add), deps=dE)
                            evac_tok[(tp, g)] = tkv
                            tile_done.append(tkv)
                        xt3_tok[sl] = tile_done[-1]
                        t_ln2 = _ln_chain(P, x1t[sl][:, :],
                                          ln2tm[:, tt * D:(tt + 1) * D], scr3,
                                          tile_done)
                        x1_tok[sl] = P.dma("sp", g_x1s[sl],
                                           x1_sp[tt * 128:(tt + 1) * 128, :],
                                           x1t[sl][:, :], deps=[t_ln2])
                        ln2_tiles.append((tt, t_ln2))
                last_o_evac = evac_tok[(3, 7)]

            P.barrier()
            with nc.psum_tensor("tps3", [128, 512], F32) as tps3:
                tp3_tok = None
                for tt, t_ln2 in ln2_tiles:
                    for g in range(4):
                        mm = []
                        for j in range(4):
                            dc = 4 * g + j
                            d0 = [t_ln2, t_eye]
                            if tp3_tok:
                                d0.append(tp3_tok)
                            mm.append(P.op("pe", lambda e, tt=tt, dc=dc, j=j: e.matmul(
                                tps3[:, j * 128:(j + 1) * 128],
                                ln2tm[:, tt * D + dc * 128: tt * D + (dc + 1) * 128],
                                eye[:, :], start=True, stop=True), deps=d0))
                        tp3_tok = P.op("act", lambda e, tt=tt, g=g: e.copy(
                            ln2t[:, :].rearrange("p (c n) -> p c n", n=TPC)
                            [:, 4 * g:4 * g + 4, tt * 128:(tt + 1) * 128],
                            tps3[:, :].rearrange("p (c n) -> p c n", n=128)), deps=mm)
                ln2t_ready = tp3_tok

            if "d_ln2t" in dbg_aps:
                P.dma("pl", g_cpl, dbg_aps["d_ln2t"][:, :], ln2t[:, :],
                      deps=[ln2t_ready])
            if "d_x1" in dbg_aps:
                pass

        P.barrier()
        s_ao.close()

        for _e in ENGINES:
            P.wait(_e, [g_x1s[0].tok(), g_x1s[1].tok()])
        with ExitStack() as p4:
            sb4 = lambda n, s, d: p4.enter_context(nc.sbuf_tensor(n, s, d))
            fuse = sb4("fuse", [128, 64 * 512], BF16)
            wgu = [sb4(f"wgu{i}", [128, NDC * 256], BF16) for i in range(2)]
            wub = [sb4(f"wub{i}", [128, NDC * 256], BF16) for i in range(2)]
            wdb = [sb4(f"wdb{i}", [128, 8 * 512], BF16) for i in range(2)]
            gtmp = sb4("gtmp", [128, 512], F32)
            utmp = sb4("utmp", [128, 512], F32)
            dtmp = sb4("dtmp", [128, 512], F32)
            x1rd = [sb4(f"x1rd{i}", [128, 512], F32) for i in range(2)]
            x2t = [sb4(f"x2t{i}", [128, 512], F32) for i in range(2)]
            sgu = sb4("sgu_sb", [128, 128], F32)
            sdb = sb4("sdb_sb", [128, D], F32)
            t_sgu = P.dma("sp", P.dgroup("c_sgu"), sgu[:, :], sgu_h[:, :])
            with nc.sbuf_tensor("sdrow_sb", [1, D], F32) as sdrow, \
                 nc.psum_tensor("sdb_ps", [128, D], F32) as sdb_ps:
                t_sdrow = P.dma("sp", P.dgroup("c_sdrow"), sdrow[:, :], sd_h[:, :])
                toks = []
                for j in range(4):
                    toks.append(P.op("pe", lambda e, j=j: e.matmul(
                        sdb_ps[:, j * 512:(j + 1) * 512], onesrow_f[:, :],
                        sdrow[0:1, j * 512:(j + 1) * 512], start=True, stop=True),
                        deps=[t_orf, t_sdrow]))
                t_sdb = P.op("dve", lambda e: e.tensor_copy(sdb[:, :], sdb_ps[:, :]),
                             deps=toks)

            g_wg = [P.dgroup(f"wg{i}") for i in range(2)]
            g_wd = [P.dgroup(f"wd{i}") for i in range(2)]
            g_x1r = [P.dgroup(f"x1rd{i}") for i in range(2)]
            g_out = [P.dgroup(f"outst{i}") for i in range(2)]

            wgu_use = [None, None]
            wd_use = [None, None]
            x2_tok = [None, None]
            x1rd_reader = [None, None]
            prev_ftok = None
            prev_x2 = None

            for th in range(2):
                with ExitStack() as p4a:
                    gu_ps = [p4a.enter_context(
                        nc.psum_tensor(f"gu_ps{th}_{i}", [128, 512], F32))
                        for i in range(4)]
                    gu_tok = [None] * 4
                    fuse_tok = {}
                    for grp in range(32):
                        slot = grp % 2
                        wdeps = [wgu_use[slot]] if wgu_use[slot] else []
                        P.dma("pl", g_wg[slot],
                              wgu[slot][:, :].rearrange("p (c o) -> p c o", o=256),
                              wg_h[:, grp * 256:(grp + 1) * 256]
                              .rearrange("(c p) o -> p c o", p=128), deps=wdeps)
                        P.dma("pl", g_wg[slot],
                              wub[slot][:, :].rearrange("p (c o) -> p c o", o=256),
                              wu_h[:, grp * 256:(grp + 1) * 256]
                              .rearrange("(c p) o -> p c o", p=128), deps=[])
                        wtok = g_wg[slot].tok()
                        lastg = None
                        for ii in range(2):
                            it = grp * 2 + ii
                            pg = it % 2
                            pu = 2 + it % 2
                            mmg, mmu = [], []
                            for hc in range(NDC):
                                d0 = []
                                if hc == 0:
                                    d0 = [wtok, ln2t_ready, t_sdb]
                                    if gu_tok[pg]:
                                        d0.append(gu_tok[pg])
                                d0u = []
                                if hc == 0:
                                    d0u = [wtok]
                                    if gu_tok[pu]:
                                        d0u.append(gu_tok[pu])
                                mmg.append(P.op("pe", lambda e, pg=pg, slot=slot, hc=hc, ii=ii, th=th: e.matmul(
                                    gu_ps[pg][:, :],
                                    wgu[slot][:, hc * 256 + ii * 128: hc * 256 + (ii + 1) * 128],
                                    ln2t[:, hc * TPC + th * 512: hc * TPC + th * 512 + 512],
                                    start=(hc == 0), stop=(hc == NDC - 1)), deps=d0))
                                mmu.append(P.op("pe", lambda e, pu=pu, slot=slot, hc=hc, ii=ii, th=th: e.matmul(
                                    gu_ps[pu][:, :],
                                    wub[slot][:, hc * 256 + ii * 128: hc * 256 + (ii + 1) * 128],
                                    ln2t[:, hc * TPC + th * 512: hc * TPC + th * 512 + 512],
                                    start=(hc == 0), stop=(hc == NDC - 1)), deps=d0u))
                            lastg = mmu[-1]
                            gdeps = mmg + [t_sgu]
                            udeps = mmu + [t_sgu]
                            if prev_ftok:
                                gdeps.append(prev_ftok)
                                udeps.append(prev_ftok)
                            t_g = P.op("act", lambda e, pg=pg, it=it: e.activation(
                                gtmp[:, :], gu_ps[pg][:, :], AF.Silu,
                                scale=sgu[:, it:it + 1]), deps=gdeps)
                            t_u = P.op("dve", lambda e, pu=pu, it=it: e.tensor_scalar(
                                utmp[:, :], gu_ps[pu][:, :], sgu[:, 64 + it:65 + it],
                                None, ALU.mult), deps=udeps)
                            ftok = P.op("dve", lambda e, it=it: e.tensor_tensor(
                                fuse[:, it * 512:(it + 1) * 512], gtmp[:, :],
                                utmp[:, :], ALU.mult), deps=[t_g, t_u])
                            gu_tok[pg] = ftok
                            gu_tok[pu] = ftok
                            fuse_tok[it] = ftok
                            prev_ftok = ftok
                        wgu_use[slot] = lastg
                    fuse_all = [fuse_tok[63]]

                P.barrier()

                with ExitStack() as p4b:
                    dn_ps = [p4b.enter_context(
                        nc.psum_tensor(f"dn_ps{th}_{i}", [128, 512], F32))
                        for i in range(8)]
                    dn_evac = {}
                    for oc in range(4):
                        base = 4 * (oc % 2)
                        lastd = None
                        for icg in range(8):
                            slot = icg % 2
                            wdeps = [wd_use[slot]] if wd_use[slot] else []
                            P.dma("pl", g_wd[slot],
                                  wdb[slot][:, :].rearrange("p (c o) -> p c o", o=512),
                                  wd_h[icg * 1024:(icg + 1) * 1024,
                                       oc * 512:(oc + 1) * 512]
                                  .rearrange("(c p) o -> p c o", p=128), deps=wdeps)
                            wtok = g_wd[slot].tok()
                            for ici in range(8):
                                ic = icg * 8 + ici
                                for ts in range(4):
                                    d0 = [wtok]
                                    if ic == 0 and ts == 0:
                                        d0 += fuse_all
                                        for tsx in range(4):
                                            if (oc - 2, tsx) in dn_evac:
                                                d0.append(dn_evac[(oc - 2, tsx)])
                                    lastd = P.op("pe", lambda e, base=base, ts=ts, slot=slot, ici=ici, ic=ic: e.matmul(
                                        dn_ps[base + ts][:, :],
                                        fuse[:, ic * 512 + ts * 128: ic * 512 + (ts + 1) * 128],
                                        wdb[slot][:, ici * 512:(ici + 1) * 512],
                                        start=(ic == 0), stop=(ic == 63)), deps=d0)
                            wd_use[slot] = lastd
                        for ts in range(4):
                            xsl = ts % 2
                            xd = [g_x1s[0].tok(), g_x1s[1].tok()]
                            if x1rd_reader[xsl]:
                                xd.append(x1rd_reader[xsl])
                            t_x1r = P.dma("sp", g_x1r[xsl],
                                          x1rd[xsl][:, :],
                                          bass.AP(x1_sp.tensor,
                                                  (th * 512 + ts * 128) * D + oc * 512,
                                                  [[D, 128], [1, 512]]), deps=xd)
                            d0 = [lastd, t_sdb]
                            if (oc - 1, ts) in dn_evac:
                                d0.append(dn_evac[(oc - 1, ts)])
                            if prev_x2:
                                d0.append(prev_x2)
                            t_sc = P.op("dve", lambda e, base=base, ts=ts, oc=oc: e.tensor_tensor(
                                dtmp[:, :], dn_ps[base + ts][:, :],
                                sdb[:, oc * 512:(oc + 1) * 512], ALU.mult), deps=d0)
                            d1 = [t_sc, t_x1r]
                            if x2_tok[xsl]:
                                d1.append(x2_tok[xsl])
                            t_x2 = P.op("dve", lambda e, ts=ts, xsl=xsl, oc=oc: e.tensor_tensor(
                                x2t[xsl][:, :], dtmp[:, :],
                                x1rd[xsl][:, :], ALU.add), deps=d1)
                            prev_x2 = t_x2
                            dn_evac[(oc, ts)] = t_x2
                            x1rd_reader[xsl] = t_x2
                            x2_tok[xsl] = P.dma("sp", g_out[xsl],
                                                bass.AP(out_h.tensor,
                                                        (th * 512 + ts * 128) * D + oc * 512,
                                                        [[D, 128], [1, 512]]),
                                                x2t[xsl][:, :], deps=[t_x2])
                if th == 0:
                    P.barrier()

            P.wait("sp", [g_out[0].tok(), g_out[1].tok()])

        P.finalize()
    return nc


def _host_layouts(inp):
    f = np.ascontiguousarray
    a = {k: np.asarray(v) for k, v in inp.items()}
    shared = {
        "wqt": f(a["L0_attn_Wq"].T),
        "wkt": f(a["L0_attn_Wk"].T),
        "wvt": f(a["L0_attn_Wv"].T),
        "wot": f(a["L0_attn_Wo"].T),
        "bqkv": f(np.concatenate([a["L0_attn_bq"].reshape(16, 128).T,
                                  a["L0_attn_bk"].reshape(16, 128).T,
                                  a["L0_attn_bv"].reshape(16, 128).T], axis=1)),
        "bo_row": f(a["L0_attn_bo"].reshape(1, D)),
        "bv_row": f(a["L0_attn_bv"].reshape(1, D)),
        "btcat": f(np.concatenate([a["L0_attn_Bq"].T, a["L0_attn_Bk"].T,
                                   a["L0_attn_Bv"].T, a["L0_attn_Bo"].T], axis=1)),
        "atcat": f(np.concatenate([a["L0_attn_Aq"].T, a["L0_attn_Ak"].T,
                                   a["L0_attn_Av"].T, a["L0_attn_Ao"].T], axis=1)),
        "sgu": f(np.concatenate([a["L0_ffn_s_gate"].reshape(64, 128).T,
                                 a["L0_ffn_s_up"].reshape(64, 128).T], axis=1)),
        "sdrow": f(a["L0_ffn_s_down"].reshape(1, D)),
        "wg": a["L0_ffn_w_gate"],
        "wu": a["L0_ffn_w_up"],
        "wd": a["L0_ffn_w_down"],
    }
    x = a["L0_attn_x"].reshape(SEQ, D)
    in_maps = []
    for r in range(NCORES):
        m = dict(shared)
        m["x"] = f(x[r * TPC:(r + 1) * TPC, :])
        in_maps.append(m)
    return in_maps


def kernel(**inputs):
    in_maps = _host_layouts(inputs)
    nc = build_nc()
    res = run_bass_kernel_spmd(nc, in_maps, core_ids=list(range(NCORES)))
    out = np.concatenate([np.asarray(r["out"]) for r in res.results], axis=0)
    return out.reshape(1, SEQ, D).astype(np.float32, copy=False)


# version 09
# speedup vs baseline: 1.0623x; 1.0161x over previous
import numpy as np
from contextlib import ExitStack

import concourse.bass as bass
import concourse.mybir as mybir
from concourse.bass_utils import run_bass_kernel_spmd
from concourse.replica_groups import maybe_share_collective_output_space

F32 = mybir.dt.float32
BF16 = mybir.dt.bfloat16
I8 = mybir.dt.int8
AF = mybir.ActivationFunctionType
ALU = mybir.AluOpType
AX = mybir.AxisListType

NCORES = 8
SEQ = 8192
TPC = SEQ // NCORES
D = 2048
H = 16
HD = 128
RK = 16
INTER = 8192
NDC = D // 128
NTT = TPC // 128
SCALE = float(HD) ** -0.5
EXP_SHIFT = -3.0
EPS = 1e-5

ENGINES = ("sp", "pe", "act", "dve", "pl")
_SECTION = {"sp": "sync", "pe": "tensor", "act": "scalar", "dve": "vector", "pl": "gpsimd"}


class DGroup:
    def __init__(self, sem):
        self.sem = sem
        self.n = 0

    def tok(self):
        return (self.sem, self.n * 16)


class Prog:

    def __init__(self, nc, ctx):
        self.nc = nc
        self.ctx = ctx
        self.blocks = []
        self.cur = {e: [] for e in ENGINES}
        self.esem = {e: ctx.enter_context(nc.semaphore(f"es_{e}")) for e in ENGINES}
        self.ecount = {e: 0 for e in ENGINES}
        self.waited = {e: {} for e in ENGINES}

    def dgroup(self, name):
        return DGroup(self.ctx.enter_context(self.nc.semaphore(f"dg_{name}")))

    def _w(self, eng, deps):
        out = []
        w = self.waited[eng]
        for d in deps:
            if d is None:
                continue
            sem, val = d
            if w.get(sem, 0) >= val:
                continue
            w[sem] = val
            out.append((sem, val))
        return out

    def op(self, eng, fn, deps=(), auto=True, inc_extra=None):
        waits = self._w(eng, deps)
        tok = None
        if auto:
            self.ecount[eng] += 1
            tok = (self.esem[eng], self.ecount[eng])

        def emit(e, fn=fn, waits=waits, tok=tok, inc_extra=inc_extra):
            for s, v in waits:
                e.wait_ge(s, v)
            ins = fn(e)
            if tok is not None:
                ins = ins.then_inc(tok[0], 1)
            if inc_extra is not None:
                ins.then_inc(*inc_extra)

        self.cur[eng].append(emit)
        return tok

    def dma(self, eng, group, out, in_, deps=()):
        waits = self._w(eng, deps)
        group.n += 1
        tok = (group.sem, group.n * 16)

        def emit(e, out=out, in_=in_, waits=waits, gsem=group.sem):
            for s, v in waits:
                e.wait_ge(s, v)
            e.dma_start(out=out, in_=in_).then_inc(gsem, 16)

        self.cur[eng].append(emit)
        return tok

    def wait(self, eng, deps):
        waits = self._w(eng, deps)
        if not waits:
            return

        def emit(e, waits=waits):
            for s, v in waits:
                e.wait_ge(s, v)

        self.cur[eng].append(emit)

    def barrier(self):
        self.blocks.append(self.cur)
        self.cur = {e: [] for e in ENGINES}
        for e2 in ENGINES:
            for e1 in ENGINES:
                s = self.esem[e1]
                self.waited[e2][s] = max(self.waited[e2].get(s, 0), self.ecount[e1])

    def finalize(self):
        if any(self.cur[e] for e in ENGINES):
            self.barrier()
        nc = self.nc
        for blk_ops in self.blocks:
            with nc.Block() as blk:
                for eng in ENGINES:
                    ops = blk_ops[eng]
                    if not ops:
                        continue

                    def section(e, ops=ops):
                        for emit in ops:
                            emit(e)

                    getattr(blk, _SECTION[eng])(section)


def _ln_chain(P, x_ap, out_ap, scr, deps):
    xsq, s1t, s2t, mt, msqt, biast, stdt, rstdt = scr
    t_s1 = P.op("dve", lambda e: e.reduce_sum(s1t[:, :], x_ap, axis=AX.X), deps=deps)
    t_sq = P.op("act", lambda e: e.activation(xsq[:, :], x_ap, AF.Square),
                deps=deps + [t_s1])
    t_s2 = P.op("dve", lambda e: e.reduce_sum(s2t[:, :], xsq[:, :], axis=AX.X),
                deps=[t_sq])
    t_m = P.op("dve", lambda e: e.tensor_scalar(mt[:, :], s1t[:, :], 1.0 / D, None,
                                                ALU.mult), deps=[t_s1])
    t_msq = P.op("dve", lambda e: e.tensor_tensor(msqt[:, :], mt[:, :], mt[:, :],
                                                  ALU.mult), deps=[t_m])
    t_bias = P.op("dve", lambda e: e.tensor_scalar(biast[:, :], msqt[:, :], EPS, -1.0,
                                                   ALU.subtract, ALU.mult), deps=[t_msq])
    t_std = P.op("act", lambda e: e.activation(stdt[:, :], s2t[:, :], AF.Sqrt,
                                               bias=biast[:, 0:1], scale=1.0 / D),
                 deps=[t_s2, t_bias])
    t_rstd = P.op("dve", lambda e: e.reciprocal(rstdt[:, :], stdt[:, :]), deps=[t_std])
    t_ln = P.op("dve", lambda e: e.tensor_scalar(out_ap, x_ap, mt[:, 0:1],
                                                 rstdt[:, 0:1], ALU.subtract, ALU.mult),
                deps=[t_rstd, t_m])
    return t_ln


def build_nc(dbg=()):
    nc = bass.Bass("TRN2", target_bir_lowering=False)

    xr = nc.dram_tensor("x", [TPC, D], F32, kind="ExternalInput").ap()
    wqt = nc.dram_tensor("wqt", [D, D], F32, kind="ExternalInput").ap()
    wkt = nc.dram_tensor("wkt", [D, D], F32, kind="ExternalInput").ap()
    wvt = nc.dram_tensor("wvt", [D, D], F32, kind="ExternalInput").ap()
    wot = nc.dram_tensor("wot", [D, D], F32, kind="ExternalInput").ap()
    bqkv_h = nc.dram_tensor("bqkv", [128, 48], F32, kind="ExternalInput").ap()
    bo_h = nc.dram_tensor("bo_row", [1, D], F32, kind="ExternalInput").ap()
    bv_h = nc.dram_tensor("bv_row", [1, D], F32, kind="ExternalInput").ap()
    bt_h = nc.dram_tensor("btcat", [D, 4 * RK], F32, kind="ExternalInput").ap()
    at_h = nc.dram_tensor("atcat", [RK, 4 * D], F32, kind="ExternalInput").ap()
    sgu_h = nc.dram_tensor("sgu", [128, 128], F32, kind="ExternalInput").ap()
    sd_h = nc.dram_tensor("sdrow", [1, D], F32, kind="ExternalInput").ap()
    wg_h = nc.dram_tensor("wg", [D, INTER], I8, kind="ExternalInput").ap()
    wu_h = nc.dram_tensor("wu", [D, INTER], I8, kind="ExternalInput").ap()
    wd_h = nc.dram_tensor("wd", [INTER, D], I8, kind="ExternalInput").ap()
    out_h = nc.dram_tensor("out", [TPC, D], F32, kind="ExternalOutput").ap()

    cc_in = nc.dram_tensor("cc_in", [4096, TPC], BF16, kind="Internal").ap()
    aspace = maybe_share_collective_output_space("AllGather", [list(range(NCORES))])
    cc_out = nc.dram_tensor("cc_out", [NCORES * 4096, TPC], BF16, kind="Internal",
                            addr_space=aspace).ap()
    x1_sp = nc.dram_tensor("x1_spill", [TPC, D], F32, kind="Internal").ap()

    dbg_aps = {}
    for name, shape in (("d_lnt", [128, NDC * TPC]), ("d_q", [128, NDC * TPC]),
                        ("d_ao", [128, NDC * TPC]), ("d_x1", [TPC, D]),
                        ("d_ln2t", [128, NDC * TPC])):
        if name in dbg:
            dbg_aps[name] = nc.dram_tensor(name, shape, F32, kind="ExternalOutput").ap()

    with ExitStack() as ctx:
        P = Prog(nc, ctx)
        sb = lambda n, s, d: ctx.enter_context(nc.sbuf_tensor(n, s, d))

        eye = sb("eye", [128, 128], BF16)
        onescol = sb("onescol", [128, 1], BF16)
        onesrow = sb("onesrow", [1, 128], BF16)
        onesrow_f = sb("onesrow_f", [1, 128], F32)
        ut = sb("ut_sb", [RK, 4 * TPC], BF16)

        g_cpl = P.dgroup("c_pl")
        g_csp = P.dgroup("c_sp")

        with nc.sbuf_tensor("eyef", [128, 128], F32) as eyef:
            t_iota = P.op("pl", lambda e: e.iota(eyef[:, :], [[1, 128]],
                                                 channel_multiplier=-1,
                                                 allow_small_or_imprecise_dtypes=True))
            t_eye = P.op("dve", lambda e: e.tensor_scalar(eye[:, :], eyef[:, :], 0.0,
                                                          None, ALU.is_equal),
                         deps=[t_iota])
        t_oc = P.op("pl", lambda e: e.memset(onescol[:, :], 1.0))
        t_or = P.op("pl", lambda e: e.memset(onesrow[:, :], 1.0))
        t_orf = P.op("pl", lambda e: e.memset(onesrow_f[:, :], 1.0))

        s_q = ExitStack()
        q_sb = s_q.enter_context(nc.sbuf_tensor("q_sb", [128, NDC * TPC], BF16))
        cc_sem = ctx.enter_context(nc.semaphore("cc_sem"))

        with ExitStack() as p1:
            sb1 = lambda n, s, d: p1.enter_context(nc.sbuf_tensor(n, s, d))
            ln1t = sb1("ln1t", [128, NDC * TPC], BF16)
            xt = [sb1(f"xt{i}", [128, D], F32) for i in range(2)]
            lnt = [sb1(f"lnt{i}", [128, D], BF16) for i in range(2)]
            scr = (sb1("xsq", [128, D], F32), sb1("s1t", [128, 1], F32),
                   sb1("s2t", [128, 1], F32), sb1("mt", [128, 1], F32),
                   sb1("msqt", [128, 1], F32), sb1("biast", [128, 1], F32),
                   sb1("stdt", [128, 1], F32), sb1("rstdt", [128, 1], F32))
            wslab = [sb1(f"wslab{i}", [128, NDC * 512], BF16) for i in range(2)]
            ktile = [sb1(f"ktile{i}", [128, TPC], BF16) for i in range(2)]
            vtile = [sb1(f"vtile{i}", [128, 512], BF16) for i in range(2)]
            bqkv = sb1("bqkv_sb", [128, 48], F32)
            bv_row = sb1("bv_sb", [1, D], BF16)
            bt = sb1("bt_sb", [128, NDC * 4 * RK], BF16)
            at = sb1("at_sb", [RK, 4 * D], BF16)
            t_bqkv = P.dma("sp", P.dgroup("c_bqkv"), bqkv[:, :], bqkv_h[:, :])
            t_bv = P.dma("pl", P.dgroup("c_bv"), bv_row[:, :], bv_h[:, :])
            t_bt = P.dma("pl", P.dgroup("c_bt1"),
                         bt[:, :].rearrange("p (c r) -> p c r", r=4 * RK),
                         bt_h.rearrange("(c p) r -> p c r", p=128))
            t_at = P.dma("pl", P.dgroup("c_at1"), at[:, :], at_h[:, :])

            tps = p1.enter_context(nc.psum_tensor("tps", [128, 512], F32))
            u_ps = p1.enter_context(nc.psum_tensor("u_ps", [RK, 512], F32))
            qk_ps = [p1.enter_context(nc.psum_tensor(f"qk_ps{i}", [128, TPC], F32))
                     for i in range(2)]
            v_ps = [p1.enter_context(nc.psum_tensor(f"v_ps{i}", [128, 512], F32))
                    for i in range(2)]

            g_x = [P.dgroup(f"x1ld{i}") for i in range(2)]
            g_ws = [P.dgroup(f"ws{i}") for i in range(2)]
            g_kst = [P.dgroup(f"kst{i}") for i in range(2)]
            g_vst = [P.dgroup(f"vst{i}") for i in range(2)]

            wsrc = {0: wqt, 1: wkt, 2: wvt}
            slab_loads = {}
            slab_loads[0] = P.dma("pl", g_ws[0],
                                  wslab[0][:, :].rearrange("p (c o) -> p c o", o=512),
                                  wqt[:, 0:512].rearrange("(c p) o -> p c o", p=128))

            xt_tok = [None, None]
            lnt_tok = [None, None]
            tp_tok = None
            for tt in range(NTT):
                sl = tt % 2
                deps = [xt_tok[sl]] if xt_tok[sl] else []
                t_x = P.dma("sp", g_x[sl], xt[sl][:, :], xr[tt * 128:(tt + 1) * 128, :],
                            deps=deps)
                nd = [t_x]
                if lnt_tok[sl]:
                    nd.append(lnt_tok[sl])
                t_ln = _ln_chain(P, xt[sl][:, :], lnt[sl][:, :], scr, nd)
                xt_tok[sl] = t_ln
                for g in range(4):
                    mm = []
                    for j in range(4):
                        dc = 4 * g + j
                        d0 = [t_ln, t_eye]
                        if tp_tok:
                            d0.append(tp_tok)
                        mm.append(P.op("pe", lambda e, sl=sl, dc=dc, j=j: e.matmul(
                            tps[:, j * 128:(j + 1) * 128],
                            lnt[sl][:, dc * 128:(dc + 1) * 128], eye[:, :],
                            start=True, stop=True), deps=d0))
                    tp_tok = P.op("act", lambda e, tt=tt, g=g: e.copy(
                        ln1t[:, :].rearrange("p (c n) -> p c n", n=TPC)
                        [:, 4 * g:4 * g + 4, tt * 128:(tt + 1) * 128],
                        tps[:, :].rearrange("p (c n) -> p c n", n=128)), deps=mm)
                    lnt_tok[sl] = tp_tok
            ln1t_ready = tp_tok

            if "d_lnt" in dbg_aps:
                P.dma("pl", g_cpl, dbg_aps["d_lnt"][:, :], ln1t[:, :],
                      deps=[ln1t_ready])

            ut_ready = []
            prev_u = None
            for l in range(3):
                for qh in range(2):
                    mm = []
                    for dc in range(NDC):
                        d0 = [ln1t_ready, t_bt]
                        if dc == 0 and prev_u:
                            d0.append(prev_u)
                        mm.append(P.op("pe", lambda e, l=l, dc=dc, qh=qh: e.matmul(
                            u_ps[:, :],
                            bt[:, dc * 64 + l * RK: dc * 64 + (l + 1) * RK],
                            ln1t[:, dc * TPC + qh * 512: dc * TPC + qh * 512 + 512],
                            start=(dc == 0), stop=(dc == NDC - 1)), deps=d0))
                    prev_u = P.op("act", lambda e, l=l, qh=qh: e.mul(
                        ut[:, l * TPC + qh * 512: l * TPC + qh * 512 + 512],
                        u_ps[:, :], 0.5), deps=mm)
                ut_ready.append(prev_u)
            ut_all = ut_ready[2]

            loads = [(l, oq) for l in range(3) for oq in range(4)]
            slab_use = {}
            qkps_tok = [None, None]
            vps_tok = [None, None]
            ktile_tok = [None, None]
            vtile_tok = [None, None]

            for idx, (l, oq) in enumerate(loads):
                slot = idx % 2
                if idx == 0:
                    t_slab = slab_loads[0]
                else:
                    wdeps = []
                    u = slab_use.get(idx - 2)
                    if u:
                        wdeps.append(u)
                    t_slab = P.dma("pl", g_ws[slot],
                                   wslab[slot][:, :].rearrange("p (c o) -> p c o", o=512),
                                   wsrc[l][:, oq * 512:(oq + 1) * 512]
                                   .rearrange("(c p) o -> p c o", p=128), deps=wdeps)
                last_pe = None
                if l < 2:
                    for j in range(4):
                        ot = oq * 4 + j
                        pp = ot % 2
                        mm = []
                        for qh in range(2):
                            for dc in range(NDC):
                                d0 = []
                                if dc == 0:
                                    d0 = [t_slab, ln1t_ready, ut_all, t_at]
                                    if qh == 0 and qkps_tok[pp]:
                                        d0.append(qkps_tok[pp])
                                mm.append(P.op("pe", lambda e, pp=pp, slot=slot, j=j, dc=dc, qh=qh: e.matmul(
                                    qk_ps[pp][:, qh * 512:qh * 512 + 512],
                                    wslab[slot][:, dc * 512 + j * 128: dc * 512 + (j + 1) * 128],
                                    ln1t[:, dc * TPC + qh * 512: dc * TPC + qh * 512 + 512],
                                    start=(dc == 0), stop=False), deps=d0))
                            mm.append(P.op("pe", lambda e, pp=pp, l=l, ot=ot, qh=qh: e.matmul(
                                qk_ps[pp][:, qh * 512:qh * 512 + 512],
                                at[:, l * D + ot * 128:l * D + (ot + 1) * 128],
                                ut[:, l * TPC + qh * 512:l * TPC + qh * 512 + 512],
                                start=False, stop=True), deps=[]))
                        last_pe = mm[-1]
                        if l == 0:
                            qkps_tok[pp] = P.op("act", lambda e, pp=pp, ot=ot: e.activation(
                                q_sb[:, ot * TPC:(ot + 1) * TPC], qk_ps[pp][:, :],
                                AF.Identity, bias=bqkv[:, ot:ot + 1]), deps=mm + [t_bqkv])
                        else:
                            ksl = ot % 2
                            d1 = mm + [t_bqkv]
                            if ktile_tok[ksl]:
                                d1.append(ktile_tok[ksl])
                            tk = P.op("act", lambda e, pp=pp, ot=ot, ksl=ksl: e.activation(
                                ktile[ksl][:, :], qk_ps[pp][:, :], AF.Identity,
                                bias=bqkv[:, 16 + ot:17 + ot]), deps=d1)
                            qkps_tok[pp] = tk
                            ktile_tok[ksl] = P.dma("sp", g_kst[ksl],
                                                   cc_in[ot * 128:(ot + 1) * 128, :],
                                                   ktile[ksl][:, :], deps=[tk])
                else:
                    for tt in range(NTT):
                        pp = tt % 2
                        mm = []
                        d0 = [t_slab, ln1t_ready, t_or, t_bv, ut_all, t_at]
                        if vps_tok[pp]:
                            d0.append(vps_tok[pp])
                        mm.append(P.op("pe", lambda e, pp=pp, oq=oq: e.matmul(
                            v_ps[pp][:, :], onesrow[:, :],
                            bv_row[0:1, oq * 512:(oq + 1) * 512],
                            start=True, stop=False), deps=d0))
                        for dc in range(NDC):
                            mm.append(P.op("pe", lambda e, pp=pp, slot=slot, dc=dc, tt=tt: e.matmul(
                                v_ps[pp][:, :],
                                ln1t[:, dc * TPC + tt * 128: dc * TPC + (tt + 1) * 128],
                                wslab[slot][:, dc * 512:(dc + 1) * 512],
                                start=False, stop=False), deps=[]))
                        mm.append(P.op("pe", lambda e, pp=pp, tt=tt, oq=oq: e.matmul(
                            v_ps[pp][:, :],
                            ut[:, 2 * TPC + tt * 128:2 * TPC + (tt + 1) * 128],
                            at[:, 2 * D + oq * 512:2 * D + (oq + 1) * 512],
                            start=False, stop=True), deps=[]))
                        last_pe = mm[-1]
                        vsl = tt % 2
                        d1 = list(mm)
                        if vtile_tok[vsl]:
                            d1.append(vtile_tok[vsl])
                        tv = P.op("act", lambda e, pp=pp, vsl=vsl: e.copy(
                            vtile[vsl][:, :], v_ps[pp][:, :]), deps=d1)
                        vps_tok[pp] = tv
                        vtile_tok[vsl] = P.dma(
                            "sp", g_vst[vsl],
                            bass.AP(cc_in.tensor,
                                    (2048 + 2 * tt * 128 + oq // 2) * TPC + (oq % 2) * 512,
                                    [[2 * TPC, 128], [1, 512]]),
                            vtile[vsl][:, :], deps=[tv])
                slab_use[idx] = last_pe

            P.op("pl", lambda e: e.collective_compute(
                "AllGather", ALU.bypass, replica_groups=[list(range(NCORES))],
                ins=[cc_in], outs=[cc_out]),
                deps=[g_kst[0].tok(), g_kst[1].tok(), g_vst[0].tok(),
                      g_vst[1].tok()], auto=False, inc_extra=(cc_sem, 1))
            cc_tok = (cc_sem, 1)

        P.barrier()

        s_ao = ExitStack()
        ao_sb = s_ao.enter_context(nc.sbuf_tensor("ao_sb", [128, NDC * TPC], BF16,
                                                   side="right"))

        with ExitStack() as p2:
            sb2 = lambda n, s, d: p2.enter_context(nc.sbuf_tensor(n, s, d))
            kh = [sb2(f"kh{i}", [128, SEQ], BF16) for i in range(2)]
            vp = [sb2(f"vp{i}", [128, 64 * 256], BF16) for i in range(2)]
            pt = [sb2(f"pt{i}", [128, TPC], BF16) for i in range(4)]
            acc_a = sb2("acc_a", [128, TPC], F32)
            acc_b = sb2("acc_b", [128, TPC], F32)
            onesf = sb2("onesf", [128, 128], F32)
            recip = sb2("recip", [1, TPC], F32)
            bcast = sb2("bcast", [128, TPC], F32)

            s_ps = [p2.enter_context(nc.psum_tensor(f"s_ps{i}", [128, TPC], F32))
                    for i in range(2)]
            pv_ps = [p2.enter_context(nc.psum_tensor(f"pv_ps{i}", [128, TPC], F32))
                     for i in range(2)]

            shiftc = sb2("shiftc", [128, 1], F32)
            t_shift = P.op("pl", lambda e: e.memset(shiftc[:, :], EXP_SHIFT))
            t_ocf = P.op("pl", lambda e: e.memset(onesf[:, :], 1.0))

            g_kh = [P.dgroup(f"kh{i}") for i in range(2)]
            g_vp = [P.dgroup(f"vp{i}") for i in range(2)]

            kh_use = [None, None]
            vp_use = [None, None]

            def load_kh(h):
                slot = h % 2
                deps = [cc_tok]
                if kh_use[slot]:
                    deps.append(kh_use[slot])
                for r in range(NCORES):
                    P.dma("sp", g_kh[slot], kh[slot][:, r * TPC:(r + 1) * TPC],
                          cc_out[r * 4096 + h * 128: r * 4096 + (h + 1) * 128, :],
                          deps=deps if r == 0 else [])

            def load_vp(pr):
                slot = pr % 2
                deps = [cc_tok]
                if vp_use[slot]:
                    deps.append(vp_use[slot])
                for r in range(NCORES):
                    src = bass.AP(cc_out.tensor,
                                  (r * 4096 + 2048) * TPC + pr * 256,
                                  [[2048, 128], [128 * 2048, 8], [1, 256]])
                    dst = vp[slot][:, r * 8 * 256:(r * 8 + 8) * 256] \
                        .rearrange("p (a d) -> p a d", d=256)
                    P.dma("act", g_vp[slot], dst, src, deps=deps if r == 0 else [])

            load_kh(0)
            load_kh(1)
            load_vp(0)

            prev_exp62 = None
            prev_rec = None
            prev_sum_a = None
            prev_sum_b = None
            prev_evac = [None, None]
            last_bcast_reader = None
            pt_readers = {0: [], 1: [], 2: [], 3: []}
            pend = None

            def emit_epi1(pd):
                t_sum_b = None
                for half in range(2):
                    t_sum_b = P.op("pe", lambda e, half=half: e.matmul(
                        s_ps[1][:, half * 512:(half + 1) * 512], onesf[:, :],
                        acc_b[:, half * 512:(half + 1) * 512],
                        start=False, stop=True), deps=[pd['acc63']])
                dr = [t_sum_b]
                if pd['last_bcast_reader']:
                    dr.append(pd['last_bcast_reader'])
                t_rec = P.op("dve", lambda e: e.reciprocal(
                    bcast[:, :], s_ps[1][:, :]), deps=dr)
                pd['t_rec'] = t_rec
                return t_sum_b

            def emit_epi2(pd):
                return P.op("dve", lambda e, pvs=pd['pvs'], h=pd['h']: e.tensor_tensor(
                    ao_sb[:, h * TPC:(h + 1) * TPC], pv_ps[pvs][:, :], bcast[:, :],
                    ALU.mult), deps=[pd['t_rec'], pd['pv63']])

            for h in range(H):
                hs = h % 2
                pvs = h % 2
                vslot = (h // 2) % 2
                khld = g_kh[hs].tok()
                vpld = g_vp[vslot].tok()
                if h % 2 == 0 and h // 2 + 1 < H // 2:
                    load_vp(h // 2 + 1)

                s_tok = {}
                exp_tok = {}
                acc_tok = {}
                pv_tok = {}
                j = h % 2
                for step in range(65):
                    if step < 64:
                        kt = step
                        sl = kt % 2
                        ps = kt % 4
                        d0 = [khld]
                        if kt >= 2:
                            d0.append(exp_tok[kt - 2])
                        elif kt == 0 and prev_exp62:
                            d0.append(prev_exp62)
                        elif kt == 1 and prev_rec:
                            d0.append(prev_rec)
                        for qh in range(2):
                            s_tok[kt] = P.op("pe", lambda e, hs=hs, kt=kt, h=h, sl=sl, qh=qh: e.matmul(
                                s_ps[sl][:, qh * 512:(qh + 1) * 512],
                                kh[hs][:, kt * 128:(kt + 1) * 128],
                                q_sb[:, h * TPC + qh * 512: h * TPC + (qh + 1) * 512],
                                start=True, stop=True), deps=d0 if qh == 0 else [])
                        if step == 0 and pend is not None:
                            prev_sum_b = emit_epi1(pend)
                            prev_rec = pend['t_rec']
                        d1 = [s_tok[kt], t_shift] + pt_readers[ps]
                        exp_tok[kt] = P.op("act", lambda e, ps=ps, sl=sl: e.activation(
                            pt[ps][:, :], s_ps[sl][:, :],
                            AF.Exp, bias=shiftc[:, 0:1], scale=SCALE), deps=d1)
                        a_buf = acc_a if kt % 2 == 0 else acc_b
                        if kt < 2:
                            da = [exp_tok[kt]]
                            if kt == 0 and prev_sum_a:
                                da.append(prev_sum_a)
                            if kt == 1 and prev_sum_b:
                                da.append(prev_sum_b)
                            acc_tok[kt] = P.op("dve", lambda e, ps=ps, a=a_buf: e.tensor_copy(
                                a[:, :], pt[ps][:, :]), deps=da)
                        else:
                            acc_tok[kt] = P.op("dve", lambda e, ps=ps, a=a_buf: e.tensor_tensor(
                                a[:, :], a[:, :], pt[ps][:, :], ALU.add),
                                deps=[exp_tok[kt], acc_tok[kt - 2]])
                    if step >= 1:
                        kt = step - 1
                        ps = kt % 4
                        d2 = [exp_tok[kt], vpld]
                        if kt == 0 and prev_evac[pvs]:
                            d2.append(prev_evac[pvs])
                        for qh in range(2):
                            pv_tok[kt] = P.op("pe", lambda e, pvs=pvs, vslot=vslot, j=j, kt=kt, qh=qh, ps=ps: e.matmul(
                                pv_ps[pvs][:, qh * 512:(qh + 1) * 512],
                                vp[vslot][:, kt * 256 + j * 128: kt * 256 + (j + 1) * 128],
                                pt[ps][:, qh * 512:(qh + 1) * 512],
                                start=(kt == 0), stop=(kt == 63)), deps=d2 if qh == 0 else [])
                        pt_readers[ps] = [pv_tok[kt], acc_tok[kt]]
                    if step == 4 and pend is not None:
                        t_evac = emit_epi2(pend)
                        prev_evac[pend['pvs']] = t_evac
                        last_bcast_reader = t_evac
                        pend = None
                kh_use[hs] = s_tok[63]
                vp_use[vslot] = pv_tok[63]
                if h + 2 < H:
                    load_kh(h + 2)

                t_sa = None
                for half in range(2):
                    t_sa = P.op("pe", lambda e, half=half: e.matmul(
                        s_ps[1][:, half * 512:(half + 1) * 512], onesf[:, :],
                        acc_a[:, half * 512:(half + 1) * 512],
                        start=True, stop=False),
                        deps=[acc_tok[62], exp_tok[63], t_ocf])
                prev_sum_a = t_sa
                prev_exp62 = exp_tok[62]
                pend = {'h': h, 'pvs': pvs, 'acc63': acc_tok[63],
                        'pv63': pv_tok[63],
                        'last_bcast_reader': last_bcast_reader}

            prev_sum_b = emit_epi1(pend)
            prev_rec = pend['t_rec']
            t_evac = emit_epi2(pend)
            prev_evac[pend['pvs']] = t_evac
            last_bcast_reader = t_evac
            pend = None

            if "d_ao" in dbg_aps:
                P.dma("pl", g_cpl, dbg_aps["d_ao"][:, :], ao_sb[:, :],
                      deps=[prev_evac[0], prev_evac[1]])
            if "d_q" in dbg_aps:
                P.dma("pl", g_cpl, dbg_aps["d_q"][:, :], q_sb[:, :],
                      deps=[prev_evac[0], prev_evac[1]])

        P.barrier()
        s_q.close()

        ln2t = sb("ln2t", [128, NDC * TPC], BF16)
        g_x1s = [P.dgroup(f"x1sp{i}") for i in range(2)]

        with ExitStack() as p3:
            sb3 = lambda n, s, d: p3.enter_context(nc.sbuf_tensor(n, s, d))
            wotb = [sb3(f"wotb{i}", [128, D], BF16) for i in range(2)]
            xt3 = [sb3(f"xt3_{i}", [128, D], F32) for i in range(2)]
            x1t = [sb3(f"x1t_{i}", [128, D], F32) for i in range(2)]
            ln2tm = sb3("ln2tm", [128, NTT * D], BF16)
            scr3 = (sb3("xsq3", [128, D], F32), sb3("s1t3", [128, 1], F32),
                    sb3("s2t3", [128, 1], F32), sb3("mt3", [128, 1], F32),
                    sb3("msqt3", [128, 1], F32), sb3("biast3", [128, 1], F32),
                    sb3("stdt3", [128, 1], F32), sb3("rstdt3", [128, 1], F32))

            bo_row = sb3("bo_sb", [1, D], BF16)
            bt3 = sb3("bt3_sb", [128, NDC * 4 * RK], BF16)
            at3 = sb3("at3_sb", [RK, 4 * D], BF16)
            t_bo = P.dma("pl", P.dgroup("c_bo"), bo_row[:, :], bo_h[:, :])
            t_bt = P.dma("pl", P.dgroup("c_bt3"),
                         bt3[:, :].rearrange("p (c r) -> p c r", r=4 * RK),
                         bt_h.rearrange("(c p) r -> p c r", p=128))
            t_at = P.dma("pl", P.dgroup("c_at3"), at3[:, :], at_h[:, :])

            g_wot = [P.dgroup(f"wot{i}") for i in range(2)]
            g_x3 = [P.dgroup(f"x3ld{i}") for i in range(2)]

            with ExitStack() as p3a:
                o_ps = [p3a.enter_context(nc.psum_tensor(f"o_ps{i}", [128, 512], F32))
                        for i in range(8)]

                uo_tok = None
                for qh in range(2):
                    mm = []
                    for dc in range(NDC):
                        d0 = [t_bt]
                        if dc == 0 and uo_tok:
                            d0.append(uo_tok)
                        mm.append(P.op("pe", lambda e, dc=dc, qh=qh: e.matmul(
                            o_ps[0][0:RK, :], bt3[:, dc * 64 + 48: dc * 64 + 64],
                            ao_sb[:, dc * TPC + qh * 512: dc * TPC + qh * 512 + 512],
                            start=(dc == 0), stop=(dc == NDC - 1)), deps=d0))
                    uo_tok = P.op("act", lambda e, qh=qh: e.mul(
                        ut[:, 3 * TPC + qh * 512:3 * TPC + qh * 512 + 512],
                        o_ps[0][0:RK, :], 0.5), deps=mm)
                uto_ready = uo_tok

                wot_use = [None, None]
                x1_tok = [None, None]
                xt3_tok = [None, None]
                ln2_tiles = []
                evac_tok = {}
                for tp in range(4):
                    seed = {}
                    for i in range(2):
                        for oc in range(4):
                            g = 4 * i + oc
                            d0 = [t_or, t_bo, uto_ready]
                            pv = evac_tok.get((tp - 1, g))
                            if pv:
                                d0.append(pv)
                            seed[g] = P.op("pe", lambda e, g=g, oc=oc: e.matmul(
                                o_ps[g][:, :], onesrow[:, :],
                                bo_row[0:1, oc * 512:(oc + 1) * 512],
                                start=True, stop=False), deps=d0)
                    last_mm = {g: [seed[g]] for g in range(8)}
                    for dc in range(NDC):
                        slot = dc % 2
                        wdeps = [wot_use[slot]] if wot_use[slot] else []
                        P.dma("pl", g_wot[slot], wotb[slot][:, :],
                              wot[dc * 128:(dc + 1) * 128, :], deps=wdeps)
                        wtok = g_wot[slot].tok()
                        lw = None
                        for i in range(2):
                            tt = 2 * tp + i
                            for oc in range(4):
                                g = 4 * i + oc
                                lw = P.op("pe", lambda e, g=g, slot=slot, dc=dc, tt=tt, oc=oc: e.matmul(
                                    o_ps[g][:, :],
                                    ao_sb[:, dc * TPC + tt * 128: dc * TPC + (tt + 1) * 128],
                                    wotb[slot][:, oc * 512:(oc + 1) * 512],
                                    start=False, stop=False), deps=[wtok])
                                last_mm[g].append(lw)
                        wot_use[slot] = lw
                    for i in range(2):
                        tt = 2 * tp + i
                        sl = tt % 2
                        xdeps = [xt3_tok[sl]] if xt3_tok[sl] else []
                        t_x = P.dma("sp", g_x3[sl], xt3[sl][:, :],
                                    xr[tt * 128:(tt + 1) * 128, :], deps=xdeps)
                        tile_done = []
                        for oc in range(4):
                            g = 4 * i + oc
                            t_lo = P.op("pe", lambda e, g=g, tt=tt, oc=oc: e.matmul(
                                o_ps[g][:, :],
                                ut[:, 3 * TPC + tt * 128:3 * TPC + (tt + 1) * 128],
                                at3[:, 3 * D + oc * 512:3 * D + (oc + 1) * 512],
                                start=False, stop=True), deps=[uto_ready, t_at])
                            dE = [t_lo, t_x] + last_mm[g]
                            if x1_tok[sl]:
                                dE.append(x1_tok[sl])
                            tkv = P.op("dve", lambda e, g=g, sl=sl, oc=oc: e.tensor_tensor(
                                x1t[sl][:, oc * 512:(oc + 1) * 512], o_ps[g][:, :],
                                xt3[sl][:, oc * 512:(oc + 1) * 512], ALU.add), deps=dE)
                            evac_tok[(tp, g)] = tkv
                            tile_done.append(tkv)
                        xt3_tok[sl] = tile_done[-1]
                        t_ln2 = _ln_chain(P, x1t[sl][:, :],
                                          ln2tm[:, tt * D:(tt + 1) * D], scr3,
                                          tile_done)
                        x1_tok[sl] = P.dma("sp", g_x1s[sl],
                                           x1_sp[tt * 128:(tt + 1) * 128, :],
                                           x1t[sl][:, :], deps=[t_ln2])
                        ln2_tiles.append((tt, t_ln2))
                last_o_evac = evac_tok[(3, 7)]

            P.barrier()
            with nc.psum_tensor("tps3", [128, 512], F32) as tps3:
                tp3_tok = None
                for tt, t_ln2 in ln2_tiles:
                    for g in range(4):
                        mm = []
                        for j in range(4):
                            dc = 4 * g + j
                            d0 = [t_ln2, t_eye]
                            if tp3_tok:
                                d0.append(tp3_tok)
                            mm.append(P.op("pe", lambda e, tt=tt, dc=dc, j=j: e.matmul(
                                tps3[:, j * 128:(j + 1) * 128],
                                ln2tm[:, tt * D + dc * 128: tt * D + (dc + 1) * 128],
                                eye[:, :], start=True, stop=True), deps=d0))
                        tp3_tok = P.op("act", lambda e, tt=tt, g=g: e.copy(
                            ln2t[:, :].rearrange("p (c n) -> p c n", n=TPC)
                            [:, 4 * g:4 * g + 4, tt * 128:(tt + 1) * 128],
                            tps3[:, :].rearrange("p (c n) -> p c n", n=128)), deps=mm)
                ln2t_ready = tp3_tok

            if "d_ln2t" in dbg_aps:
                P.dma("pl", g_cpl, dbg_aps["d_ln2t"][:, :], ln2t[:, :],
                      deps=[ln2t_ready])
            if "d_x1" in dbg_aps:
                pass

        P.barrier()
        s_ao.close()

        for _e in ENGINES:
            P.wait(_e, [g_x1s[0].tok(), g_x1s[1].tok()])
        with ExitStack() as p4:
            sb4 = lambda n, s, d: p4.enter_context(nc.sbuf_tensor(n, s, d))
            fuse = sb4("fuse", [128, 64 * 512], BF16)
            wgu = [sb4(f"wgu{i}", [128, NDC * 256], BF16) for i in range(2)]
            wub = [sb4(f"wub{i}", [128, NDC * 256], BF16) for i in range(2)]
            wdb = [sb4(f"wdb{i}", [128, 8 * 512], BF16) for i in range(2)]
            gtmp = sb4("gtmp", [128, 512], F32)
            utmp = sb4("utmp", [128, 512], F32)
            dtmp = sb4("dtmp", [128, 512], F32)
            x1rd = [sb4(f"x1rd{i}", [128, 512], F32) for i in range(2)]
            x2t = [sb4(f"x2t{i}", [128, 512], F32) for i in range(2)]
            sgu = sb4("sgu_sb", [128, 128], F32)
            sdb = sb4("sdb_sb", [128, D], F32)
            t_sgu = P.dma("sp", P.dgroup("c_sgu"), sgu[:, :], sgu_h[:, :])
            with nc.sbuf_tensor("sdrow_sb", [1, D], F32) as sdrow, \
                 nc.psum_tensor("sdb_ps", [128, D], F32) as sdb_ps:
                t_sdrow = P.dma("sp", P.dgroup("c_sdrow"), sdrow[:, :], sd_h[:, :])
                toks = []
                for j in range(4):
                    toks.append(P.op("pe", lambda e, j=j: e.matmul(
                        sdb_ps[:, j * 512:(j + 1) * 512], onesrow_f[:, :],
                        sdrow[0:1, j * 512:(j + 1) * 512], start=True, stop=True),
                        deps=[t_orf, t_sdrow]))
                t_sdb = P.op("dve", lambda e: e.tensor_copy(sdb[:, :], sdb_ps[:, :]),
                             deps=toks)

            g_wg = [P.dgroup(f"wg{i}") for i in range(2)]
            g_wd = [P.dgroup(f"wd{i}") for i in range(2)]
            g_x1r = [P.dgroup(f"x1rd{i}") for i in range(2)]
            g_out = [P.dgroup(f"outst{i}") for i in range(2)]

            wgu_use = [None, None]
            wd_use = [None, None]
            x2_tok = [None, None]
            x1rd_reader = [None, None]
            prev_ftok = None
            prev_x2 = None

            for th in range(2):
                with ExitStack() as p4a:
                    gu_ps = [p4a.enter_context(
                        nc.psum_tensor(f"gu_ps{th}_{i}", [128, 512], F32))
                        for i in range(4)]
                    gu_tok = [None] * 4
                    fuse_tok = {}
                    for grp in range(32):
                        slot = grp % 2
                        wdeps = [wgu_use[slot]] if wgu_use[slot] else []
                        P.dma("pl", g_wg[slot],
                              wgu[slot][:, :].rearrange("p (c o) -> p c o", o=256),
                              wg_h[:, grp * 256:(grp + 1) * 256]
                              .rearrange("(c p) o -> p c o", p=128), deps=wdeps)
                        P.dma("pl", g_wg[slot],
                              wub[slot][:, :].rearrange("p (c o) -> p c o", o=256),
                              wu_h[:, grp * 256:(grp + 1) * 256]
                              .rearrange("(c p) o -> p c o", p=128), deps=[])
                        wtok = g_wg[slot].tok()
                        lastg = None
                        for ii in range(2):
                            it = grp * 2 + ii
                            pg = it % 2
                            pu = 2 + it % 2
                            mmg, mmu = [], []
                            for hc in range(NDC):
                                d0 = []
                                if hc == 0:
                                    d0 = [wtok, ln2t_ready, t_sdb]
                                    if gu_tok[pg]:
                                        d0.append(gu_tok[pg])
                                d0u = []
                                if hc == 0:
                                    d0u = [wtok]
                                    if gu_tok[pu]:
                                        d0u.append(gu_tok[pu])
                                mmg.append(P.op("pe", lambda e, pg=pg, slot=slot, hc=hc, ii=ii, th=th: e.matmul(
                                    gu_ps[pg][:, :],
                                    wgu[slot][:, hc * 256 + ii * 128: hc * 256 + (ii + 1) * 128],
                                    ln2t[:, hc * TPC + th * 512: hc * TPC + th * 512 + 512],
                                    start=(hc == 0), stop=(hc == NDC - 1)), deps=d0))
                                mmu.append(P.op("pe", lambda e, pu=pu, slot=slot, hc=hc, ii=ii, th=th: e.matmul(
                                    gu_ps[pu][:, :],
                                    wub[slot][:, hc * 256 + ii * 128: hc * 256 + (ii + 1) * 128],
                                    ln2t[:, hc * TPC + th * 512: hc * TPC + th * 512 + 512],
                                    start=(hc == 0), stop=(hc == NDC - 1)), deps=d0u))
                            lastg = mmu[-1]
                            gdeps = mmg + [t_sgu]
                            udeps = mmu + [t_sgu]
                            if prev_ftok:
                                gdeps.append(prev_ftok)
                                udeps.append(prev_ftok)
                            t_g = P.op("act", lambda e, pg=pg, it=it: e.activation(
                                gtmp[:, :], gu_ps[pg][:, :], AF.Silu,
                                scale=sgu[:, it:it + 1]), deps=gdeps)
                            t_u = P.op("dve", lambda e, pu=pu, it=it: e.tensor_scalar(
                                utmp[:, :], gu_ps[pu][:, :], sgu[:, 64 + it:65 + it],
                                None, ALU.mult), deps=udeps)
                            ftok = P.op("dve", lambda e, it=it: e.tensor_tensor(
                                fuse[:, it * 512:(it + 1) * 512], gtmp[:, :],
                                utmp[:, :], ALU.mult), deps=[t_g, t_u])
                            gu_tok[pg] = ftok
                            gu_tok[pu] = ftok
                            fuse_tok[it] = ftok
                            prev_ftok = ftok
                        wgu_use[slot] = lastg
                    fuse_all = [fuse_tok[63]]

                P.barrier()

                with ExitStack() as p4b:
                    dn_ps = [p4b.enter_context(
                        nc.psum_tensor(f"dn_ps{th}_{i}", [128, 512], F32))
                        for i in range(8)]
                    dn_evac = {}
                    for oc in range(4):
                        base = 4 * (oc % 2)
                        lastd = None
                        for icg in range(8):
                            slot = icg % 2
                            wdeps = [wd_use[slot]] if wd_use[slot] else []
                            P.dma("pl", g_wd[slot],
                                  wdb[slot][:, :].rearrange("p (c o) -> p c o", o=512),
                                  wd_h[icg * 1024:(icg + 1) * 1024,
                                       oc * 512:(oc + 1) * 512]
                                  .rearrange("(c p) o -> p c o", p=128), deps=wdeps)
                            wtok = g_wd[slot].tok()
                            for ici in range(8):
                                ic = icg * 8 + ici
                                for ts in range(4):
                                    d0 = [wtok]
                                    if ic == 0 and ts == 0:
                                        d0 += fuse_all
                                        for tsx in range(4):
                                            if (oc - 2, tsx) in dn_evac:
                                                d0.append(dn_evac[(oc - 2, tsx)])
                                    lastd = P.op("pe", lambda e, base=base, ts=ts, slot=slot, ici=ici, ic=ic: e.matmul(
                                        dn_ps[base + ts][:, :],
                                        fuse[:, ic * 512 + ts * 128: ic * 512 + (ts + 1) * 128],
                                        wdb[slot][:, ici * 512:(ici + 1) * 512],
                                        start=(ic == 0), stop=(ic == 63)), deps=d0)
                            wd_use[slot] = lastd
                        for ts in range(4):
                            xsl = ts % 2
                            xd = [g_x1s[0].tok(), g_x1s[1].tok()]
                            if x1rd_reader[xsl]:
                                xd.append(x1rd_reader[xsl])
                            t_x1r = P.dma("sp", g_x1r[xsl],
                                          x1rd[xsl][:, :],
                                          bass.AP(x1_sp.tensor,
                                                  (th * 512 + ts * 128) * D + oc * 512,
                                                  [[D, 128], [1, 512]]), deps=xd)
                            d0 = [lastd, t_sdb]
                            if (oc - 1, ts) in dn_evac:
                                d0.append(dn_evac[(oc - 1, ts)])
                            if prev_x2:
                                d0.append(prev_x2)
                            t_sc = P.op("dve", lambda e, base=base, ts=ts, oc=oc: e.tensor_tensor(
                                dtmp[:, :], dn_ps[base + ts][:, :],
                                sdb[:, oc * 512:(oc + 1) * 512], ALU.mult), deps=d0)
                            d1 = [t_sc, t_x1r]
                            if x2_tok[xsl]:
                                d1.append(x2_tok[xsl])
                            t_x2 = P.op("dve", lambda e, ts=ts, xsl=xsl, oc=oc: e.tensor_tensor(
                                x2t[xsl][:, :], dtmp[:, :],
                                x1rd[xsl][:, :], ALU.add), deps=d1)
                            prev_x2 = t_x2
                            dn_evac[(oc, ts)] = t_x2
                            x1rd_reader[xsl] = t_x2
                            x2_tok[xsl] = P.dma("sp", g_out[xsl],
                                                bass.AP(out_h.tensor,
                                                        (th * 512 + ts * 128) * D + oc * 512,
                                                        [[D, 128], [1, 512]]),
                                                x2t[xsl][:, :], deps=[t_x2])
                if th == 0:
                    P.barrier()

            P.wait("sp", [g_out[0].tok(), g_out[1].tok()])

        P.finalize()
    return nc


def _host_layouts(inp):
    f = np.ascontiguousarray
    a = {k: np.asarray(v) for k, v in inp.items()}
    shared = {
        "wqt": f(a["L0_attn_Wq"].T),
        "wkt": f(a["L0_attn_Wk"].T),
        "wvt": f(a["L0_attn_Wv"].T),
        "wot": f(a["L0_attn_Wo"].T),
        "bqkv": f(np.concatenate([a["L0_attn_bq"].reshape(16, 128).T,
                                  a["L0_attn_bk"].reshape(16, 128).T,
                                  a["L0_attn_bv"].reshape(16, 128).T], axis=1)),
        "bo_row": f(a["L0_attn_bo"].reshape(1, D)),
        "bv_row": f(a["L0_attn_bv"].reshape(1, D)),
        "btcat": f(np.concatenate([a["L0_attn_Bq"].T, a["L0_attn_Bk"].T,
                                   a["L0_attn_Bv"].T, a["L0_attn_Bo"].T], axis=1)),
        "atcat": f(np.concatenate([a["L0_attn_Aq"].T, a["L0_attn_Ak"].T,
                                   a["L0_attn_Av"].T, a["L0_attn_Ao"].T], axis=1)),
        "sgu": f(np.concatenate([a["L0_ffn_s_gate"].reshape(64, 128).T,
                                 a["L0_ffn_s_up"].reshape(64, 128).T], axis=1)),
        "sdrow": f(a["L0_ffn_s_down"].reshape(1, D)),
        "wg": a["L0_ffn_w_gate"],
        "wu": a["L0_ffn_w_up"],
        "wd": a["L0_ffn_w_down"],
    }
    x = a["L0_attn_x"].reshape(SEQ, D)
    in_maps = []
    for r in range(NCORES):
        m = dict(shared)
        m["x"] = f(x[r * TPC:(r + 1) * TPC, :])
        in_maps.append(m)
    return in_maps


def kernel(**inputs):
    in_maps = _host_layouts(inputs)
    nc = build_nc()
    res = run_bass_kernel_spmd(nc, in_maps, core_ids=list(range(NCORES)))
    out = np.concatenate([np.asarray(r["out"]) for r in res.results], axis=0)
    return out.reshape(1, SEQ, D).astype(np.float32, copy=False)


# version 10
# speedup vs baseline: 1.0860x; 1.0224x over previous
import numpy as np
from contextlib import ExitStack

import concourse.bass as bass
import concourse.mybir as mybir
from concourse.bass_utils import run_bass_kernel_spmd
from concourse.replica_groups import maybe_share_collective_output_space

F32 = mybir.dt.float32
BF16 = mybir.dt.bfloat16
I8 = mybir.dt.int8
AF = mybir.ActivationFunctionType
ALU = mybir.AluOpType
AX = mybir.AxisListType

NCORES = 8
SEQ = 8192
TPC = SEQ // NCORES
D = 2048
H = 16
HD = 128
RK = 16
INTER = 8192
NDC = D // 128
NTT = TPC // 128
SCALE = float(HD) ** -0.5
EXP_SHIFT = -3.0
EPS = 1e-5

ENGINES = ("sp", "pe", "act", "dve", "pl")
_SECTION = {"sp": "sync", "pe": "tensor", "act": "scalar", "dve": "vector", "pl": "gpsimd"}


class DGroup:
    def __init__(self, sem):
        self.sem = sem
        self.n = 0

    def tok(self):
        return (self.sem, self.n * 16)


class Prog:

    def __init__(self, nc, ctx):
        self.nc = nc
        self.ctx = ctx
        self.blocks = []
        self.cur = {e: [] for e in ENGINES}
        self.esem = {e: ctx.enter_context(nc.semaphore(f"es_{e}")) for e in ENGINES}
        self.ecount = {e: 0 for e in ENGINES}
        self.waited = {e: {} for e in ENGINES}

    def dgroup(self, name):
        return DGroup(self.ctx.enter_context(self.nc.semaphore(f"dg_{name}")))

    def _w(self, eng, deps):
        out = []
        w = self.waited[eng]
        for d in deps:
            if d is None:
                continue
            sem, val = d
            if w.get(sem, 0) >= val:
                continue
            w[sem] = val
            out.append((sem, val))
        return out

    def op(self, eng, fn, deps=(), auto=True, inc_extra=None):
        waits = self._w(eng, deps)
        tok = None
        if auto:
            self.ecount[eng] += 1
            tok = (self.esem[eng], self.ecount[eng])

        def emit(e, fn=fn, waits=waits, tok=tok, inc_extra=inc_extra):
            for s, v in waits:
                e.wait_ge(s, v)
            ins = fn(e)
            if tok is not None:
                ins = ins.then_inc(tok[0], 1)
            if inc_extra is not None:
                ins.then_inc(*inc_extra)

        self.cur[eng].append(emit)
        return tok

    def dma(self, eng, group, out, in_, deps=()):
        waits = self._w(eng, deps)
        group.n += 1
        tok = (group.sem, group.n * 16)

        def emit(e, out=out, in_=in_, waits=waits, gsem=group.sem):
            for s, v in waits:
                e.wait_ge(s, v)
            e.dma_start(out=out, in_=in_).then_inc(gsem, 16)

        self.cur[eng].append(emit)
        return tok

    def wait(self, eng, deps):
        waits = self._w(eng, deps)
        if not waits:
            return

        def emit(e, waits=waits):
            for s, v in waits:
                e.wait_ge(s, v)

        self.cur[eng].append(emit)

    def barrier(self):
        self.blocks.append(self.cur)
        self.cur = {e: [] for e in ENGINES}
        for e2 in ENGINES:
            for e1 in ENGINES:
                s = self.esem[e1]
                self.waited[e2][s] = max(self.waited[e2].get(s, 0), self.ecount[e1])

    def finalize(self):
        if any(self.cur[e] for e in ENGINES):
            self.barrier()
        nc = self.nc
        for blk_ops in self.blocks:
            with nc.Block() as blk:
                for eng in ENGINES:
                    ops = blk_ops[eng]
                    if not ops:
                        continue

                    def section(e, ops=ops):
                        for emit in ops:
                            emit(e)

                    getattr(blk, _SECTION[eng])(section)


def _ln_chain(P, x_ap, out_ap, scr, deps):
    xsq, s1t, s2t, mt, msqt, biast, stdt, rstdt = scr
    t_s1 = P.op("dve", lambda e: e.reduce_sum(s1t[:, :], x_ap, axis=AX.X), deps=deps)
    t_sq = P.op("act", lambda e: e.activation(xsq[:, :], x_ap, AF.Square),
                deps=deps + [t_s1])
    t_s2 = P.op("dve", lambda e: e.reduce_sum(s2t[:, :], xsq[:, :], axis=AX.X),
                deps=[t_sq])
    t_m = P.op("dve", lambda e: e.tensor_scalar(mt[:, :], s1t[:, :], 1.0 / D, None,
                                                ALU.mult), deps=[t_s1])
    t_msq = P.op("dve", lambda e: e.tensor_tensor(msqt[:, :], mt[:, :], mt[:, :],
                                                  ALU.mult), deps=[t_m])
    t_bias = P.op("dve", lambda e: e.tensor_scalar(biast[:, :], msqt[:, :], EPS, -1.0,
                                                   ALU.subtract, ALU.mult), deps=[t_msq])
    t_std = P.op("act", lambda e: e.activation(stdt[:, :], s2t[:, :], AF.Sqrt,
                                               bias=biast[:, 0:1], scale=1.0 / D),
                 deps=[t_s2, t_bias])
    t_rstd = P.op("dve", lambda e: e.reciprocal(rstdt[:, :], stdt[:, :]), deps=[t_std])
    t_ln = P.op("dve", lambda e: e.tensor_scalar(out_ap, x_ap, mt[:, 0:1],
                                                 rstdt[:, 0:1], ALU.subtract, ALU.mult),
                deps=[t_rstd, t_m])
    return t_ln


def build_nc(dbg=()):
    nc = bass.Bass("TRN2", target_bir_lowering=False)

    xr = nc.dram_tensor("x", [TPC, D], F32, kind="ExternalInput").ap()
    wqt = nc.dram_tensor("wqt", [D, D], F32, kind="ExternalInput").ap()
    wkt = nc.dram_tensor("wkt", [D, D], F32, kind="ExternalInput").ap()
    wvt = nc.dram_tensor("wvt", [D, D], F32, kind="ExternalInput").ap()
    wot = nc.dram_tensor("wot", [D, D], F32, kind="ExternalInput").ap()
    bqkv_h = nc.dram_tensor("bqkv", [128, 48], F32, kind="ExternalInput").ap()
    bo_h = nc.dram_tensor("bo_row", [1, D], F32, kind="ExternalInput").ap()
    bv_h = nc.dram_tensor("bv_row", [1, D], F32, kind="ExternalInput").ap()
    bt_h = nc.dram_tensor("btcat", [D, 4 * RK], F32, kind="ExternalInput").ap()
    at_h = nc.dram_tensor("atcat", [RK, 4 * D], F32, kind="ExternalInput").ap()
    sgu_h = nc.dram_tensor("sgu", [128, 128], F32, kind="ExternalInput").ap()
    sd_h = nc.dram_tensor("sdrow", [1, D], F32, kind="ExternalInput").ap()
    wg_h = nc.dram_tensor("wg", [D, INTER], I8, kind="ExternalInput").ap()
    wu_h = nc.dram_tensor("wu", [D, INTER], I8, kind="ExternalInput").ap()
    wd_h = nc.dram_tensor("wd", [INTER, D], I8, kind="ExternalInput").ap()
    out_h = nc.dram_tensor("out", [TPC, D], F32, kind="ExternalOutput").ap()

    cc_in = nc.dram_tensor("cc_in", [4096, TPC], BF16, kind="Internal").ap()
    aspace = maybe_share_collective_output_space("AllGather", [list(range(NCORES))])
    cc_out = nc.dram_tensor("cc_out", [NCORES * 4096, TPC], BF16, kind="Internal",
                            addr_space=aspace).ap()
    x1_sp = nc.dram_tensor("x1_spill", [TPC, D], F32, kind="Internal").ap()

    dbg_aps = {}
    for name, shape in (("d_lnt", [128, NDC * TPC]), ("d_q", [128, NDC * TPC]),
                        ("d_ao", [128, NDC * TPC]), ("d_x1", [TPC, D]),
                        ("d_ln2t", [128, NDC * TPC])):
        if name in dbg:
            dbg_aps[name] = nc.dram_tensor(name, shape, F32, kind="ExternalOutput").ap()

    with ExitStack() as ctx:
        P = Prog(nc, ctx)
        sb = lambda n, s, d: ctx.enter_context(nc.sbuf_tensor(n, s, d))

        eye = sb("eye", [128, 128], BF16)
        onescol = sb("onescol", [128, 1], BF16)
        onesrow = sb("onesrow", [1, 128], BF16)
        onesrow_f = sb("onesrow_f", [1, 128], F32)
        ut = sb("ut_sb", [RK, 4 * TPC], BF16)

        g_cpl = P.dgroup("c_pl")
        g_csp = P.dgroup("c_sp")

        with nc.sbuf_tensor("eyef", [128, 128], F32) as eyef:
            t_iota = P.op("pl", lambda e: e.iota(eyef[:, :], [[1, 128]],
                                                 channel_multiplier=-1,
                                                 allow_small_or_imprecise_dtypes=True))
            t_eye = P.op("dve", lambda e: e.tensor_scalar(eye[:, :], eyef[:, :], 0.0,
                                                          None, ALU.is_equal),
                         deps=[t_iota])
        t_oc = P.op("pl", lambda e: e.memset(onescol[:, :], 1.0))
        t_or = P.op("pl", lambda e: e.memset(onesrow[:, :], 1.0))
        t_orf = P.op("pl", lambda e: e.memset(onesrow_f[:, :], 1.0))

        s_q = ExitStack()
        q_sb = s_q.enter_context(nc.sbuf_tensor("q_sb", [128, NDC * TPC], BF16))
        cc_sem = ctx.enter_context(nc.semaphore("cc_sem"))

        with ExitStack() as p1:
            sb1 = lambda n, s, d: p1.enter_context(nc.sbuf_tensor(n, s, d))
            ln1t = sb1("ln1t", [128, NDC * TPC], BF16)
            xt = [sb1(f"xt{i}", [128, D], F32) for i in range(2)]
            lnt = [sb1(f"lnt{i}", [128, D], BF16) for i in range(2)]
            scr = (sb1("xsq", [128, D], F32), sb1("s1t", [128, 1], F32),
                   sb1("s2t", [128, 1], F32), sb1("mt", [128, 1], F32),
                   sb1("msqt", [128, 1], F32), sb1("biast", [128, 1], F32),
                   sb1("stdt", [128, 1], F32), sb1("rstdt", [128, 1], F32))
            wslab = [sb1(f"wslab{i}", [128, NDC * 512], BF16) for i in range(2)]
            wslab_q = [sb1(f"wslabq{i}", [128, NDC * 512], BF16) for i in range(2)]
            ktile = [sb1(f"ktile{i}", [128, TPC], BF16) for i in range(2)]
            vtile = [sb1(f"vtile{i}", [128, 512], BF16) for i in range(2)]
            bqkv = sb1("bqkv_sb", [128, 48], F32)
            bv_row = sb1("bv_sb", [1, D], BF16)
            bt = sb1("bt_sb", [128, NDC * 4 * RK], BF16)
            at = sb1("at_sb", [RK, 4 * D], BF16)
            t_bqkv = P.dma("sp", P.dgroup("c_bqkv"), bqkv[:, :], bqkv_h[:, :])
            t_bv = P.dma("pl", P.dgroup("c_bv"), bv_row[:, :], bv_h[:, :])
            t_bt = P.dma("pl", P.dgroup("c_bt1"),
                         bt[:, :].rearrange("p (c r) -> p c r", r=4 * RK),
                         bt_h.rearrange("(c p) r -> p c r", p=128))
            t_at = P.dma("pl", P.dgroup("c_at1"), at[:, :], at_h[:, :])

            tps = p1.enter_context(nc.psum_tensor("tps", [128, 512], F32))
            u_ps = p1.enter_context(nc.psum_tensor("u_ps", [RK, 512], F32))
            qk_ps = [p1.enter_context(nc.psum_tensor(f"qk_ps{i}", [128, TPC], F32))
                     for i in range(2)]
            v_ps = [p1.enter_context(nc.psum_tensor(f"v_ps{i}", [128, 512], F32))
                    for i in range(2)]

            g_x = [P.dgroup(f"x1ld{i}") for i in range(2)]
            g_ws = [P.dgroup(f"ws{i}") for i in range(2)]
            g_kst = [P.dgroup(f"kst{i}") for i in range(2)]
            g_vst = [P.dgroup(f"vst{i}") for i in range(2)]

            wsrc = {0: wqt, 1: wkt, 2: wvt}
            kind_order = (1, 2, 0)
            g_wsq = [P.dgroup(f"wsq{i}") for i in range(2)]
            slab_loads = {}
            slab_loads[0] = P.dma("pl", g_ws[0],
                                  wslab[0][:, :].rearrange("p (c o) -> p c o", o=512),
                                  wkt[:, 0:512].rearrange("(c p) o -> p c o", p=128))
            qslab_tok = [P.dma("pl", g_wsq[i],
                               wslab_q[i][:, :].rearrange("p (c o) -> p c o", o=512),
                               wqt[:, i * 512:(i + 1) * 512]
                               .rearrange("(c p) o -> p c o", p=128))
                         for i in range(2)]

            xt_tok = [None, None]
            lnt_tok = [None, None]
            tp_tok = None
            for tt in range(NTT):
                sl = tt % 2
                deps = [xt_tok[sl]] if xt_tok[sl] else []
                t_x = P.dma("sp", g_x[sl], xt[sl][:, :], xr[tt * 128:(tt + 1) * 128, :],
                            deps=deps)
                nd = [t_x]
                if lnt_tok[sl]:
                    nd.append(lnt_tok[sl])
                t_ln = _ln_chain(P, xt[sl][:, :], lnt[sl][:, :], scr, nd)
                xt_tok[sl] = t_ln
                for g in range(4):
                    mm = []
                    for j in range(4):
                        dc = 4 * g + j
                        d0 = [t_ln, t_eye]
                        if tp_tok:
                            d0.append(tp_tok)
                        mm.append(P.op("pe", lambda e, sl=sl, dc=dc, j=j: e.matmul(
                            tps[:, j * 128:(j + 1) * 128],
                            lnt[sl][:, dc * 128:(dc + 1) * 128], eye[:, :],
                            start=True, stop=True), deps=d0))
                    tp_tok = P.op("act", lambda e, tt=tt, g=g: e.copy(
                        ln1t[:, :].rearrange("p (c n) -> p c n", n=TPC)
                        [:, 4 * g:4 * g + 4, tt * 128:(tt + 1) * 128],
                        tps[:, :].rearrange("p (c n) -> p c n", n=128)), deps=mm)
                    lnt_tok[sl] = tp_tok
            ln1t_ready = tp_tok

            if "d_lnt" in dbg_aps:
                P.dma("pl", g_cpl, dbg_aps["d_lnt"][:, :], ln1t[:, :],
                      deps=[ln1t_ready])

            ut_done = {}
            prev_u = None
            for l in kind_order:
                for qh in range(2):
                    mm = []
                    for dc in range(NDC):
                        d0 = [ln1t_ready, t_bt]
                        if dc == 0 and prev_u:
                            d0.append(prev_u)
                        mm.append(P.op("pe", lambda e, l=l, dc=dc, qh=qh: e.matmul(
                            u_ps[:, :],
                            bt[:, dc * 64 + l * RK: dc * 64 + (l + 1) * RK],
                            ln1t[:, dc * TPC + qh * 512: dc * TPC + qh * 512 + 512],
                            start=(dc == 0), stop=(dc == NDC - 1)), deps=d0))
                    prev_u = P.op("act", lambda e, l=l, qh=qh: e.mul(
                        ut[:, l * TPC + qh * 512: l * TPC + qh * 512 + 512],
                        u_ps[:, :], 0.5), deps=mm)
                ut_done[l] = prev_u

            qkps_tok = [None, None]
            vps_tok = [None, None]
            ktile_tok = [None, None]
            vtile_tok = [None, None]
            ring_use = {}
            ridx = [0]

            def ring_load(l, oq):
                r = ridx[0]
                ridx[0] += 1
                slot = r % 2
                if r == 0:
                    return wslab[0], slab_loads[0]
                wdeps = []
                u = ring_use.get(r - 2)
                if u:
                    wdeps.append(u)
                t = P.dma("pl", g_ws[slot],
                          wslab[slot][:, :].rearrange("p (c o) -> p c o", o=512),
                          wsrc[l][:, oq * 512:(oq + 1) * 512]
                          .rearrange("(c p) o -> p c o", p=128), deps=wdeps)
                return wslab[slot], t

            def emit_qk_group(l, oq, wb, t_slab):
                last_pe = None
                for j in range(4):
                    ot = oq * 4 + j
                    pp = ot % 2
                    mm = []
                    for qh in range(2):
                        for dc in range(NDC):
                            d0 = []
                            if dc == 0:
                                d0 = [t_slab, ln1t_ready, ut_done[l], t_at]
                                if qh == 0 and qkps_tok[pp]:
                                    d0.append(qkps_tok[pp])
                            mm.append(P.op("pe", lambda e, pp=pp, wb=wb, j=j, dc=dc, qh=qh: e.matmul(
                                qk_ps[pp][:, qh * 512:qh * 512 + 512],
                                wb[:, dc * 512 + j * 128: dc * 512 + (j + 1) * 128],
                                ln1t[:, dc * TPC + qh * 512: dc * TPC + qh * 512 + 512],
                                start=(dc == 0), stop=False), deps=d0))
                        mm.append(P.op("pe", lambda e, pp=pp, l=l, ot=ot, qh=qh: e.matmul(
                            qk_ps[pp][:, qh * 512:qh * 512 + 512],
                            at[:, l * D + ot * 128:l * D + (ot + 1) * 128],
                            ut[:, l * TPC + qh * 512:l * TPC + qh * 512 + 512],
                            start=False, stop=True), deps=[]))
                    last_pe = mm[-1]
                    if l == 0:
                        qkps_tok[pp] = P.op("act", lambda e, pp=pp, ot=ot: e.activation(
                            q_sb[:, ot * TPC:(ot + 1) * TPC], qk_ps[pp][:, :],
                            AF.Identity, bias=bqkv[:, ot:ot + 1]), deps=mm + [t_bqkv])
                    else:
                        ksl = ot % 2
                        d1 = mm + [t_bqkv]
                        if ktile_tok[ksl]:
                            d1.append(ktile_tok[ksl])
                        tk = P.op("act", lambda e, pp=pp, ot=ot, ksl=ksl: e.activation(
                            ktile[ksl][:, :], qk_ps[pp][:, :], AF.Identity,
                            bias=bqkv[:, 16 + ot:17 + ot]), deps=d1)
                        qkps_tok[pp] = tk
                        ktile_tok[ksl] = P.dma("sp", g_kst[ksl],
                                               cc_in[ot * 128:(ot + 1) * 128, :],
                                               ktile[ksl][:, :], deps=[tk])
                return last_pe

            def emit_v_group(oq, wb, t_slab):
                last_pe = None
                for tt in range(NTT):
                    pp = tt % 2
                    mm = []
                    d0 = [t_slab, ln1t_ready, t_or, t_bv, ut_done[2], t_at]
                    if vps_tok[pp]:
                        d0.append(vps_tok[pp])
                    mm.append(P.op("pe", lambda e, pp=pp, oq=oq: e.matmul(
                        v_ps[pp][:, :], onesrow[:, :],
                        bv_row[0:1, oq * 512:(oq + 1) * 512],
                        start=True, stop=False), deps=d0))
                    for dc in range(NDC):
                        mm.append(P.op("pe", lambda e, pp=pp, wb=wb, dc=dc, tt=tt: e.matmul(
                            v_ps[pp][:, :],
                            ln1t[:, dc * TPC + tt * 128: dc * TPC + (tt + 1) * 128],
                            wb[:, dc * 512:(dc + 1) * 512],
                            start=False, stop=False), deps=[]))
                    mm.append(P.op("pe", lambda e, pp=pp, tt=tt, oq=oq: e.matmul(
                        v_ps[pp][:, :],
                        ut[:, 2 * TPC + tt * 128:2 * TPC + (tt + 1) * 128],
                        at[:, 2 * D + oq * 512:2 * D + (oq + 1) * 512],
                        start=False, stop=True), deps=[]))
                    last_pe = mm[-1]
                    vsl = tt % 2
                    d1 = list(mm)
                    if vtile_tok[vsl]:
                        d1.append(vtile_tok[vsl])
                    tv = P.op("act", lambda e, pp=pp, vsl=vsl: e.copy(
                        vtile[vsl][:, :], v_ps[pp][:, :]), deps=d1)
                    vps_tok[pp] = tv
                    vtile_tok[vsl] = P.dma(
                        "sp", g_vst[vsl],
                        bass.AP(cc_in.tensor,
                                (2048 + 2 * tt * 128 + oq // 2) * TPC + (oq % 2) * 512,
                                [[2 * TPC, 128], [1, 512]]),
                        vtile[vsl][:, :], deps=[tv])
                return last_pe

            for oq in range(4):
                wb, t_slab = ring_load(1, oq)
                ring_use[ridx[0] - 1] = emit_qk_group(1, oq, wb, t_slab)
            for oq in range(4):
                wb, t_slab = ring_load(2, oq)
                ring_use[ridx[0] - 1] = emit_v_group(oq, wb, t_slab)
            q_ring = [ring_load(0, 2), ring_load(0, 3)]
            P.op("pl", lambda e: e.collective_compute(
                "AllGather", ALU.bypass, replica_groups=[list(range(NCORES))],
                ins=[cc_in], outs=[cc_out]),
                deps=[g_kst[0].tok(), g_kst[1].tok(), g_vst[0].tok(),
                      g_vst[1].tok()], auto=False, inc_extra=(cc_sem, 1))
            cc_tok = (cc_sem, 1)
            for oq in range(4):
                if oq < 2:
                    wb, t_slab = wslab_q[oq], qslab_tok[oq]
                else:
                    wb, t_slab = q_ring[oq - 2]
                last = emit_qk_group(0, oq, wb, t_slab)
                if oq >= 2:
                    ring_use[8 + (oq - 2)] = last

        P.barrier()

        s_ao = ExitStack()
        ao_sb = s_ao.enter_context(nc.sbuf_tensor("ao_sb", [128, NDC * TPC], BF16,
                                                   side="right"))

        with ExitStack() as p2:
            sb2 = lambda n, s, d: p2.enter_context(nc.sbuf_tensor(n, s, d))
            kh = [sb2(f"kh{i}", [128, SEQ], BF16) for i in range(2)]
            vp = [sb2(f"vp{i}", [128, 64 * 256], BF16) for i in range(2)]
            pt = [sb2(f"pt{i}", [128, TPC], BF16) for i in range(4)]
            acc_a = sb2("acc_a", [128, TPC], F32)
            acc_b = sb2("acc_b", [128, TPC], F32)
            onesf = sb2("onesf", [128, 128], F32)
            recip = sb2("recip", [1, TPC], F32)
            bcast = sb2("bcast", [128, TPC], F32)

            s_ps = [p2.enter_context(nc.psum_tensor(f"s_ps{i}", [128, TPC], F32))
                    for i in range(2)]
            pv_ps = [p2.enter_context(nc.psum_tensor(f"pv_ps{i}", [128, TPC], F32))
                     for i in range(2)]

            shiftc = sb2("shiftc", [128, 1], F32)
            t_shift = P.op("pl", lambda e: e.memset(shiftc[:, :], EXP_SHIFT))
            t_ocf = P.op("pl", lambda e: e.memset(onesf[:, :], 1.0))

            g_kh = [P.dgroup(f"kh{i}") for i in range(2)]
            g_vp = [P.dgroup(f"vp{i}") for i in range(2)]

            kh_use = [None, None]
            vp_use = [None, None]

            def load_kh(h):
                slot = h % 2
                deps = [cc_tok]
                if kh_use[slot]:
                    deps.append(kh_use[slot])
                for r in range(NCORES):
                    P.dma("sp", g_kh[slot], kh[slot][:, r * TPC:(r + 1) * TPC],
                          cc_out[r * 4096 + h * 128: r * 4096 + (h + 1) * 128, :],
                          deps=deps if r == 0 else [])

            def load_vp(pr):
                slot = pr % 2
                deps = [cc_tok]
                if vp_use[slot]:
                    deps.append(vp_use[slot])
                for r in range(NCORES):
                    src = bass.AP(cc_out.tensor,
                                  (r * 4096 + 2048) * TPC + pr * 256,
                                  [[2048, 128], [128 * 2048, 8], [1, 256]])
                    dst = vp[slot][:, r * 8 * 256:(r * 8 + 8) * 256] \
                        .rearrange("p (a d) -> p a d", d=256)
                    P.dma("act", g_vp[slot], dst, src, deps=deps if r == 0 else [])

            load_kh(0)
            load_kh(1)
            load_vp(0)

            prev_exp62 = None
            prev_rec = None
            prev_sum_a = None
            prev_sum_b = None
            prev_evac = [None, None]
            last_bcast_reader = None
            pt_readers = {0: [], 1: [], 2: [], 3: []}
            pend = None

            def emit_epi1(pd):
                t_sum_b = None
                for half in range(2):
                    t_sum_b = P.op("pe", lambda e, half=half: e.matmul(
                        s_ps[1][:, half * 512:(half + 1) * 512], onesf[:, :],
                        acc_b[:, half * 512:(half + 1) * 512],
                        start=False, stop=True), deps=[pd['acc63']])
                dr = [t_sum_b]
                if pd['last_bcast_reader']:
                    dr.append(pd['last_bcast_reader'])
                t_rec = P.op("dve", lambda e: e.reciprocal(
                    bcast[:, :], s_ps[1][:, :]), deps=dr)
                pd['t_rec'] = t_rec
                return t_sum_b

            def emit_epi2(pd):
                return P.op("dve", lambda e, pvs=pd['pvs'], h=pd['h']: e.tensor_tensor(
                    ao_sb[:, h * TPC:(h + 1) * TPC], pv_ps[pvs][:, :], bcast[:, :],
                    ALU.mult), deps=[pd['t_rec'], pd['pv63']])

            for h in range(H):
                hs = h % 2
                pvs = h % 2
                vslot = (h // 2) % 2
                khld = g_kh[hs].tok()
                vpld = g_vp[vslot].tok()
                if h % 2 == 0 and h // 2 + 1 < H // 2:
                    load_vp(h // 2 + 1)

                s_tok = {}
                exp_tok = {}
                acc_tok = {}
                pv_tok = {}
                j = h % 2
                for step in range(65):
                    if step < 64:
                        kt = step
                        sl = kt % 2
                        ps = kt % 4
                        d0 = [khld]
                        if kt >= 2:
                            d0.append(exp_tok[kt - 2])
                        elif kt == 0 and prev_exp62:
                            d0.append(prev_exp62)
                        elif kt == 1 and prev_rec:
                            d0.append(prev_rec)
                        for qh in range(2):
                            s_tok[kt] = P.op("pe", lambda e, hs=hs, kt=kt, h=h, sl=sl, qh=qh: e.matmul(
                                s_ps[sl][:, qh * 512:(qh + 1) * 512],
                                kh[hs][:, kt * 128:(kt + 1) * 128],
                                q_sb[:, h * TPC + qh * 512: h * TPC + (qh + 1) * 512],
                                start=True, stop=True), deps=d0 if qh == 0 else [])
                        if step == 0 and pend is not None:
                            prev_sum_b = emit_epi1(pend)
                            prev_rec = pend['t_rec']
                        d1 = [s_tok[kt], t_shift] + pt_readers[ps]
                        exp_tok[kt] = P.op("act", lambda e, ps=ps, sl=sl: e.activation(
                            pt[ps][:, :], s_ps[sl][:, :],
                            AF.Exp, bias=shiftc[:, 0:1], scale=SCALE), deps=d1)
                        a_buf = acc_a if kt % 2 == 0 else acc_b
                        if kt < 2:
                            da = [exp_tok[kt]]
                            if kt == 0 and prev_sum_a:
                                da.append(prev_sum_a)
                            if kt == 1 and prev_sum_b:
                                da.append(prev_sum_b)
                            acc_tok[kt] = P.op("dve", lambda e, ps=ps, a=a_buf: e.tensor_copy(
                                a[:, :], pt[ps][:, :]), deps=da)
                        else:
                            acc_tok[kt] = P.op("dve", lambda e, ps=ps, a=a_buf: e.tensor_tensor(
                                a[:, :], a[:, :], pt[ps][:, :], ALU.add),
                                deps=[exp_tok[kt], acc_tok[kt - 2]])
                    if step >= 1:
                        kt = step - 1
                        ps = kt % 4
                        d2 = [exp_tok[kt], vpld]
                        if kt == 0 and prev_evac[pvs]:
                            d2.append(prev_evac[pvs])
                        for qh in range(2):
                            pv_tok[kt] = P.op("pe", lambda e, pvs=pvs, vslot=vslot, j=j, kt=kt, qh=qh, ps=ps: e.matmul(
                                pv_ps[pvs][:, qh * 512:(qh + 1) * 512],
                                vp[vslot][:, kt * 256 + j * 128: kt * 256 + (j + 1) * 128],
                                pt[ps][:, qh * 512:(qh + 1) * 512],
                                start=(kt == 0), stop=(kt == 63)), deps=d2 if qh == 0 else [])
                        pt_readers[ps] = [pv_tok[kt], acc_tok[kt]]
                    if step == 4 and pend is not None:
                        t_evac = emit_epi2(pend)
                        prev_evac[pend['pvs']] = t_evac
                        last_bcast_reader = t_evac
                        pend = None
                kh_use[hs] = s_tok[63]
                vp_use[vslot] = pv_tok[63]
                if h + 2 < H:
                    load_kh(h + 2)

                t_sa = None
                for half in range(2):
                    t_sa = P.op("pe", lambda e, half=half: e.matmul(
                        s_ps[1][:, half * 512:(half + 1) * 512], onesf[:, :],
                        acc_a[:, half * 512:(half + 1) * 512],
                        start=True, stop=False),
                        deps=[acc_tok[62], exp_tok[63], t_ocf])
                prev_sum_a = t_sa
                prev_exp62 = exp_tok[62]
                pend = {'h': h, 'pvs': pvs, 'acc63': acc_tok[63],
                        'pv63': pv_tok[63],
                        'last_bcast_reader': last_bcast_reader}

            prev_sum_b = emit_epi1(pend)
            prev_rec = pend['t_rec']
            t_evac = emit_epi2(pend)
            prev_evac[pend['pvs']] = t_evac
            last_bcast_reader = t_evac
            pend = None

            if "d_ao" in dbg_aps:
                P.dma("pl", g_cpl, dbg_aps["d_ao"][:, :], ao_sb[:, :],
                      deps=[prev_evac[0], prev_evac[1]])
            if "d_q" in dbg_aps:
                P.dma("pl", g_cpl, dbg_aps["d_q"][:, :], q_sb[:, :],
                      deps=[prev_evac[0], prev_evac[1]])

        P.barrier()
        s_q.close()

        ln2t = sb("ln2t", [128, NDC * TPC], BF16)
        g_x1s = [P.dgroup(f"x1sp{i}") for i in range(2)]

        with ExitStack() as p3:
            sb3 = lambda n, s, d: p3.enter_context(nc.sbuf_tensor(n, s, d))
            wotb = [sb3(f"wotb{i}", [128, D], BF16) for i in range(2)]
            xt3 = [sb3(f"xt3_{i}", [128, D], F32) for i in range(2)]
            x1t = [sb3(f"x1t_{i}", [128, D], F32) for i in range(2)]
            ln2tm = sb3("ln2tm", [128, NTT * D], BF16)
            scr3 = (sb3("xsq3", [128, D], F32), sb3("s1t3", [128, 1], F32),
                    sb3("s2t3", [128, 1], F32), sb3("mt3", [128, 1], F32),
                    sb3("msqt3", [128, 1], F32), sb3("biast3", [128, 1], F32),
                    sb3("stdt3", [128, 1], F32), sb3("rstdt3", [128, 1], F32))

            bo_row = sb3("bo_sb", [1, D], BF16)
            bt3 = sb3("bt3_sb", [128, NDC * 4 * RK], BF16)
            at3 = sb3("at3_sb", [RK, 4 * D], BF16)
            t_bo = P.dma("pl", P.dgroup("c_bo"), bo_row[:, :], bo_h[:, :])
            t_bt = P.dma("pl", P.dgroup("c_bt3"),
                         bt3[:, :].rearrange("p (c r) -> p c r", r=4 * RK),
                         bt_h.rearrange("(c p) r -> p c r", p=128))
            t_at = P.dma("pl", P.dgroup("c_at3"), at3[:, :], at_h[:, :])

            g_wot = [P.dgroup(f"wot{i}") for i in range(2)]
            g_x3 = [P.dgroup(f"x3ld{i}") for i in range(2)]

            with ExitStack() as p3a:
                o_ps = [p3a.enter_context(nc.psum_tensor(f"o_ps{i}", [128, 512], F32))
                        for i in range(8)]

                uo_tok = None
                for qh in range(2):
                    mm = []
                    for dc in range(NDC):
                        d0 = [t_bt]
                        if dc == 0 and uo_tok:
                            d0.append(uo_tok)
                        mm.append(P.op("pe", lambda e, dc=dc, qh=qh: e.matmul(
                            o_ps[0][0:RK, :], bt3[:, dc * 64 + 48: dc * 64 + 64],
                            ao_sb[:, dc * TPC + qh * 512: dc * TPC + qh * 512 + 512],
                            start=(dc == 0), stop=(dc == NDC - 1)), deps=d0))
                    uo_tok = P.op("act", lambda e, qh=qh: e.mul(
                        ut[:, 3 * TPC + qh * 512:3 * TPC + qh * 512 + 512],
                        o_ps[0][0:RK, :], 0.5), deps=mm)
                uto_ready = uo_tok

                wot_use = [None, None]
                x1_tok = [None, None]
                xt3_tok = [None, None]
                ln2_tiles = []
                evac_tok = {}
                for tp in range(4):
                    seed = {}
                    for i in range(2):
                        for oc in range(4):
                            g = 4 * i + oc
                            d0 = [t_or, t_bo, uto_ready]
                            pv = evac_tok.get((tp - 1, g))
                            if pv:
                                d0.append(pv)
                            seed[g] = P.op("pe", lambda e, g=g, oc=oc: e.matmul(
                                o_ps[g][:, :], onesrow[:, :],
                                bo_row[0:1, oc * 512:(oc + 1) * 512],
                                start=True, stop=False), deps=d0)
                    last_mm = {g: [seed[g]] for g in range(8)}
                    for dc in range(NDC):
                        slot = dc % 2
                        wdeps = [wot_use[slot]] if wot_use[slot] else []
                        P.dma("pl", g_wot[slot], wotb[slot][:, :],
                              wot[dc * 128:(dc + 1) * 128, :], deps=wdeps)
                        wtok = g_wot[slot].tok()
                        lw = None
                        for i in range(2):
                            tt = 2 * tp + i
                            for oc in range(4):
                                g = 4 * i + oc
                                lw = P.op("pe", lambda e, g=g, slot=slot, dc=dc, tt=tt, oc=oc: e.matmul(
                                    o_ps[g][:, :],
                                    ao_sb[:, dc * TPC + tt * 128: dc * TPC + (tt + 1) * 128],
                                    wotb[slot][:, oc * 512:(oc + 1) * 512],
                                    start=False, stop=False), deps=[wtok])
                                last_mm[g].append(lw)
                        wot_use[slot] = lw
                    for i in range(2):
                        tt = 2 * tp + i
                        sl = tt % 2
                        xdeps = [xt3_tok[sl]] if xt3_tok[sl] else []
                        t_x = P.dma("sp", g_x3[sl], xt3[sl][:, :],
                                    xr[tt * 128:(tt + 1) * 128, :], deps=xdeps)
                        tile_done = []
                        for oc in range(4):
                            g = 4 * i + oc
                            t_lo = P.op("pe", lambda e, g=g, tt=tt, oc=oc: e.matmul(
                                o_ps[g][:, :],
                                ut[:, 3 * TPC + tt * 128:3 * TPC + (tt + 1) * 128],
                                at3[:, 3 * D + oc * 512:3 * D + (oc + 1) * 512],
                                start=False, stop=True), deps=[uto_ready, t_at])
                            dE = [t_lo, t_x] + last_mm[g]
                            if x1_tok[sl]:
                                dE.append(x1_tok[sl])
                            tkv = P.op("dve", lambda e, g=g, sl=sl, oc=oc: e.tensor_tensor(
                                x1t[sl][:, oc * 512:(oc + 1) * 512], o_ps[g][:, :],
                                xt3[sl][:, oc * 512:(oc + 1) * 512], ALU.add), deps=dE)
                            evac_tok[(tp, g)] = tkv
                            tile_done.append(tkv)
                        xt3_tok[sl] = tile_done[-1]
                        t_ln2 = _ln_chain(P, x1t[sl][:, :],
                                          ln2tm[:, tt * D:(tt + 1) * D], scr3,
                                          tile_done)
                        x1_tok[sl] = P.dma("sp", g_x1s[sl],
                                           x1_sp[tt * 128:(tt + 1) * 128, :],
                                           x1t[sl][:, :], deps=[t_ln2])
                        ln2_tiles.append((tt, t_ln2))
                last_o_evac = evac_tok[(3, 7)]

            P.barrier()
            with nc.psum_tensor("tps3", [128, 512], F32) as tps3:
                tp3_tok = None
                for tt, t_ln2 in ln2_tiles:
                    for g in range(4):
                        mm = []
                        for j in range(4):
                            dc = 4 * g + j
                            d0 = [t_ln2, t_eye]
                            if tp3_tok:
                                d0.append(tp3_tok)
                            mm.append(P.op("pe", lambda e, tt=tt, dc=dc, j=j: e.matmul(
                                tps3[:, j * 128:(j + 1) * 128],
                                ln2tm[:, tt * D + dc * 128: tt * D + (dc + 1) * 128],
                                eye[:, :], start=True, stop=True), deps=d0))
                        tp3_tok = P.op("act", lambda e, tt=tt, g=g: e.copy(
                            ln2t[:, :].rearrange("p (c n) -> p c n", n=TPC)
                            [:, 4 * g:4 * g + 4, tt * 128:(tt + 1) * 128],
                            tps3[:, :].rearrange("p (c n) -> p c n", n=128)), deps=mm)
                ln2t_ready = tp3_tok

            if "d_ln2t" in dbg_aps:
                P.dma("pl", g_cpl, dbg_aps["d_ln2t"][:, :], ln2t[:, :],
                      deps=[ln2t_ready])
            if "d_x1" in dbg_aps:
                pass

        P.barrier()
        s_ao.close()

        for _e in ENGINES:
            P.wait(_e, [g_x1s[0].tok(), g_x1s[1].tok()])
        with ExitStack() as p4:
            sb4 = lambda n, s, d: p4.enter_context(nc.sbuf_tensor(n, s, d))
            fuse = sb4("fuse", [128, 64 * 512], BF16)
            wgu = [sb4(f"wgu{i}", [128, NDC * 256], BF16) for i in range(2)]
            wub = [sb4(f"wub{i}", [128, NDC * 256], BF16) for i in range(2)]
            wdb = [sb4(f"wdb{i}", [128, 8 * 512], BF16) for i in range(2)]
            gtmp = sb4("gtmp", [128, 512], F32)
            utmp = sb4("utmp", [128, 512], F32)
            dtmp = sb4("dtmp", [128, 512], F32)
            x1rd = [sb4(f"x1rd{i}", [128, 512], F32) for i in range(2)]
            x2t = [sb4(f"x2t{i}", [128, 512], F32) for i in range(2)]
            sgu = sb4("sgu_sb", [128, 128], F32)
            sdb = sb4("sdb_sb", [128, D], F32)
            t_sgu = P.dma("sp", P.dgroup("c_sgu"), sgu[:, :], sgu_h[:, :])
            with nc.sbuf_tensor("sdrow_sb", [1, D], F32) as sdrow, \
                 nc.psum_tensor("sdb_ps", [128, D], F32) as sdb_ps:
                t_sdrow = P.dma("sp", P.dgroup("c_sdrow"), sdrow[:, :], sd_h[:, :])
                toks = []
                for j in range(4):
                    toks.append(P.op("pe", lambda e, j=j: e.matmul(
                        sdb_ps[:, j * 512:(j + 1) * 512], onesrow_f[:, :],
                        sdrow[0:1, j * 512:(j + 1) * 512], start=True, stop=True),
                        deps=[t_orf, t_sdrow]))
                t_sdb = P.op("dve", lambda e: e.tensor_copy(sdb[:, :], sdb_ps[:, :]),
                             deps=toks)

            g_wg = [P.dgroup(f"wg{i}") for i in range(2)]
            g_wd = [P.dgroup(f"wd{i}") for i in range(2)]
            g_x1r = [P.dgroup(f"x1rd{i}") for i in range(2)]
            g_out = [P.dgroup(f"outst{i}") for i in range(2)]

            wgu_use = [None, None]
            wd_use = [None, None]
            x2_tok = [None, None]
            x1rd_reader = [None, None]
            prev_ftok = None
            prev_x2 = None

            for th in range(2):
                with ExitStack() as p4a:
                    gu_ps = [p4a.enter_context(
                        nc.psum_tensor(f"gu_ps{th}_{i}", [128, 512], F32))
                        for i in range(4)]
                    gu_tok = [None] * 4
                    fuse_tok = {}
                    for grp in range(32):
                        slot = grp % 2
                        wdeps = [wgu_use[slot]] if wgu_use[slot] else []
                        P.dma("pl", g_wg[slot],
                              wgu[slot][:, :].rearrange("p (c o) -> p c o", o=256),
                              wg_h[:, grp * 256:(grp + 1) * 256]
                              .rearrange("(c p) o -> p c o", p=128), deps=wdeps)
                        P.dma("pl", g_wg[slot],
                              wub[slot][:, :].rearrange("p (c o) -> p c o", o=256),
                              wu_h[:, grp * 256:(grp + 1) * 256]
                              .rearrange("(c p) o -> p c o", p=128), deps=[])
                        wtok = g_wg[slot].tok()
                        lastg = None
                        for ii in range(2):
                            it = grp * 2 + ii
                            pg = it % 2
                            pu = 2 + it % 2
                            mmg, mmu = [], []
                            for hc in range(NDC):
                                d0 = []
                                if hc == 0:
                                    d0 = [wtok, ln2t_ready, t_sdb]
                                    if gu_tok[pg]:
                                        d0.append(gu_tok[pg])
                                d0u = []
                                if hc == 0:
                                    d0u = [wtok]
                                    if gu_tok[pu]:
                                        d0u.append(gu_tok[pu])
                                mmg.append(P.op("pe", lambda e, pg=pg, slot=slot, hc=hc, ii=ii, th=th: e.matmul(
                                    gu_ps[pg][:, :],
                                    wgu[slot][:, hc * 256 + ii * 128: hc * 256 + (ii + 1) * 128],
                                    ln2t[:, hc * TPC + th * 512: hc * TPC + th * 512 + 512],
                                    start=(hc == 0), stop=(hc == NDC - 1)), deps=d0))
                                mmu.append(P.op("pe", lambda e, pu=pu, slot=slot, hc=hc, ii=ii, th=th: e.matmul(
                                    gu_ps[pu][:, :],
                                    wub[slot][:, hc * 256 + ii * 128: hc * 256 + (ii + 1) * 128],
                                    ln2t[:, hc * TPC + th * 512: hc * TPC + th * 512 + 512],
                                    start=(hc == 0), stop=(hc == NDC - 1)), deps=d0u))
                            lastg = mmu[-1]
                            gdeps = mmg + [t_sgu]
                            udeps = mmu + [t_sgu]
                            if prev_ftok:
                                gdeps.append(prev_ftok)
                                udeps.append(prev_ftok)
                            t_g = P.op("act", lambda e, pg=pg, it=it: e.activation(
                                gtmp[:, :], gu_ps[pg][:, :], AF.Silu,
                                scale=sgu[:, it:it + 1]), deps=gdeps)
                            t_u = P.op("dve", lambda e, pu=pu, it=it: e.tensor_scalar(
                                utmp[:, :], gu_ps[pu][:, :], sgu[:, 64 + it:65 + it],
                                None, ALU.mult), deps=udeps)
                            ftok = P.op("dve", lambda e, it=it: e.tensor_tensor(
                                fuse[:, it * 512:(it + 1) * 512], gtmp[:, :],
                                utmp[:, :], ALU.mult), deps=[t_g, t_u])
                            gu_tok[pg] = ftok
                            gu_tok[pu] = ftok
                            fuse_tok[it] = ftok
                            prev_ftok = ftok
                        wgu_use[slot] = lastg
                    fuse_all = [fuse_tok[63]]

                P.barrier()

                with ExitStack() as p4b:
                    dn_ps = [p4b.enter_context(
                        nc.psum_tensor(f"dn_ps{th}_{i}", [128, 512], F32))
                        for i in range(8)]
                    dn_evac = {}
                    for oc in range(4):
                        base = 4 * (oc % 2)
                        lastd = None
                        for icg in range(8):
                            slot = icg % 2
                            wdeps = [wd_use[slot]] if wd_use[slot] else []
                            P.dma("pl", g_wd[slot],
                                  wdb[slot][:, :].rearrange("p (c o) -> p c o", o=512),
                                  wd_h[icg * 1024:(icg + 1) * 1024,
                                       oc * 512:(oc + 1) * 512]
                                  .rearrange("(c p) o -> p c o", p=128), deps=wdeps)
                            wtok = g_wd[slot].tok()
                            for ici in range(8):
                                ic = icg * 8 + ici
                                for ts in range(4):
                                    d0 = [wtok]
                                    if ic == 0 and ts == 0:
                                        d0 += fuse_all
                                        for tsx in range(4):
                                            if (oc - 2, tsx) in dn_evac:
                                                d0.append(dn_evac[(oc - 2, tsx)])
                                    lastd = P.op("pe", lambda e, base=base, ts=ts, slot=slot, ici=ici, ic=ic: e.matmul(
                                        dn_ps[base + ts][:, :],
                                        fuse[:, ic * 512 + ts * 128: ic * 512 + (ts + 1) * 128],
                                        wdb[slot][:, ici * 512:(ici + 1) * 512],
                                        start=(ic == 0), stop=(ic == 63)), deps=d0)
                            wd_use[slot] = lastd
                        for ts in range(4):
                            xsl = ts % 2
                            xd = [g_x1s[0].tok(), g_x1s[1].tok()]
                            if x1rd_reader[xsl]:
                                xd.append(x1rd_reader[xsl])
                            t_x1r = P.dma("sp", g_x1r[xsl],
                                          x1rd[xsl][:, :],
                                          bass.AP(x1_sp.tensor,
                                                  (th * 512 + ts * 128) * D + oc * 512,
                                                  [[D, 128], [1, 512]]), deps=xd)
                            d0 = [lastd, t_sdb]
                            if (oc - 1, ts) in dn_evac:
                                d0.append(dn_evac[(oc - 1, ts)])
                            if prev_x2:
                                d0.append(prev_x2)
                            t_sc = P.op("dve", lambda e, base=base, ts=ts, oc=oc: e.tensor_tensor(
                                dtmp[:, :], dn_ps[base + ts][:, :],
                                sdb[:, oc * 512:(oc + 1) * 512], ALU.mult), deps=d0)
                            d1 = [t_sc, t_x1r]
                            if x2_tok[xsl]:
                                d1.append(x2_tok[xsl])
                            t_x2 = P.op("dve", lambda e, ts=ts, xsl=xsl, oc=oc: e.tensor_tensor(
                                x2t[xsl][:, :], dtmp[:, :],
                                x1rd[xsl][:, :], ALU.add), deps=d1)
                            prev_x2 = t_x2
                            dn_evac[(oc, ts)] = t_x2
                            x1rd_reader[xsl] = t_x2
                            x2_tok[xsl] = P.dma("sp", g_out[xsl],
                                                bass.AP(out_h.tensor,
                                                        (th * 512 + ts * 128) * D + oc * 512,
                                                        [[D, 128], [1, 512]]),
                                                x2t[xsl][:, :], deps=[t_x2])
                if th == 0:
                    P.barrier()

            P.wait("sp", [g_out[0].tok(), g_out[1].tok()])

        P.finalize()
    return nc


def _host_layouts(inp):
    f = np.ascontiguousarray
    a = {k: np.asarray(v) for k, v in inp.items()}
    shared = {
        "wqt": f(a["L0_attn_Wq"].T),
        "wkt": f(a["L0_attn_Wk"].T),
        "wvt": f(a["L0_attn_Wv"].T),
        "wot": f(a["L0_attn_Wo"].T),
        "bqkv": f(np.concatenate([a["L0_attn_bq"].reshape(16, 128).T,
                                  a["L0_attn_bk"].reshape(16, 128).T,
                                  a["L0_attn_bv"].reshape(16, 128).T], axis=1)),
        "bo_row": f(a["L0_attn_bo"].reshape(1, D)),
        "bv_row": f(a["L0_attn_bv"].reshape(1, D)),
        "btcat": f(np.concatenate([a["L0_attn_Bq"].T, a["L0_attn_Bk"].T,
                                   a["L0_attn_Bv"].T, a["L0_attn_Bo"].T], axis=1)),
        "atcat": f(np.concatenate([a["L0_attn_Aq"].T, a["L0_attn_Ak"].T,
                                   a["L0_attn_Av"].T, a["L0_attn_Ao"].T], axis=1)),
        "sgu": f(np.concatenate([a["L0_ffn_s_gate"].reshape(64, 128).T,
                                 a["L0_ffn_s_up"].reshape(64, 128).T], axis=1)),
        "sdrow": f(a["L0_ffn_s_down"].reshape(1, D)),
        "wg": a["L0_ffn_w_gate"],
        "wu": a["L0_ffn_w_up"],
        "wd": a["L0_ffn_w_down"],
    }
    x = a["L0_attn_x"].reshape(SEQ, D)
    in_maps = []
    for r in range(NCORES):
        m = dict(shared)
        m["x"] = f(x[r * TPC:(r + 1) * TPC, :])
        in_maps.append(m)
    return in_maps


def kernel(**inputs):
    in_maps = _host_layouts(inputs)
    nc = build_nc()
    res = run_bass_kernel_spmd(nc, in_maps, core_ids=list(range(NCORES)))
    out = np.concatenate([np.asarray(r["out"]) for r in res.results], axis=0)
    return out.reshape(1, SEQ, D).astype(np.float32, copy=False)


# version 11 (final)
# speedup vs baseline: 1.1026x; 1.0152x over previous
import numpy as np
from contextlib import ExitStack

import concourse.bass as bass
import concourse.mybir as mybir
from concourse.bass_utils import run_bass_kernel_spmd
from concourse.replica_groups import maybe_share_collective_output_space

F32 = mybir.dt.float32
BF16 = mybir.dt.bfloat16
I8 = mybir.dt.int8
AF = mybir.ActivationFunctionType
ALU = mybir.AluOpType
AX = mybir.AxisListType

NCORES = 8
SEQ = 8192
TPC = SEQ // NCORES
D = 2048
H = 16
HD = 128
RK = 16
INTER = 8192
NDC = D // 128
NTT = TPC // 128
SCALE = float(HD) ** -0.5
EXP_SHIFT = -3.0
EPS = 1e-5

ENGINES = ("sp", "pe", "act", "dve", "pl")
_SECTION = {"sp": "sync", "pe": "tensor", "act": "scalar", "dve": "vector", "pl": "gpsimd"}


class DGroup:
    def __init__(self, sem):
        self.sem = sem
        self.n = 0

    def tok(self):
        return (self.sem, self.n * 16)


class Prog:

    def __init__(self, nc, ctx):
        self.nc = nc
        self.ctx = ctx
        self.blocks = []
        self.cur = {e: [] for e in ENGINES}
        self.esem = {e: ctx.enter_context(nc.semaphore(f"es_{e}")) for e in ENGINES}
        self.ecount = {e: 0 for e in ENGINES}
        self.waited = {e: {} for e in ENGINES}

    def dgroup(self, name):
        return DGroup(self.ctx.enter_context(self.nc.semaphore(f"dg_{name}")))

    def _w(self, eng, deps):
        out = []
        w = self.waited[eng]
        for d in deps:
            if d is None:
                continue
            sem, val = d
            if w.get(sem, 0) >= val:
                continue
            w[sem] = val
            out.append((sem, val))
        return out

    def op(self, eng, fn, deps=(), auto=True, inc_extra=None):
        waits = self._w(eng, deps)
        tok = None
        if auto:
            self.ecount[eng] += 1
            tok = (self.esem[eng], self.ecount[eng])

        def emit(e, fn=fn, waits=waits, tok=tok, inc_extra=inc_extra):
            for s, v in waits:
                e.wait_ge(s, v)
            ins = fn(e)
            if tok is not None:
                ins = ins.then_inc(tok[0], 1)
            if inc_extra is not None:
                ins.then_inc(*inc_extra)

        self.cur[eng].append(emit)
        return tok

    def dma(self, eng, group, out, in_, deps=()):
        waits = self._w(eng, deps)
        group.n += 1
        tok = (group.sem, group.n * 16)

        def emit(e, out=out, in_=in_, waits=waits, gsem=group.sem):
            for s, v in waits:
                e.wait_ge(s, v)
            e.dma_start(out=out, in_=in_).then_inc(gsem, 16)

        self.cur[eng].append(emit)
        return tok

    def wait(self, eng, deps):
        waits = self._w(eng, deps)
        if not waits:
            return

        def emit(e, waits=waits):
            for s, v in waits:
                e.wait_ge(s, v)

        self.cur[eng].append(emit)

    def barrier(self):
        self.blocks.append(self.cur)
        self.cur = {e: [] for e in ENGINES}
        for e2 in ENGINES:
            for e1 in ENGINES:
                s = self.esem[e1]
                self.waited[e2][s] = max(self.waited[e2].get(s, 0), self.ecount[e1])

    def finalize(self):
        if any(self.cur[e] for e in ENGINES):
            self.barrier()
        nc = self.nc
        for blk_ops in self.blocks:
            with nc.Block() as blk:
                for eng in ENGINES:
                    ops = blk_ops[eng]
                    if not ops:
                        continue

                    def section(e, ops=ops):
                        for emit in ops:
                            emit(e)

                    getattr(blk, _SECTION[eng])(section)


def _ln_chain(P, x_ap, out_ap, scr, deps):
    xsq, s1t, s2t, mt, msqt, biast, stdt, rstdt = scr
    t_s1 = P.op("dve", lambda e: e.reduce_sum(s1t[:, :], x_ap, axis=AX.X), deps=deps)
    t_sq = P.op("act", lambda e: e.activation(xsq[:, :], x_ap, AF.Square),
                deps=deps + [t_s1])
    t_s2 = P.op("dve", lambda e: e.reduce_sum(s2t[:, :], xsq[:, :], axis=AX.X),
                deps=[t_sq])
    t_m = P.op("dve", lambda e: e.tensor_scalar(mt[:, :], s1t[:, :], 1.0 / D, None,
                                                ALU.mult), deps=[t_s1])
    t_msq = P.op("dve", lambda e: e.tensor_tensor(msqt[:, :], mt[:, :], mt[:, :],
                                                  ALU.mult), deps=[t_m])
    t_bias = P.op("dve", lambda e: e.tensor_scalar(biast[:, :], msqt[:, :], EPS, -1.0,
                                                   ALU.subtract, ALU.mult), deps=[t_msq])
    t_std = P.op("act", lambda e: e.activation(stdt[:, :], s2t[:, :], AF.Sqrt,
                                               bias=biast[:, 0:1], scale=1.0 / D),
                 deps=[t_s2, t_bias])
    t_rstd = P.op("dve", lambda e: e.reciprocal(rstdt[:, :], stdt[:, :]), deps=[t_std])
    t_ln = P.op("dve", lambda e: e.tensor_scalar(out_ap, x_ap, mt[:, 0:1],
                                                 rstdt[:, 0:1], ALU.subtract, ALU.mult),
                deps=[t_rstd, t_m])
    return t_ln


def build_nc(dbg=()):
    nc = bass.Bass("TRN2", target_bir_lowering=False)

    xr = nc.dram_tensor("x", [TPC, D], F32, kind="ExternalInput").ap()
    wqt = nc.dram_tensor("wqt", [D, D], F32, kind="ExternalInput").ap()
    wkt = nc.dram_tensor("wkt", [D, D], F32, kind="ExternalInput").ap()
    wvt = nc.dram_tensor("wvt", [D, D], F32, kind="ExternalInput").ap()
    wot = nc.dram_tensor("wot", [D, D], F32, kind="ExternalInput").ap()
    bqkv_h = nc.dram_tensor("bqkv", [128, 48], F32, kind="ExternalInput").ap()
    bo_h = nc.dram_tensor("bo_row", [1, D], F32, kind="ExternalInput").ap()
    bv_h = nc.dram_tensor("bv_row", [1, D], F32, kind="ExternalInput").ap()
    bt_h = nc.dram_tensor("btcat", [D, 4 * RK], F32, kind="ExternalInput").ap()
    at_h = nc.dram_tensor("atcat", [RK, 4 * D], F32, kind="ExternalInput").ap()
    sgu_h = nc.dram_tensor("sgu", [128, 128], F32, kind="ExternalInput").ap()
    sd_h = nc.dram_tensor("sdrow", [1, D], F32, kind="ExternalInput").ap()
    wg_h = nc.dram_tensor("wg", [D, INTER], I8, kind="ExternalInput").ap()
    wu_h = nc.dram_tensor("wu", [D, INTER], I8, kind="ExternalInput").ap()
    wd_h = nc.dram_tensor("wd", [INTER, D], I8, kind="ExternalInput").ap()
    out_h = nc.dram_tensor("out", [TPC, D], F32, kind="ExternalOutput").ap()

    cc_in = nc.dram_tensor("cc_in", [4096, TPC], BF16, kind="Internal").ap()
    aspace = maybe_share_collective_output_space("AllGather", [list(range(NCORES))])
    cc_out = nc.dram_tensor("cc_out", [NCORES * 4096, TPC], BF16, kind="Internal",
                            addr_space=aspace).ap()
    x1_sp = nc.dram_tensor("x1_spill", [TPC, D], F32, kind="Internal").ap()

    dbg_aps = {}
    for name, shape in (("d_lnt", [128, NDC * TPC]), ("d_q", [128, NDC * TPC]),
                        ("d_ao", [128, NDC * TPC]), ("d_x1", [TPC, D]),
                        ("d_ln2t", [128, NDC * TPC])):
        if name in dbg:
            dbg_aps[name] = nc.dram_tensor(name, shape, F32, kind="ExternalOutput").ap()

    with ExitStack() as ctx:
        P = Prog(nc, ctx)
        sb = lambda n, s, d: ctx.enter_context(nc.sbuf_tensor(n, s, d))

        eye = sb("eye", [128, 128], BF16)
        onescol = sb("onescol", [128, 1], BF16)
        onesrow = sb("onesrow", [1, 128], BF16)
        onesrow_f = sb("onesrow_f", [1, 128], F32)
        ut = sb("ut_sb", [RK, 4 * TPC], BF16)

        g_cpl = P.dgroup("c_pl")
        g_csp = P.dgroup("c_sp")

        with nc.sbuf_tensor("eyef", [128, 128], F32) as eyef:
            t_iota = P.op("pl", lambda e: e.iota(eyef[:, :], [[1, 128]],
                                                 channel_multiplier=-1,
                                                 allow_small_or_imprecise_dtypes=True))
            t_eye = P.op("dve", lambda e: e.tensor_scalar(eye[:, :], eyef[:, :], 0.0,
                                                          None, ALU.is_equal),
                         deps=[t_iota])
        t_oc = P.op("pl", lambda e: e.memset(onescol[:, :], 1.0))
        t_or = P.op("pl", lambda e: e.memset(onesrow[:, :], 1.0))
        t_orf = P.op("pl", lambda e: e.memset(onesrow_f[:, :], 1.0))

        s_q = ExitStack()
        q_sb = s_q.enter_context(nc.sbuf_tensor("q_sb", [128, NDC * TPC], BF16))
        cc_sem = ctx.enter_context(nc.semaphore("cc_sem"))

        with ExitStack() as p1:
            sb1 = lambda n, s, d: p1.enter_context(nc.sbuf_tensor(n, s, d))
            ln1t = sb1("ln1t", [128, NDC * TPC], BF16)
            xt = [sb1(f"xt{i}", [128, D], F32) for i in range(2)]
            lnt = [sb1(f"lnt{i}", [128, D], BF16) for i in range(2)]
            scr = (sb1("xsq", [128, D], F32), sb1("s1t", [128, 1], F32),
                   sb1("s2t", [128, 1], F32), sb1("mt", [128, 1], F32),
                   sb1("msqt", [128, 1], F32), sb1("biast", [128, 1], F32),
                   sb1("stdt", [128, 1], F32), sb1("rstdt", [128, 1], F32))
            wslab = [sb1(f"wslab{i}", [128, NDC * 512], BF16) for i in range(2)]
            wslab_q = [sb1(f"wslabq{i}", [128, NDC * 512], BF16) for i in range(2)]
            ktile = [sb1(f"ktile{i}", [128, TPC], BF16) for i in range(2)]
            vtile = [sb1(f"vtile{i}", [128, 512], BF16) for i in range(2)]
            bqkv = sb1("bqkv_sb", [128, 48], F32)
            bv_row = sb1("bv_sb", [1, D], BF16)
            bt = sb1("bt_sb", [128, NDC * 4 * RK], BF16)
            at = sb1("at_sb", [RK, 4 * D], BF16)
            t_bqkv = P.dma("sp", P.dgroup("c_bqkv"), bqkv[:, :], bqkv_h[:, :])
            t_bv = P.dma("pl", P.dgroup("c_bv"), bv_row[:, :], bv_h[:, :])
            t_bt = P.dma("pl", P.dgroup("c_bt1"),
                         bt[:, :].rearrange("p (c r) -> p c r", r=4 * RK),
                         bt_h.rearrange("(c p) r -> p c r", p=128))
            t_at = P.dma("pl", P.dgroup("c_at1"), at[:, :], at_h[:, :])

            tps = p1.enter_context(nc.psum_tensor("tps", [128, 512], F32))
            u_ps = p1.enter_context(nc.psum_tensor("u_ps", [RK, 512], F32))
            qk_ps = [p1.enter_context(nc.psum_tensor(f"qk_ps{i}", [128, TPC], F32))
                     for i in range(2)]
            v_ps = [p1.enter_context(nc.psum_tensor(f"v_ps{i}", [128, 512], F32))
                    for i in range(2)]

            g_x = [P.dgroup(f"x1ld{i}") for i in range(2)]
            g_ws = [P.dgroup(f"ws{i}") for i in range(2)]
            g_kst = [P.dgroup(f"kst{i}") for i in range(2)]
            g_vst = [P.dgroup(f"vst{i}") for i in range(2)]

            wsrc = {0: wqt, 1: wkt, 2: wvt}
            kind_order = (1, 2, 0)
            g_wsq = [P.dgroup(f"wsq{i}") for i in range(2)]
            slab_loads = {}
            slab_loads[0] = P.dma("pl", g_ws[0],
                                  wslab[0][:, :].rearrange("p (c o) -> p c o", o=512),
                                  wkt[:, 0:512].rearrange("(c p) o -> p c o", p=128))
            qslab_tok = [P.dma("pl", g_wsq[i],
                               wslab_q[i][:, :].rearrange("p (c o) -> p c o", o=512),
                               wqt[:, i * 512:(i + 1) * 512]
                               .rearrange("(c p) o -> p c o", p=128))
                         for i in range(2)]

            xt_tok = [None, None]
            lnt_tok = [None, None]
            tp_tok = None
            for tt in range(NTT):
                sl = tt % 2
                deps = [xt_tok[sl]] if xt_tok[sl] else []
                t_x = P.dma("sp", g_x[sl], xt[sl][:, :], xr[tt * 128:(tt + 1) * 128, :],
                            deps=deps)
                nd = [t_x]
                if lnt_tok[sl]:
                    nd.append(lnt_tok[sl])
                t_ln = _ln_chain(P, xt[sl][:, :], lnt[sl][:, :], scr, nd)
                xt_tok[sl] = t_ln
                for g in range(4):
                    mm = []
                    for j in range(4):
                        dc = 4 * g + j
                        d0 = [t_ln, t_eye]
                        if tp_tok:
                            d0.append(tp_tok)
                        mm.append(P.op("pe", lambda e, sl=sl, dc=dc, j=j: e.matmul(
                            tps[:, j * 128:(j + 1) * 128],
                            lnt[sl][:, dc * 128:(dc + 1) * 128], eye[:, :],
                            start=True, stop=True), deps=d0))
                    tp_tok = P.op("act", lambda e, tt=tt, g=g: e.copy(
                        ln1t[:, :].rearrange("p (c n) -> p c n", n=TPC)
                        [:, 4 * g:4 * g + 4, tt * 128:(tt + 1) * 128],
                        tps[:, :].rearrange("p (c n) -> p c n", n=128)), deps=mm)
                    lnt_tok[sl] = tp_tok
            ln1t_ready = tp_tok

            if "d_lnt" in dbg_aps:
                P.dma("pl", g_cpl, dbg_aps["d_lnt"][:, :], ln1t[:, :],
                      deps=[ln1t_ready])

            ut_done = {}
            prev_u = None
            for l in kind_order:
                for qh in range(2):
                    mm = []
                    for dc in range(NDC):
                        d0 = [ln1t_ready, t_bt]
                        if dc == 0 and prev_u:
                            d0.append(prev_u)
                        mm.append(P.op("pe", lambda e, l=l, dc=dc, qh=qh: e.matmul(
                            u_ps[:, :],
                            bt[:, dc * 64 + l * RK: dc * 64 + (l + 1) * RK],
                            ln1t[:, dc * TPC + qh * 512: dc * TPC + qh * 512 + 512],
                            start=(dc == 0), stop=(dc == NDC - 1)), deps=d0))
                    prev_u = P.op("act", lambda e, l=l, qh=qh: e.mul(
                        ut[:, l * TPC + qh * 512: l * TPC + qh * 512 + 512],
                        u_ps[:, :], 0.5), deps=mm)
                ut_done[l] = prev_u

            qkps_tok = [None, None]
            vps_tok = [None, None]
            ktile_tok = [None, None]
            vtile_tok = [None, None]
            ring_use = {}
            ridx = [0]

            def ring_load(l, oq):
                r = ridx[0]
                ridx[0] += 1
                slot = r % 2
                if r == 0:
                    return wslab[0], slab_loads[0]
                wdeps = []
                u = ring_use.get(r - 2)
                if u:
                    wdeps.append(u)
                t = P.dma("pl", g_ws[slot],
                          wslab[slot][:, :].rearrange("p (c o) -> p c o", o=512),
                          wsrc[l][:, oq * 512:(oq + 1) * 512]
                          .rearrange("(c p) o -> p c o", p=128), deps=wdeps)
                return wslab[slot], t

            def emit_qk_group(l, oq, wb, t_slab):
                last_pe = None
                for j in range(4):
                    ot = oq * 4 + j
                    pp = ot % 2
                    mm = []
                    for qh in range(2):
                        for dc in range(NDC):
                            d0 = []
                            if dc == 0:
                                d0 = [t_slab, ln1t_ready, ut_done[l], t_at]
                                if qh == 0 and qkps_tok[pp]:
                                    d0.append(qkps_tok[pp])
                            mm.append(P.op("pe", lambda e, pp=pp, wb=wb, j=j, dc=dc, qh=qh: e.matmul(
                                qk_ps[pp][:, qh * 512:qh * 512 + 512],
                                wb[:, dc * 512 + j * 128: dc * 512 + (j + 1) * 128],
                                ln1t[:, dc * TPC + qh * 512: dc * TPC + qh * 512 + 512],
                                start=(dc == 0), stop=False), deps=d0))
                        mm.append(P.op("pe", lambda e, pp=pp, l=l, ot=ot, qh=qh: e.matmul(
                            qk_ps[pp][:, qh * 512:qh * 512 + 512],
                            at[:, l * D + ot * 128:l * D + (ot + 1) * 128],
                            ut[:, l * TPC + qh * 512:l * TPC + qh * 512 + 512],
                            start=False, stop=True), deps=[]))
                    last_pe = mm[-1]
                    if l == 0:
                        qkps_tok[pp] = P.op("act", lambda e, pp=pp, ot=ot: e.activation(
                            q_sb[:, ot * TPC:(ot + 1) * TPC], qk_ps[pp][:, :],
                            AF.Identity, bias=bqkv[:, ot:ot + 1]), deps=mm + [t_bqkv])
                    else:
                        ksl = ot % 2
                        d1 = mm + [t_bqkv]
                        if ktile_tok[ksl]:
                            d1.append(ktile_tok[ksl])
                        tk = P.op("act", lambda e, pp=pp, ot=ot, ksl=ksl: e.activation(
                            ktile[ksl][:, :], qk_ps[pp][:, :], AF.Identity,
                            bias=bqkv[:, 16 + ot:17 + ot]), deps=d1)
                        qkps_tok[pp] = tk
                        ktile_tok[ksl] = P.dma("sp", g_kst[ksl],
                                               cc_in[ot * 128:(ot + 1) * 128, :],
                                               ktile[ksl][:, :], deps=[tk])
                return last_pe

            def emit_v_group(oq, wb, t_slab):
                last_pe = None
                for tt in range(NTT):
                    pp = tt % 2
                    mm = []
                    d0 = [t_slab, ln1t_ready, t_or, t_bv, ut_done[2], t_at]
                    if vps_tok[pp]:
                        d0.append(vps_tok[pp])
                    mm.append(P.op("pe", lambda e, pp=pp, oq=oq: e.matmul(
                        v_ps[pp][:, :], onesrow[:, :],
                        bv_row[0:1, oq * 512:(oq + 1) * 512],
                        start=True, stop=False), deps=d0))
                    for dc in range(NDC):
                        mm.append(P.op("pe", lambda e, pp=pp, wb=wb, dc=dc, tt=tt: e.matmul(
                            v_ps[pp][:, :],
                            ln1t[:, dc * TPC + tt * 128: dc * TPC + (tt + 1) * 128],
                            wb[:, dc * 512:(dc + 1) * 512],
                            start=False, stop=False), deps=[]))
                    mm.append(P.op("pe", lambda e, pp=pp, tt=tt, oq=oq: e.matmul(
                        v_ps[pp][:, :],
                        ut[:, 2 * TPC + tt * 128:2 * TPC + (tt + 1) * 128],
                        at[:, 2 * D + oq * 512:2 * D + (oq + 1) * 512],
                        start=False, stop=True), deps=[]))
                    last_pe = mm[-1]
                    vsl = tt % 2
                    d1 = list(mm)
                    if vtile_tok[vsl]:
                        d1.append(vtile_tok[vsl])
                    tv = P.op("act", lambda e, pp=pp, vsl=vsl: e.copy(
                        vtile[vsl][:, :], v_ps[pp][:, :]), deps=d1)
                    vps_tok[pp] = tv
                    vtile_tok[vsl] = P.dma(
                        "sp", g_vst[vsl],
                        bass.AP(cc_in.tensor,
                                (2048 + 2 * tt * 128 + oq // 2) * TPC + (oq % 2) * 512,
                                [[2 * TPC, 128], [1, 512]]),
                        vtile[vsl][:, :], deps=[tv])
                return last_pe

            for oq in range(4):
                wb, t_slab = ring_load(1, oq)
                ring_use[ridx[0] - 1] = emit_qk_group(1, oq, wb, t_slab)
            for oq in range(4):
                wb, t_slab = ring_load(2, oq)
                ring_use[ridx[0] - 1] = emit_v_group(oq, wb, t_slab)
            q_ring = [ring_load(0, 2), ring_load(0, 3)]
            P.op("pl", lambda e: e.collective_compute(
                "AllGather", ALU.bypass, replica_groups=[list(range(NCORES))],
                ins=[cc_in], outs=[cc_out]),
                deps=[g_kst[0].tok(), g_kst[1].tok(), g_vst[0].tok(),
                      g_vst[1].tok()], auto=False, inc_extra=(cc_sem, 1))
            cc_tok = (cc_sem, 1)
            for oq in range(4):
                if oq < 2:
                    wb, t_slab = wslab_q[oq], qslab_tok[oq]
                else:
                    wb, t_slab = q_ring[oq - 2]
                last = emit_qk_group(0, oq, wb, t_slab)
                if oq >= 2:
                    ring_use[8 + (oq - 2)] = last

        P.barrier()

        s_ao = ExitStack()
        ao_sb = s_ao.enter_context(nc.sbuf_tensor("ao_sb", [128, NDC * TPC], BF16,
                                                   side="right"))

        with ExitStack() as p2:
            sb2 = lambda n, s, d: p2.enter_context(nc.sbuf_tensor(n, s, d))
            kh = [sb2(f"kh{i}", [128, SEQ], BF16) for i in range(2)]
            vp = [sb2(f"vp{i}", [128, 64 * 256], BF16) for i in range(2)]
            pt = [sb2(f"pt{i}", [128, TPC], BF16) for i in range(4)]
            acc_a = sb2("acc_a", [128, TPC], F32)
            acc_b = sb2("acc_b", [128, TPC], F32)
            lnsum = sb2("lnsum", [128, TPC], F32)
            onesf = sb2("onesf", [128, 128], F32)
            recip = sb2("recip", [1, TPC], F32)
            bcast = sb2("bcast", [128, TPC], F32)

            s_ps = [p2.enter_context(nc.psum_tensor(f"s_ps{i}", [128, TPC], F32))
                    for i in range(2)]
            pv_ps = [p2.enter_context(nc.psum_tensor(f"pv_ps{i}", [128, TPC], F32))
                     for i in range(2)]

            shiftc = sb2("shiftc", [128, 1], F32)
            t_shift = P.op("pl", lambda e: e.memset(shiftc[:, :], EXP_SHIFT))
            t_ocf = P.op("pl", lambda e: e.memset(onesf[:, :], 1.0))

            g_kh = [P.dgroup(f"kh{i}") for i in range(2)]
            g_vp = [P.dgroup(f"vp{i}") for i in range(2)]

            kh_use = [None, None]
            vp_use = [None, None]

            def load_kh(h):
                slot = h % 2
                deps = [cc_tok]
                if kh_use[slot]:
                    deps.append(kh_use[slot])
                for r in range(NCORES):
                    P.dma("sp", g_kh[slot], kh[slot][:, r * TPC:(r + 1) * TPC],
                          cc_out[r * 4096 + h * 128: r * 4096 + (h + 1) * 128, :],
                          deps=deps if r == 0 else [])

            def load_vp(pr):
                slot = pr % 2
                deps = [cc_tok]
                if vp_use[slot]:
                    deps.append(vp_use[slot])
                for r in range(NCORES):
                    src = bass.AP(cc_out.tensor,
                                  (r * 4096 + 2048) * TPC + pr * 256,
                                  [[2048, 128], [128 * 2048, 8], [1, 256]])
                    dst = vp[slot][:, r * 8 * 256:(r * 8 + 8) * 256] \
                        .rearrange("p (a d) -> p a d", d=256)
                    P.dma("act", g_vp[slot], dst, src, deps=deps if r == 0 else [])

            load_kh(0)
            load_kh(1)
            load_vp(0)

            prev_exp62 = None
            prev_rec = None
            prev_sum_a = None
            prev_sum_b = None
            prev_evac = [None, None]
            last_bcast_reader = None
            pt_readers = {0: [], 1: [], 2: [], 3: []}
            pend = None

            def emit_epi1(pd):
                t_sum_b = None
                for half in range(2):
                    t_sum_b = P.op("pe", lambda e, half=half: e.matmul(
                        s_ps[1][:, half * 512:(half + 1) * 512], onesf[:, :],
                        acc_b[:, half * 512:(half + 1) * 512],
                        start=False, stop=True), deps=[pd['acc63']])
                t_ln = P.op("act", lambda e: e.activation(
                    lnsum[:, :], s_ps[1][:, :], AF.Ln), deps=[t_sum_b])
                pd['t_ln'] = t_ln
                return t_sum_b, t_ln

            def emit_epi1b(pd):
                dx = [pd['t_ln']]
                if pd['last_bcast_reader']:
                    dx.append(pd['last_bcast_reader'])
                t_exp = P.op("act", lambda e: e.activation(
                    bcast[:, :], lnsum[:, :], AF.Exp, scale=-1.0), deps=dx)
                pd['t_exp'] = t_exp

            def emit_epi2(pd):
                return P.op("dve", lambda e, pvs=pd['pvs'], h=pd['h']: e.tensor_tensor(
                    ao_sb[:, h * TPC:(h + 1) * TPC], pv_ps[pvs][:, :], bcast[:, :],
                    ALU.mult), deps=[pd['t_exp'], pd['pv63']])

            for h in range(H):
                hs = h % 2
                pvs = h % 2
                vslot = (h // 2) % 2
                khld = g_kh[hs].tok()
                vpld = g_vp[vslot].tok()
                if h % 2 == 0 and h // 2 + 1 < H // 2:
                    load_vp(h // 2 + 1)

                s_tok = {}
                exp_tok = {}
                acc_tok = {}
                pv_tok = {}
                j = h % 2
                for step in range(65):
                    if step < 64:
                        kt = step
                        sl = kt % 2
                        ps = kt % 4
                        d0 = [khld]
                        if kt >= 2:
                            d0.append(exp_tok[kt - 2])
                        elif kt == 0 and prev_exp62:
                            d0.append(prev_exp62)
                        elif kt == 1 and prev_rec:
                            d0.append(prev_rec)
                        for qh in range(2):
                            s_tok[kt] = P.op("pe", lambda e, hs=hs, kt=kt, h=h, sl=sl, qh=qh: e.matmul(
                                s_ps[sl][:, qh * 512:(qh + 1) * 512],
                                kh[hs][:, kt * 128:(kt + 1) * 128],
                                q_sb[:, h * TPC + qh * 512: h * TPC + (qh + 1) * 512],
                                start=True, stop=True), deps=d0 if qh == 0 else [])
                        if step == 0 and pend is not None:
                            prev_sum_b, prev_rec = emit_epi1(pend)
                        if step == 2 and pend is not None:
                            emit_epi1b(pend)
                        d1 = [s_tok[kt], t_shift] + pt_readers[ps]
                        exp_tok[kt] = P.op("act", lambda e, ps=ps, sl=sl: e.activation(
                            pt[ps][:, :], s_ps[sl][:, :],
                            AF.Exp, bias=shiftc[:, 0:1], scale=SCALE), deps=d1)
                        a_buf = acc_a if kt % 2 == 0 else acc_b
                        if kt < 2:
                            da = [exp_tok[kt]]
                            if kt == 0 and prev_sum_a:
                                da.append(prev_sum_a)
                            if kt == 1 and prev_sum_b:
                                da.append(prev_sum_b)
                            acc_tok[kt] = P.op("dve", lambda e, ps=ps, a=a_buf: e.tensor_copy(
                                a[:, :], pt[ps][:, :]), deps=da)
                        else:
                            acc_tok[kt] = P.op("dve", lambda e, ps=ps, a=a_buf: e.tensor_tensor(
                                a[:, :], a[:, :], pt[ps][:, :], ALU.add),
                                deps=[exp_tok[kt], acc_tok[kt - 2]])
                    if step >= 1:
                        kt = step - 1
                        ps = kt % 4
                        d2 = [exp_tok[kt], vpld]
                        if kt == 0 and prev_evac[pvs]:
                            d2.append(prev_evac[pvs])
                        for qh in range(2):
                            pv_tok[kt] = P.op("pe", lambda e, pvs=pvs, vslot=vslot, j=j, kt=kt, qh=qh, ps=ps: e.matmul(
                                pv_ps[pvs][:, qh * 512:(qh + 1) * 512],
                                vp[vslot][:, kt * 256 + j * 128: kt * 256 + (j + 1) * 128],
                                pt[ps][:, qh * 512:(qh + 1) * 512],
                                start=(kt == 0), stop=(kt == 63)), deps=d2 if qh == 0 else [])
                        pt_readers[ps] = [pv_tok[kt], acc_tok[kt]]
                    if step == 4 and pend is not None:
                        t_evac = emit_epi2(pend)
                        prev_evac[pend['pvs']] = t_evac
                        last_bcast_reader = t_evac
                        pend = None
                kh_use[hs] = s_tok[63]
                vp_use[vslot] = pv_tok[63]
                if h + 2 < H:
                    load_kh(h + 2)

                t_sa = None
                for half in range(2):
                    t_sa = P.op("pe", lambda e, half=half: e.matmul(
                        s_ps[1][:, half * 512:(half + 1) * 512], onesf[:, :],
                        acc_a[:, half * 512:(half + 1) * 512],
                        start=True, stop=False),
                        deps=[acc_tok[62], exp_tok[63], t_ocf])
                prev_sum_a = t_sa
                prev_exp62 = exp_tok[62]
                pend = {'h': h, 'pvs': pvs, 'acc63': acc_tok[63],
                        'pv63': pv_tok[63],
                        'last_bcast_reader': last_bcast_reader}

            prev_sum_b, prev_rec = emit_epi1(pend)
            emit_epi1b(pend)
            t_evac = emit_epi2(pend)
            prev_evac[pend['pvs']] = t_evac
            last_bcast_reader = t_evac
            pend = None

            if "d_ao" in dbg_aps:
                P.dma("pl", g_cpl, dbg_aps["d_ao"][:, :], ao_sb[:, :],
                      deps=[prev_evac[0], prev_evac[1]])
            if "d_q" in dbg_aps:
                P.dma("pl", g_cpl, dbg_aps["d_q"][:, :], q_sb[:, :],
                      deps=[prev_evac[0], prev_evac[1]])

        P.barrier()
        s_q.close()

        ln2t = sb("ln2t", [128, NDC * TPC], BF16)
        g_x1s = [P.dgroup(f"x1sp{i}") for i in range(2)]

        with ExitStack() as p3:
            sb3 = lambda n, s, d: p3.enter_context(nc.sbuf_tensor(n, s, d))
            wotb = [sb3(f"wotb{i}", [128, D], BF16) for i in range(2)]
            xt3 = [sb3(f"xt3_{i}", [128, D], F32) for i in range(2)]
            x1t = [sb3(f"x1t_{i}", [128, D], F32) for i in range(2)]
            ln2tm = sb3("ln2tm", [128, NTT * D], BF16)
            scr3 = (sb3("xsq3", [128, D], F32), sb3("s1t3", [128, 1], F32),
                    sb3("s2t3", [128, 1], F32), sb3("mt3", [128, 1], F32),
                    sb3("msqt3", [128, 1], F32), sb3("biast3", [128, 1], F32),
                    sb3("stdt3", [128, 1], F32), sb3("rstdt3", [128, 1], F32))

            bo_row = sb3("bo_sb", [1, D], BF16)
            bt3 = sb3("bt3_sb", [128, NDC * 4 * RK], BF16)
            at3 = sb3("at3_sb", [RK, 4 * D], BF16)
            t_bo = P.dma("pl", P.dgroup("c_bo"), bo_row[:, :], bo_h[:, :])
            t_bt = P.dma("pl", P.dgroup("c_bt3"),
                         bt3[:, :].rearrange("p (c r) -> p c r", r=4 * RK),
                         bt_h.rearrange("(c p) r -> p c r", p=128))
            t_at = P.dma("pl", P.dgroup("c_at3"), at3[:, :], at_h[:, :])

            g_wot = [P.dgroup(f"wot{i}") for i in range(2)]
            g_x3 = [P.dgroup(f"x3ld{i}") for i in range(2)]

            with ExitStack() as p3a:
                o_ps = [p3a.enter_context(nc.psum_tensor(f"o_ps{i}", [128, 512], F32))
                        for i in range(8)]

                uo_tok = None
                for qh in range(2):
                    mm = []
                    for dc in range(NDC):
                        d0 = [t_bt]
                        if dc == 0 and uo_tok:
                            d0.append(uo_tok)
                        mm.append(P.op("pe", lambda e, dc=dc, qh=qh: e.matmul(
                            o_ps[0][0:RK, :], bt3[:, dc * 64 + 48: dc * 64 + 64],
                            ao_sb[:, dc * TPC + qh * 512: dc * TPC + qh * 512 + 512],
                            start=(dc == 0), stop=(dc == NDC - 1)), deps=d0))
                    uo_tok = P.op("act", lambda e, qh=qh: e.mul(
                        ut[:, 3 * TPC + qh * 512:3 * TPC + qh * 512 + 512],
                        o_ps[0][0:RK, :], 0.5), deps=mm)
                uto_ready = uo_tok

                wot_use = [None, None]
                x1_tok = [None, None]
                xt3_tok = [None, None]
                ln2_tiles = []
                evac_tok = {}
                for tp in range(4):
                    seed = {}
                    for i in range(2):
                        for oc in range(4):
                            g = 4 * i + oc
                            d0 = [t_or, t_bo, uto_ready]
                            pv = evac_tok.get((tp - 1, g))
                            if pv:
                                d0.append(pv)
                            seed[g] = P.op("pe", lambda e, g=g, oc=oc: e.matmul(
                                o_ps[g][:, :], onesrow[:, :],
                                bo_row[0:1, oc * 512:(oc + 1) * 512],
                                start=True, stop=False), deps=d0)
                    last_mm = {g: [seed[g]] for g in range(8)}
                    for dc in range(NDC):
                        slot = dc % 2
                        wdeps = [wot_use[slot]] if wot_use[slot] else []
                        P.dma("pl", g_wot[slot], wotb[slot][:, :],
                              wot[dc * 128:(dc + 1) * 128, :], deps=wdeps)
                        wtok = g_wot[slot].tok()
                        lw = None
                        for i in range(2):
                            tt = 2 * tp + i
                            for oc in range(4):
                                g = 4 * i + oc
                                lw = P.op("pe", lambda e, g=g, slot=slot, dc=dc, tt=tt, oc=oc: e.matmul(
                                    o_ps[g][:, :],
                                    ao_sb[:, dc * TPC + tt * 128: dc * TPC + (tt + 1) * 128],
                                    wotb[slot][:, oc * 512:(oc + 1) * 512],
                                    start=False, stop=False), deps=[wtok])
                                last_mm[g].append(lw)
                        wot_use[slot] = lw
                    for i in range(2):
                        tt = 2 * tp + i
                        sl = tt % 2
                        xdeps = [xt3_tok[sl]] if xt3_tok[sl] else []
                        t_x = P.dma("sp", g_x3[sl], xt3[sl][:, :],
                                    xr[tt * 128:(tt + 1) * 128, :], deps=xdeps)
                        tile_done = []
                        for oc in range(4):
                            g = 4 * i + oc
                            t_lo = P.op("pe", lambda e, g=g, tt=tt, oc=oc: e.matmul(
                                o_ps[g][:, :],
                                ut[:, 3 * TPC + tt * 128:3 * TPC + (tt + 1) * 128],
                                at3[:, 3 * D + oc * 512:3 * D + (oc + 1) * 512],
                                start=False, stop=True), deps=[uto_ready, t_at])
                            dE = [t_lo, t_x] + last_mm[g]
                            if x1_tok[sl]:
                                dE.append(x1_tok[sl])
                            tkv = P.op("dve", lambda e, g=g, sl=sl, oc=oc: e.tensor_tensor(
                                x1t[sl][:, oc * 512:(oc + 1) * 512], o_ps[g][:, :],
                                xt3[sl][:, oc * 512:(oc + 1) * 512], ALU.add), deps=dE)
                            evac_tok[(tp, g)] = tkv
                            tile_done.append(tkv)
                        xt3_tok[sl] = tile_done[-1]
                        t_ln2 = _ln_chain(P, x1t[sl][:, :],
                                          ln2tm[:, tt * D:(tt + 1) * D], scr3,
                                          tile_done)
                        x1_tok[sl] = P.dma("sp", g_x1s[sl],
                                           x1_sp[tt * 128:(tt + 1) * 128, :],
                                           x1t[sl][:, :], deps=[t_ln2])
                        ln2_tiles.append((tt, t_ln2))
                last_o_evac = evac_tok[(3, 7)]

            P.barrier()
            with nc.psum_tensor("tps3", [128, 512], F32) as tps3:
                tp3_tok = None
                for tt, t_ln2 in ln2_tiles:
                    for g in range(4):
                        mm = []
                        for j in range(4):
                            dc = 4 * g + j
                            d0 = [t_ln2, t_eye]
                            if tp3_tok:
                                d0.append(tp3_tok)
                            mm.append(P.op("pe", lambda e, tt=tt, dc=dc, j=j: e.matmul(
                                tps3[:, j * 128:(j + 1) * 128],
                                ln2tm[:, tt * D + dc * 128: tt * D + (dc + 1) * 128],
                                eye[:, :], start=True, stop=True), deps=d0))
                        tp3_tok = P.op("act", lambda e, tt=tt, g=g: e.copy(
                            ln2t[:, :].rearrange("p (c n) -> p c n", n=TPC)
                            [:, 4 * g:4 * g + 4, tt * 128:(tt + 1) * 128],
                            tps3[:, :].rearrange("p (c n) -> p c n", n=128)), deps=mm)
                ln2t_ready = tp3_tok

            if "d_ln2t" in dbg_aps:
                P.dma("pl", g_cpl, dbg_aps["d_ln2t"][:, :], ln2t[:, :],
                      deps=[ln2t_ready])
            if "d_x1" in dbg_aps:
                pass

        P.barrier()
        s_ao.close()

        for _e in ENGINES:
            P.wait(_e, [g_x1s[0].tok(), g_x1s[1].tok()])
        with ExitStack() as p4:
            sb4 = lambda n, s, d: p4.enter_context(nc.sbuf_tensor(n, s, d))
            fuse = sb4("fuse", [128, 64 * 512], BF16)
            wgu = [sb4(f"wgu{i}", [128, NDC * 256], BF16) for i in range(2)]
            wub = [sb4(f"wub{i}", [128, NDC * 256], BF16) for i in range(2)]
            wdb = [sb4(f"wdb{i}", [128, 8 * 512], BF16) for i in range(2)]
            gtmp = sb4("gtmp", [128, 512], F32)
            utmp = sb4("utmp", [128, 512], F32)
            dtmp = sb4("dtmp", [128, 512], F32)
            x1rd = [sb4(f"x1rd{i}", [128, 512], F32) for i in range(2)]
            x2t = [sb4(f"x2t{i}", [128, 512], F32) for i in range(2)]
            sgu = sb4("sgu_sb", [128, 128], F32)
            sdb = sb4("sdb_sb", [128, D], F32)
            t_sgu = P.dma("sp", P.dgroup("c_sgu"), sgu[:, :], sgu_h[:, :])
            with nc.sbuf_tensor("sdrow_sb", [1, D], F32) as sdrow, \
                 nc.psum_tensor("sdb_ps", [128, D], F32) as sdb_ps:
                t_sdrow = P.dma("sp", P.dgroup("c_sdrow"), sdrow[:, :], sd_h[:, :])
                toks = []
                for j in range(4):
                    toks.append(P.op("pe", lambda e, j=j: e.matmul(
                        sdb_ps[:, j * 512:(j + 1) * 512], onesrow_f[:, :],
                        sdrow[0:1, j * 512:(j + 1) * 512], start=True, stop=True),
                        deps=[t_orf, t_sdrow]))
                t_sdb = P.op("dve", lambda e: e.tensor_copy(sdb[:, :], sdb_ps[:, :]),
                             deps=toks)

            g_wg = [P.dgroup(f"wg{i}") for i in range(2)]
            g_wd = [P.dgroup(f"wd{i}") for i in range(2)]
            g_x1r = [P.dgroup(f"x1rd{i}") for i in range(2)]
            g_out = [P.dgroup(f"outst{i}") for i in range(2)]

            wgu_use = [None, None]
            wd_use = [None, None]
            x2_tok = [None, None]
            x1rd_reader = [None, None]
            prev_ftok = None
            prev_x2 = None

            for th in range(2):
                with ExitStack() as p4a:
                    gu_ps = [p4a.enter_context(
                        nc.psum_tensor(f"gu_ps{th}_{i}", [128, 512], F32))
                        for i in range(4)]
                    gu_tok = [None] * 4
                    fuse_tok = {}
                    for grp in range(32):
                        slot = grp % 2
                        wdeps = [wgu_use[slot]] if wgu_use[slot] else []
                        P.dma("pl", g_wg[slot],
                              wgu[slot][:, :].rearrange("p (c o) -> p c o", o=256),
                              wg_h[:, grp * 256:(grp + 1) * 256]
                              .rearrange("(c p) o -> p c o", p=128), deps=wdeps)
                        P.dma("pl", g_wg[slot],
                              wub[slot][:, :].rearrange("p (c o) -> p c o", o=256),
                              wu_h[:, grp * 256:(grp + 1) * 256]
                              .rearrange("(c p) o -> p c o", p=128), deps=[])
                        wtok = g_wg[slot].tok()
                        lastg = None
                        for ii in range(2):
                            it = grp * 2 + ii
                            pg = it % 2
                            pu = 2 + it % 2
                            mmg, mmu = [], []
                            for hc in range(NDC):
                                d0 = []
                                if hc == 0:
                                    d0 = [wtok, ln2t_ready, t_sdb]
                                    if gu_tok[pg]:
                                        d0.append(gu_tok[pg])
                                d0u = []
                                if hc == 0:
                                    d0u = [wtok]
                                    if gu_tok[pu]:
                                        d0u.append(gu_tok[pu])
                                mmg.append(P.op("pe", lambda e, pg=pg, slot=slot, hc=hc, ii=ii, th=th: e.matmul(
                                    gu_ps[pg][:, :],
                                    wgu[slot][:, hc * 256 + ii * 128: hc * 256 + (ii + 1) * 128],
                                    ln2t[:, hc * TPC + th * 512: hc * TPC + th * 512 + 512],
                                    start=(hc == 0), stop=(hc == NDC - 1)), deps=d0))
                                mmu.append(P.op("pe", lambda e, pu=pu, slot=slot, hc=hc, ii=ii, th=th: e.matmul(
                                    gu_ps[pu][:, :],
                                    wub[slot][:, hc * 256 + ii * 128: hc * 256 + (ii + 1) * 128],
                                    ln2t[:, hc * TPC + th * 512: hc * TPC + th * 512 + 512],
                                    start=(hc == 0), stop=(hc == NDC - 1)), deps=d0u))
                            lastg = mmu[-1]
                            gdeps = mmg + [t_sgu]
                            udeps = mmu + [t_sgu]
                            if prev_ftok:
                                gdeps.append(prev_ftok)
                                udeps.append(prev_ftok)
                            t_g = P.op("act", lambda e, pg=pg, it=it: e.activation(
                                gtmp[:, :], gu_ps[pg][:, :], AF.Silu,
                                scale=sgu[:, it:it + 1]), deps=gdeps)
                            t_u = P.op("dve", lambda e, pu=pu, it=it: e.tensor_scalar(
                                utmp[:, :], gu_ps[pu][:, :], sgu[:, 64 + it:65 + it],
                                None, ALU.mult), deps=udeps)
                            ftok = P.op("dve", lambda e, it=it: e.tensor_tensor(
                                fuse[:, it * 512:(it + 1) * 512], gtmp[:, :],
                                utmp[:, :], ALU.mult), deps=[t_g, t_u])
                            gu_tok[pg] = ftok
                            gu_tok[pu] = ftok
                            fuse_tok[it] = ftok
                            prev_ftok = ftok
                        wgu_use[slot] = lastg
                    fuse_all = [fuse_tok[63]]

                P.barrier()

                with ExitStack() as p4b:
                    dn_ps = [p4b.enter_context(
                        nc.psum_tensor(f"dn_ps{th}_{i}", [128, 512], F32))
                        for i in range(8)]
                    dn_evac = {}
                    for oc in range(4):
                        base = 4 * (oc % 2)
                        lastd = None
                        for icg in range(8):
                            slot = icg % 2
                            wdeps = [wd_use[slot]] if wd_use[slot] else []
                            P.dma("pl", g_wd[slot],
                                  wdb[slot][:, :].rearrange("p (c o) -> p c o", o=512),
                                  wd_h[icg * 1024:(icg + 1) * 1024,
                                       oc * 512:(oc + 1) * 512]
                                  .rearrange("(c p) o -> p c o", p=128), deps=wdeps)
                            wtok = g_wd[slot].tok()
                            for ici in range(8):
                                ic = icg * 8 + ici
                                for ts in range(4):
                                    d0 = [wtok]
                                    if ic == 0 and ts == 0:
                                        d0 += fuse_all
                                        for tsx in range(4):
                                            if (oc - 2, tsx) in dn_evac:
                                                d0.append(dn_evac[(oc - 2, tsx)])
                                    lastd = P.op("pe", lambda e, base=base, ts=ts, slot=slot, ici=ici, ic=ic: e.matmul(
                                        dn_ps[base + ts][:, :],
                                        fuse[:, ic * 512 + ts * 128: ic * 512 + (ts + 1) * 128],
                                        wdb[slot][:, ici * 512:(ici + 1) * 512],
                                        start=(ic == 0), stop=(ic == 63)), deps=d0)
                            wd_use[slot] = lastd
                        for ts in range(4):
                            xsl = ts % 2
                            xd = [g_x1s[0].tok(), g_x1s[1].tok()]
                            if x1rd_reader[xsl]:
                                xd.append(x1rd_reader[xsl])
                            t_x1r = P.dma("sp", g_x1r[xsl],
                                          x1rd[xsl][:, :],
                                          bass.AP(x1_sp.tensor,
                                                  (th * 512 + ts * 128) * D + oc * 512,
                                                  [[D, 128], [1, 512]]), deps=xd)
                            d0 = [lastd, t_sdb]
                            if (oc - 1, ts) in dn_evac:
                                d0.append(dn_evac[(oc - 1, ts)])
                            if prev_x2:
                                d0.append(prev_x2)
                            t_sc = P.op("dve", lambda e, base=base, ts=ts, oc=oc: e.tensor_tensor(
                                dtmp[:, :], dn_ps[base + ts][:, :],
                                sdb[:, oc * 512:(oc + 1) * 512], ALU.mult), deps=d0)
                            d1 = [t_sc, t_x1r]
                            if x2_tok[xsl]:
                                d1.append(x2_tok[xsl])
                            t_x2 = P.op("dve", lambda e, ts=ts, xsl=xsl, oc=oc: e.tensor_tensor(
                                x2t[xsl][:, :], dtmp[:, :],
                                x1rd[xsl][:, :], ALU.add), deps=d1)
                            prev_x2 = t_x2
                            dn_evac[(oc, ts)] = t_x2
                            x1rd_reader[xsl] = t_x2
                            x2_tok[xsl] = P.dma("sp", g_out[xsl],
                                                bass.AP(out_h.tensor,
                                                        (th * 512 + ts * 128) * D + oc * 512,
                                                        [[D, 128], [1, 512]]),
                                                x2t[xsl][:, :], deps=[t_x2])
                if th == 0:
                    P.barrier()

            P.wait("sp", [g_out[0].tok(), g_out[1].tok()])

        P.finalize()
    return nc


def _host_layouts(inp):
    f = np.ascontiguousarray
    a = {k: np.asarray(v) for k, v in inp.items()}
    shared = {
        "wqt": f(a["L0_attn_Wq"].T),
        "wkt": f(a["L0_attn_Wk"].T),
        "wvt": f(a["L0_attn_Wv"].T),
        "wot": f(a["L0_attn_Wo"].T),
        "bqkv": f(np.concatenate([a["L0_attn_bq"].reshape(16, 128).T,
                                  a["L0_attn_bk"].reshape(16, 128).T,
                                  a["L0_attn_bv"].reshape(16, 128).T], axis=1)),
        "bo_row": f(a["L0_attn_bo"].reshape(1, D)),
        "bv_row": f(a["L0_attn_bv"].reshape(1, D)),
        "btcat": f(np.concatenate([a["L0_attn_Bq"].T, a["L0_attn_Bk"].T,
                                   a["L0_attn_Bv"].T, a["L0_attn_Bo"].T], axis=1)),
        "atcat": f(np.concatenate([a["L0_attn_Aq"].T, a["L0_attn_Ak"].T,
                                   a["L0_attn_Av"].T, a["L0_attn_Ao"].T], axis=1)),
        "sgu": f(np.concatenate([a["L0_ffn_s_gate"].reshape(64, 128).T,
                                 a["L0_ffn_s_up"].reshape(64, 128).T], axis=1)),
        "sdrow": f(a["L0_ffn_s_down"].reshape(1, D)),
        "wg": a["L0_ffn_w_gate"],
        "wu": a["L0_ffn_w_up"],
        "wd": a["L0_ffn_w_down"],
    }
    x = a["L0_attn_x"].reshape(SEQ, D)
    in_maps = []
    for r in range(NCORES):
        m = dict(shared)
        m["x"] = f(x[r * TPC:(r + 1) * TPC, :])
        in_maps.append(m)
    return in_maps


def kernel(**inputs):
    in_maps = _host_layouts(inputs)
    nc = build_nc()
    res = run_bass_kernel_spmd(nc, in_maps, core_ids=list(range(NCORES)))
    out = np.concatenate([np.asarray(r["out"]) for r in res.results], axis=0)
    return out.reshape(1, SEQ, D).astype(np.float32, copy=False)
